# version 01
# baseline (speedup 1.0000x reference)
import contextlib
import os
import numpy as np
import concourse.bass as bass
import concourse.mybir as mybir
from concourse.bass_utils import run_bass_kernel_spmd

F32 = mybir.dt.float32
BF16 = mybir.dt.bfloat16
U8 = mybir.dt.uint8
AF = mybir.ActivationFunctionType
ALU = mybir.AluOpType

D = 1024
T = 2048
KT = D // 128
NTC = T // 512
LN_EPS = 1e-5


class Sem:
    def __init__(self, handle):
        self.h = handle
        self.n = 0

    def inc(self, instr, k=1):
        instr.then_inc(self.h, k)
        self.n += k
        return instr


class P:
    def __init__(self, nc, block, es, sems, tensors):
        self.nc = nc
        self.block = block
        self.es = es
        self.s = sems
        self.t = tensors
        self.init_done = 0
        self.stream_ready = 0


def _emit_init(p):
    nc, s, t = p.nc, p.s, p.t
    ld, init = s["ld"], s["init"]
    ld0 = ld.n

    def sync(e):
        ld.inc(e.dma_start(out=t["ident"][:], in_=t["ident_hbm"][:, :]), 16)
        ld.inc(e.dma_start(out=t["s012_sb"][0:1, :], in_=t["s012_hbm"].unsqueeze(0)), 16)
    p.block.sync(sync)

    def vec(e):
        init.inc(e.memset(t["ones_col"][:], 1.0), 1)
        init.inc(e.memset(t["ones_col_b"][:], 1.0), 1)
        init.inc(e.memset(t["ones_row"][:], 1.0), 1)
        init.inc(e.memset(t["ones_row_b"][:], 1.0), 1)
        e.wait_ge(ld.h, ld0 + 32)
        init.inc(e.tensor_copy(t["identb"][:], t["ident"][:]), 1)
    p.block.vector(vec)
    p.init_done = init.n


def _emit_l0(p, stream_src):
    nc, s, t = p.nc, p.s, p.t
    ld, mm, ev, act, hh = s["ld"], s["mm"], s["ev"], s["act"], s["hh"]

    phase = contextlib.ExitStack()
    xs = phase.enter_context(nc.sbuf_tensor("l0_xs", [128, KT * T], F32))
    xib_scope = contextlib.ExitStack()
    xibT = xib_scope.enter_context(nc.sbuf_tensor("xibT", [128, KT * T], F32))
    inner = contextlib.ExitStack()
    xb = inner.enter_context(nc.sbuf_tensor("l0_xb", [128, KT * T], BF16))
    stag = inner.enter_context(nc.sbuf_tensor("l0_stag", [128, 2 * D], F32))
    wiT_sb = inner.enter_context(nc.sbuf_tensor("wiT_sb", [128, KT * D], BF16))
    bihT = inner.enter_context(nc.sbuf_tensor("bihT", [128, 2 * KT], F32))
    ps_x = [inner.enter_context(nc.psum_tensor(f"l0ps{i}", [128, 512], F32)) for i in range(4)]

    ld0 = ld.n
    ev0 = ev.n

    def sync(e):
        for k in range(KT):
            ld.inc(e.dma_start(out=xs[:, k * T:(k + 1) * T], in_=stream_src[k * 128:(k + 1) * 128, :]), 16)
        for m in range(KT):
            ld.inc(e.dma_start(out=bihT[:, m:m + 1], in_=t["bi_hbm"][m * 128:(m + 1) * 128].unsqueeze(1)), 16)
            ld.inc(e.dma_start(out=bihT[:, KT + m:KT + m + 1], in_=t["bh_hbm"][m * 128:(m + 1) * 128].unsqueeze(1)), 16)
        for q in range(4):
            if q > 0:
                e.wait_ge(ev.h, ev0 + 1 + q)
            ld.inc(e.dma_start(out=stag[:, :].rearrange("p (k d) -> p k d", k=2),
                               in_=t["wiT_hbm"].rearrange("(k p) d -> p k d", p=128)[:, 2 * q:2 * q + 2, :]), 16)
    p.block.sync(sync)
    after_stream = ld0 + 16 * KT
    after_bias = after_stream + 16 * 2 * KT

    def vec(e):
        e.wait_ge(s["init"].h, p.init_done)
        e.wait_ge(ld.h, after_stream)
        for k in range(KT):
            e.tensor_copy(xb[:, k * T:(k + 1) * T], xs[:, k * T:(k + 1) * T])
        e.wait_ge(ld.h, after_bias)
        ev.inc(e.tensor_tensor(bihT[:, 0:KT], bihT[:, 0:KT], bihT[:, KT:2 * KT], ALU.add), 1)
        for q in range(4):
            e.wait_ge(ld.h, after_bias + 16 * (q + 1))
            ev.inc(e.tensor_copy(wiT_sb[:, q * 2 * D:(q + 1) * 2 * D], stag[:, :]), 1)
    p.block.vector(vec)
    wi_ready = ev.n

    mm0 = mm.n
    ev_xi0 = ev.n

    def tens(e):
        e.wait_ge(ev.h, wi_ready)
        for idx in range(KT * NTC):
            mt, tc = divmod(idx, NTC)
            if idx >= 4:
                e.wait_ge(ev.h, ev_xi0 + (idx - 4) + 1)
            pb = ps_x[idx % 4]
            ins = None
            for k in range(KT):
                ins = e.matmul(pb[:, :],
                               wiT_sb[:, k * D + mt * 128: k * D + mt * 128 + 128],
                               xb[:, k * T + tc * 512: k * T + tc * 512 + 512],
                               start=(k == 0), stop=(k == KT - 1))
            mm.inc(ins, 1)
    p.block.tensor(tens)

    def vec2(e):
        for idx in range(KT * NTC):
            mt, tc = divmod(idx, NTC)
            e.wait_ge(mm.h, mm0 + idx + 1)
            pb = ps_x[idx % 4]
            ev.inc(e.tensor_scalar(xibT[:, mt * T + tc * 512: mt * T + tc * 512 + 512],
                                   pb[:, :], bihT[:, mt:mt + 1], None, ALU.add), 1)
    p.block.vector(vec2)
    xib_done = ev.n
    mm_xi_done = mm.n
    inner.close()

    if os.environ.get("MK_DBG") == "xib":
        ln = s["ln"]
        def vend(e):
            e.wait_ge(ev.h, xib_done)
            for k in range(KT):
                e.tensor_copy(xs[:, k * T:(k + 1) * T], xibT[:, k * T:(k + 1) * T])
            ln.inc(e.tensor_copy(t["junk"][:, 0:1], t["junk"][:, 1:2]), 1)
        p.block.vector(vend)
        p.stream_ready = ln.n
        p._l0_ret = (xs, phase)
        return

    inner2 = contextlib.ExitStack()
    whT_sb = inner2.enter_context(nc.sbuf_tensor("whT_sb", [128, KT * D], BF16))
    stag_scope = contextlib.ExitStack()
    stag2 = stag_scope.enter_context(nc.sbuf_tensor("l0_stag2", [128, 2 * D], F32))

    ld_wh0 = ld.n
    ev_wh0 = ev.n

    def sync2(e):
        e.wait_ge(mm.h, mm_xi_done)
        for q in range(4):
            if q > 0:
                e.wait_ge(ev.h, ev_wh0 + q)
            ld.inc(e.dma_start(out=stag2[:, :].rearrange("p (k d) -> p k d", k=2),
                               in_=t["whT_hbm"].rearrange("(k p) d -> p k d", p=128)[:, 2 * q:2 * q + 2, :]), 16)
    p.block.sync(sync2)

    def vec3(e):
        for q in range(4):
            e.wait_ge(ld.h, ld_wh0 + 16 * (q + 1))
            ev.inc(e.tensor_copy(whT_sb[:, q * 2 * D:(q + 1) * 2 * D], stag2[:, :]), 1)
    p.block.vector(vec3)
    wh_ready = ev.n
    stag_scope.close()
    hsT = inner2.enter_context(nc.sbuf_tensor("hsT", [128, KT * (T + 1)], BF16))
    ps_h = [inner2.enter_context(nc.psum_tensor(f"rnnps{m}", [128, 512], F32)) for m in range(KT)]

    hh0 = hh.n

    def vec4(e):
        for m in range(KT):
            hh.inc(e.memset(hsT[:, m * (T + 1):m * (T + 1) + 1], 0.0), 1)
    p.block.vector(vec4)

    mm_rnn0 = mm.n

    def tens2(e):
        e.wait_ge(ev.h, wh_ready)
        for tt in range(1, T + 1):
            e.wait_ge(hh.h, hh0 + KT * tt)
            for m in range(KT):
                ins = None
                for k in range(KT):
                    ins = e.matmul(ps_h[m][:, 0:1],
                                   whT_sb[:, k * D + m * 128:k * D + m * 128 + 128],
                                   hsT[:, k * (T + 1) + (tt - 1):k * (T + 1) + tt],
                                   start=(k == 0), stop=(k == KT - 1))
                mm.inc(ins, 1)
    p.block.tensor(tens2)

    def scal(e):
        for tt in range(1, T + 1):
            for m in range(KT):
                e.wait_ge(mm.h, mm_rnn0 + KT * (tt - 1) + m + 1)
                hh.inc(e.activation(out=hsT[:, m * (T + 1) + tt:m * (T + 1) + tt + 1],
                                    in_=ps_h[m][:, 0:1], func=AF.Tanh,
                                    bias=xibT[:, m * T + (tt - 1):m * T + tt], scale=1.0), 1)
    p.block.scalar(scal)
    rnn_done = hh.n

    if os.environ.get("MK_DBG") == "whs":
        ln = s["ln"]
        def vwhs(e):
            e.wait_ge(ev.h, wh_ready)
            for q in range(4):
                e.tensor_copy(xs[:, q * 2 * D:(q + 1) * 2 * D], whT_sb[:, q * 2 * D:(q + 1) * 2 * D])
            e.wait_ge(hh.h, hh0 + KT)
            for m in range(KT):
                e.tensor_copy(xs[:, 4 * 2 * D + m: 4 * 2 * D + m + 1], hsT[:, m * (T + 1):m * (T + 1) + 1])
            ln.inc(e.tensor_copy(t["junk"][:, 0:1], t["junk"][:, 1:2]), 1)
        p.block.vector(vwhs)
        p.stream_ready = ln.n
        inner2.close()
        p._l0_ret = (xs, phase)
        return

    if os.environ.get("MK_DBG") == "rnn":
        ln = s["ln"]
        def vrnn(e):
            e.wait_ge(hh.h, rnn_done)
            for k in range(KT):
                e.tensor_copy(xs[:, k * T:(k + 1) * T], hsT[:, k * (T + 1) + 1:k * (T + 1) + T + 1])
            ln.inc(e.tensor_copy(t["junk"][:, 0:1], t["junk"][:, 1:2]), 1)
        p.block.vector(vrnn)
        p.stream_ready = ln.n
        inner2.close()
        p._l0_ret = (xs, phase)
        return

    def vec5(e):
        e.wait_ge(hh.h, rnn_done)
        for k in range(KT):
            ev.inc(e.tensor_tensor(xs[:, k * T:(k + 1) * T], xs[:, k * T:(k + 1) * T],
                                   hsT[:, k * (T + 1) + 1:k * (T + 1) + T + 1], ALU.add), 1)
    p.block.vector(vec5)
    y_ready = ev.n
    inner2.close()
    xib_scope.close()
    p._l0_ret = (xs, phase)

    if os.environ.get("MK_DBG") == "noln":
        ln = s["ln"]
        def vend(e):
            e.wait_ge(ev.h, y_ready)
            ln.inc(e.tensor_copy(t["junk"][:, 0:1], t["junk"][:, 1:2]), 1)
        p.block.vector(vend)
        p.stream_ready = ln.n
    else:
        _emit_ln(p, xs, y_ready, tag="ln1")


def _emit_ln(p, xs, ev_y_ready, tag):
    nc, s, t = p.nc, p.s, p.t
    mm, ev, act, ln = s["mm"], s["ev"], s["act"], s["ln"]

    inner = contextlib.ExitStack()
    ysq = inner.enter_context(nc.sbuf_tensor(f"{tag}_ysq", [128, T], F32))
    rows = inner.enter_context(nc.sbuf_tensor(f"{tag}_rows", [1, 4 * T], F32))
    MU, VB, RA, TMP = (i * T for i in range(4))

    act0 = act.n
    mm0 = mm.n

    stats = contextlib.ExitStack()
    ps_s = stats.enter_context(nc.psum_tensor(f"{tag}_ps_s", [1, T], F32))
    ps_q = stats.enter_context(nc.psum_tensor(f"{tag}_ps_q", [1, T], F32))

    def scal(e):
        for k in range(KT):
            e.wait_ge(ev.h, ev_y_ready - KT + k + 1)
            if k >= 1:
                e.wait_ge(mm.h, mm0 + k * NTC)
            act.inc(e.activation(out=ysq[:, :], in_=xs[:, k * T:(k + 1) * T], func=AF.Square), 1)
    p.block.scalar(scal)

    def tens(e):
        e.wait_ge(s["init"].h, p.init_done)
        e.wait_ge(ev.h, ev_y_ready)
        for k in range(KT):
            e.wait_ge(act.h, act0 + k + 1)
            for tc in range(NTC):
                e.matmul(ps_s[0:1, tc * 512:(tc + 1) * 512], t["ones_col"][:, 0:1],
                         xs[:, k * T + tc * 512: k * T + tc * 512 + 512],
                         start=(k == 0), stop=(k == KT - 1))
                mm.inc(e.matmul(ps_q[0:1, tc * 512:(tc + 1) * 512], t["ones_col"][:, 0:1],
                                ysq[:, tc * 512:(tc + 1) * 512],
                                start=(k == 0), stop=(k == KT - 1)), 1)
    p.block.tensor(tens)
    mm_stats = mm.n

    ev_v0 = ev.n

    def vec(e):
        e.wait_ge(mm.h, mm_stats)
        e.tensor_scalar(rows[0:1, MU:MU + T], ps_s[0:1, :], 1.0 / D, None, ALU.mult)
        e.tensor_scalar(rows[0:1, TMP:TMP + T], ps_q[0:1, :], 1.0 / D, None, ALU.mult)
        e.tensor_tensor(rows[0:1, RA:RA + T], rows[0:1, MU:MU + T], rows[0:1, MU:MU + T], ALU.mult)
        ev.inc(e.tensor_tensor(rows[0:1, VB:VB + T], rows[0:1, TMP:TMP + T], rows[0:1, RA:RA + T], ALU.subtract), 1)
    p.block.vector(vec)
    ev_var = ev.n

    def vec1b(e):
        e.wait_ge(ev.h, ev_var)
        ev.inc(e.tensor_scalar(rows[0:1, VB:VB + T], rows[0:1, VB:VB + T], LN_EPS, None, ALU.add), 1)
    p.block.vector(vec1b)
    ev_vareps = ev.n

    def scal2(e):
        e.wait_ge(ev.h, ev_vareps)
        e.activation(out=rows[0:1, TMP:TMP + T], in_=rows[0:1, VB:VB + T], func=AF.Ln, scale=1.0)
        act.inc(e.activation(out=rows[0:1, RA:RA + T], in_=rows[0:1, TMP:TMP + T], func=AF.Exp, scale=-0.5), 1)
    p.block.scalar(scal2)
    act_rstd = act.n

    def vec2(e):
        e.wait_ge(act.h, act_rstd)
        e.tensor_tensor(rows[0:1, TMP:TMP + T], rows[0:1, MU:MU + T], rows[0:1, RA:RA + T], ALU.mult)
        ev.inc(e.tensor_scalar(rows[0:1, VB:VB + T], rows[0:1, TMP:TMP + T], -1.0, None, ALU.mult), 1)
    p.block.vector(vec2)
    ev_rows = ev.n
    stats.close()

    bc = contextlib.ExitStack()
    ps_bc = [bc.enter_context(nc.psum_tensor(f"{tag}_bc{i}", [128, 512], F32)) for i in range(2)]
    mm_bc0 = mm.n

    def tens2(e):
        e.wait_ge(ev.h, ev_rows)
        for tc in range(NTC):
            if tc >= 1:
                e.wait_ge(ev.h, ev_rows + tc * KT)
            e.matmul(ps_bc[0][:, :], t["ones_row"][0:1, :], rows[0:1, RA + tc * 512:RA + tc * 512 + 512],
                     start=True, stop=True)
            mm.inc(e.matmul(ps_bc[1][:, :], t["ones_row"][0:1, :], rows[0:1, VB + tc * 512:VB + tc * 512 + 512],
                            start=True, stop=True), 1)
    p.block.tensor(tens2)

    def vec3(e):
        for tc in range(NTC):
            e.wait_ge(mm.h, mm_bc0 + tc + 1)
            for k in range(KT):
                sl = slice(k * T + tc * 512, k * T + tc * 512 + 512)
                e.tensor_tensor(xs[:, sl], xs[:, sl], ps_bc[0][:, :], ALU.mult)
                ev.inc(e.tensor_tensor(xs[:, sl], xs[:, sl], ps_bc[1][:, :], ALU.add), 1)
        ln.inc(e.tensor_copy(t["junk"][:, 0:1], t["junk"][:, 1:2]), 1)
    p.block.vector(vec3)
    p.stream_ready = ln.n
    bc.close()
    inner.close()


def _emit_moe(p):
    nc, s, t = p.nc, p.s, p.t
    ld, mm, ev, act, ln = s["ld"], s["mm"], s["ev"], s["act"], s["ln"]
    IC = 16
    TH = 1024
    NE = 9

    phase = contextlib.ExitStack()
    xs = phase.enter_context(nc.sbuf_tensor("moe_xs", [128, KT * T], F32))
    inner = contextlib.ExitStack()
    xb = inner.enter_context(nc.sbuf_tensor("moe_xb", [128, KT * TH], BF16))
    aT = inner.enter_context(nc.sbuf_tensor("moe_aT", [128, IC * TH], BF16))
    wd_sb = inner.enter_context(nc.sbuf_tensor("moe_wd", [128, IC * D], BF16))
    stag_gu = [inner.enter_context(nc.sbuf_tensor(f"moe_sgu{i}", [128, D], F32)) for i in range(2)]
    wg_sb = [inner.enter_context(nc.sbuf_tensor(f"moe_wg{i}", [128, D], BF16)) for i in range(2)]
    wu_sb = [inner.enter_context(nc.sbuf_tensor(f"moe_wu{i}", [128, D], BF16)) for i in range(2)]
    stag_d = inner.enter_context(nc.sbuf_tensor("moe_sd", [128, D], F32))
    cw_tm = inner.enter_context(nc.sbuf_tensor("moe_cwtm", [128, 2 * 64], F32))
    cw_row = inner.enter_context(nc.sbuf_tensor("moe_cwrow", [1, TH], BF16))
    cw_b = inner.enter_context(nc.sbuf_tensor("moe_cwb", [128, TH], BF16))
    g_sb = [inner.enter_context(nc.sbuf_tensor(f"moe_g{i}", [128, 512], BF16)) for i in range(2)]
    u_sb = [inner.enter_context(nc.sbuf_tensor(f"moe_u{i}", [128, 512], BF16)) for i in range(2)]
    wr_sb = inner.enter_context(nc.sbuf_tensor("moe_wr", [128, KT * 8], F32))
    rb_sb = inner.enter_context(nc.sbuf_tensor("moe_rb", [8, 1], F32))
    m1 = inner.enter_context(nc.sbuf_tensor("moe_m1", [128, 2], F32))
    msk = inner.enter_context(nc.sbuf_tensor("moe_msk", [128, 24], F32))
    probsT = inner.enter_context(nc.sbuf_tensor("moe_probs", [8, TH], F32))

    ps_g = [inner.enter_context(nc.psum_tensor(f"moe_psg{i}", [128, 512], F32)) for i in range(2)]
    ps_u = [inner.enter_context(nc.psum_tensor(f"moe_psu{i}", [128, 512], F32)) for i in range(2)]
    ps_d = [inner.enter_context(nc.psum_tensor(f"moe_psd{i}", [128, 512], F32)) for i in range(4)]

    stream_ready = p.stream_ready
    mm_fence = mm.n

    wgs = [t["wgsT_hbm"]] + [t[f"wg{e}T_hbm"] for e in range(8)]
    wus = [t["wusT_hbm"]] + [t[f"wu{e}T_hbm"] for e in range(8)]
    wds = [t["wdsT_hbm"]] + [t[f"wd{e}T_hbm"] for e in range(8)]

    sgu = s["sgu"]; sdn = s["sdn"]; ssl = s["ssl"]; sat = s["sat"]
    sde = s["sde"]; swu = s["swu"]; swd = s["swd"]; scw = s["scw"]
    b_gu, b_dn, b_sl, b_at, b_de, b_wu, b_wd, b_cw = (x.n for x in (sgu, sdn, ssl, sat, sde, swu, swd, scw))
    b_ld = ld.n
    b_ev = ev.n
    b_act = act.n

    NLD0 = 17

    def ld_wg_done(gi):
        J, ic = divmod(gi, IC)
        return b_ld + (NLD0 + J * 48 + 2 * ic + 1) * 16

    def ld_wu_done(gi):
        J, ic = divmod(gi, IC)
        return b_ld + (NLD0 + J * 48 + 2 * ic + 2) * 16

    def ld_wd_done(gi):
        J, ic = divmod(gi, IC)
        return b_ld + (NLD0 + J * 48 + 32 + (ic + 1)) * 16

    st = s["st"]
    stream_in_dram = p.stream_in_dram

    def sync(e):
        e.wait_ge(st.h, stream_in_dram)
        e.wait_ge(ln.h, stream_ready)
        e.wait_ge(mm.h, mm_fence)
        for k in range(KT):
            ld.inc(e.dma_start(out=xs[:, k * T:(k + 1) * T], in_=t["xs_dram"][k * 128:(k + 1) * 128, :]), 16)
        for k in range(KT):
            ld.inc(e.dma_start(out=wr_sb[:, k * 8:(k + 1) * 8], in_=t["wrT_hbm"][k * 128:(k + 1) * 128, :]), 16)
        ld.inc(e.dma_start(out=rb_sb[0:8, 0:1], in_=t["rb_hbm"].unsqueeze(1)), 16)
        for th in range(2):
            for j in range(NE):
                J = th * NE + j
                for ic in range(IC):
                    gi = J * IC + ic
                    if gi >= 1:
                        e.wait_ge(swu.h, b_wu + 2 * (gi - 1) + 1)
                    ld.inc(e.dma_start(out=stag_gu[0][:, :].rearrange("p (k c) -> p k c", k=KT),
                                       in_=wgs[j].rearrange("(k p) i -> p k i", p=128)[:, :, ic * 128:(ic + 1) * 128]), 16)
                    if gi >= 1:
                        e.wait_ge(swu.h, b_wu + 2 * (gi - 1) + 2)
                    ld.inc(e.dma_start(out=stag_gu[1][:, :].rearrange("p (k c) -> p k c", k=KT),
                                       in_=wus[j].rearrange("(k p) i -> p k i", p=128)[:, :, ic * 128:(ic + 1) * 128]), 16)
                for ic in range(IC):
                    gi = J * IC + ic
                    if gi >= 1:
                        e.wait_ge(swd.h, b_wd + gi)
                    ld.inc(e.dma_start(out=stag_d[:, :], in_=wds[j][ic * 128:(ic + 1) * 128, :]), 16)
    p.block.sync(sync)

    ev_n = [b_ev]
    act_n = [b_act]
    mm_n = [mm_fence]

    for th in range(2):
        toff = th * TH

        ev_xb0 = ev_n[0]

        def vec_xb(e, th=th, toff=toff, ev_xb0=ev_xb0):
            if th == 0:
                e.wait_ge(ld.h, b_ld + 8 * 16)
            for k in range(KT):
                ev.inc(e.tensor_copy(xb[:, k * TH:(k + 1) * TH], xs[:, k * T + toff:k * T + toff + TH]), 1)
        p.block.vector(vec_xb)
        ev_n[0] = ev.n
        xb_ready = ev.n

        mm_r0 = mm_n[0]

        def tens_r(e, th=th, toff=toff, mm_r0=mm_r0, xb_ready=xb_ready):
            e.wait_ge(ev.h, xb_ready)
            e.wait_ge(ld.h, b_ld + NLD0 * 16)
            e.wait_ge(sde.h, b_de + 16 * (th * NE))
            for tc in range(2):
                ins = None
                for k in range(KT):
                    ins = e.matmul(ps_d[tc % 2][0:8, :], wr_sb[:, k * 8:(k + 1) * 8],
                                   xs[:, k * T + toff + tc * 512: k * T + toff + tc * 512 + 512],
                                   start=(k == 0), stop=(k == KT - 1))
                mm.inc(ins, 1)
        p.block.tensor(tens_r)
        mm_router = mm.n
        act_r0 = act_n[0]

        def act_r(e, th=th, mm_r0=mm_r0):
            for tc in range(2):
                e.wait_ge(mm.h, mm_r0 + tc + 1)
                act.inc(e.activation(out=probsT[0:8, tc * 512:(tc + 1) * 512], in_=ps_d[tc % 2][0:8, :],
                                     func=AF.Sigmoid, bias=rb_sb[0:8, 0:1], scale=1.0), 1)
        p.block.scalar(act_r)
        act_n[0] = act.n
        act_probs = act.n
        ev_r0 = ev.n

        def tens_r2(e, th=th, act_probs=act_probs, ev_r0=ev_r0):
            e.wait_ge(act.h, act_probs)
            for tt in range(8):
                if tt >= 2:
                    e.wait_ge(ev.h, ev_r0 + 7 * (tt - 1))
                mm.inc(e.transpose(ps_d[2 + tt % 2][0:128, 0:8], probsT[0:8, tt * 128:(tt + 1) * 128],
                                   t["ident"][0:8, 0:8]), 1)
        p.block.tensor(tens_r2)
        mm_tr0 = mm_router

        def vec_r(e, th=th, mm_tr0=mm_tr0):
            tp = s["tp"]

            def st_(ins):
                tp.inc(ins, 1)
                e.wait_ge(tp.h, tp.n)
                return ins
            for tt in range(8):
                e.wait_ge(mm.h, mm_tr0 + tt + 1)
                pb = ps_d[2 + tt % 2]
                c0 = th * 64 + tt * 8
                st_(e.tensor_copy(cw_tm[:, c0:c0 + 8], pb[0:128, 0:8]))
                st_(e.reduce_max(m1[:, 0:1], cw_tm[:, c0:c0 + 8], axis=mybir.AxisListType.X))
                st_(e.tensor_scalar(msk[:, 0:8], cw_tm[:, c0:c0 + 8], m1[:, 0:1], None, ALU.is_ge))
                st_(e.tensor_scalar(msk[:, 8:16], msk[:, 0:8], -1e30, None, ALU.mult))
                st_(e.tensor_tensor(msk[:, 16:24], cw_tm[:, c0:c0 + 8], msk[:, 8:16], ALU.add))
                st_(e.reduce_max(m1[:, 1:2], msk[:, 16:24], axis=mybir.AxisListType.X))
                st_(e.tensor_scalar(msk[:, 0:8], cw_tm[:, c0:c0 + 8], m1[:, 1:2], None, ALU.is_ge))
                if os.environ.get("MK_DBG") == "cwprobs":
                    ev.inc(e.tensor_copy(cw_tm[:, c0:c0 + 8], cw_tm[:, c0:c0 + 8]) if False else e.tensor_scalar(cw_tm[:, c0:c0 + 8], cw_tm[:, c0:c0 + 8], 1.0, None, ALU.mult), 7)
                else:
                    ev.inc(e.tensor_tensor(cw_tm[:, c0:c0 + 8], cw_tm[:, c0:c0 + 8], msk[:, 0:8], ALU.mult), 7)
        p.block.vector(vec_r)
        ev_n[0] = ev.n
        ev_cwtm = ev.n
        mm_n[0] = mm.n

        for j in range(NE):
            J = th * NE + j
            routed = j >= 1
            R = th * 8 + (j - 1)
            c0cw = b_cw + R * 10

            if routed:
                mm_cw0 = mm.n

                def tens_cw(e, th=th, j=j, J=J, R=R, c0cw=c0cw, ev_cwtm=ev_cwtm):
                    e.wait_ge(ev.h, ev_cwtm)
                    e.wait_ge(sde.h, b_de + 16 * J)
                    for tt in range(8):
                        if tt >= 4:
                            e.wait_ge(scw.h, c0cw + (tt - 4) + 1)
                        mm.inc(e.transpose(ps_d[tt % 4][0:1, 0:128],
                                           cw_tm[:, th * 64 + tt * 8 + (j - 1):th * 64 + tt * 8 + j],
                                           t["ident"][:, :]), 1)
                    for tc in range(2):
                        e.wait_ge(scw.h, c0cw + max(4 * (tc + 1), 5 + tc))
                        mm.inc(e.matmul(ps_d[tc % 4][:, :], t["ones_row_b"][0:1, :],
                                        cw_row[0:1, tc * 512:tc * 512 + 512], start=True, stop=True), 1)
                p.block.tensor(tens_cw)

                def vec_cw(e, mm_cw0=mm_cw0):
                    for tt in range(8):
                        e.wait_ge(mm.h, mm_cw0 + tt + 1)
                        scw.inc(e.tensor_copy(cw_row[0:1, tt * 128:(tt + 1) * 128], ps_d[tt % 4][0:1, 0:128]), 1)
                    for tc in range(2):
                        e.wait_ge(mm.h, mm_cw0 + 8 + tc + 1)
                        scw.inc(e.tensor_copy(cw_b[:, tc * 512:(tc + 1) * 512], ps_d[tc % 4][:, :]), 1)
                p.block.vector(vec_cw)
                mm_n[0] = mm.n

            def tens_gu(e, J=J, xb_ready=xb_ready):
                e.wait_ge(ev.h, xb_ready)
                for ic in range(IC):
                    gi = J * IC + ic
                    db = gi % 2
                    e.wait_ge(swu.h, b_wu + 2 * (gi + 1))
                    for tc in range(2):
                        qi = gi * 2 + tc
                        if qi >= 2:
                            e.wait_ge(ssl.h, b_sl + qi - 1)
                            e.wait_ge(sat.h, b_at + qi - 1)
                        for k in range(KT):
                            e.matmul(ps_g[qi % 2][:, :], wg_sb[db][:, k * 128:(k + 1) * 128],
                                     xb[:, k * TH + tc * 512:k * TH + tc * 512 + 512],
                                     start=(k == 0), stop=(k == KT - 1))
                        ins = None
                        for k in range(KT):
                            ins = e.matmul(ps_u[qi % 2][:, :], wu_sb[db][:, k * 128:(k + 1) * 128],
                                           xb[:, k * TH + tc * 512:k * TH + tc * 512 + 512],
                                           start=(k == 0), stop=(k == KT - 1))
                        sgu.inc(ins, 1)
            p.block.tensor(tens_gu)

            def gp_wgu(e, J=J):
                for ic in range(IC):
                    gi = J * IC + ic
                    db = gi % 2
                    e.wait_ge(ld.h, ld_wg_done(gi))
                    if gi >= 2:
                        e.wait_ge(sgu.h, b_gu + 2 * (gi - 1))
                    swu.inc(e.tensor_copy(wg_sb[db][:, :], stag_gu[0][:, :]), 1)
                    e.wait_ge(ld.h, ld_wu_done(gi))
                    swu.inc(e.tensor_copy(wu_sb[db][:, :], stag_gu[1][:, :]), 1)
            p.block.gpsimd(gp_wgu)

            def act_sl(e, J=J):
                for ic in range(IC):
                    for tc in range(2):
                        qi = (J * IC + ic) * 2 + tc
                        e.wait_ge(sgu.h, b_gu + qi + 1)
                        if qi >= 2:
                            e.wait_ge(sat.h, b_at + qi - 1)
                        ssl.inc(e.activation(out=g_sb[qi % 2][:, :], in_=ps_g[qi % 2][:, :], func=AF.Silu), 1)
            p.block.scalar(act_sl)

            def vec_au(e, J=J, routed=routed, c0cw=c0cw):
                for ic in range(IC):
                    for tc in range(2):
                        qi = (J * IC + ic) * 2 + tc
                        e.wait_ge(sgu.h, b_gu + qi + 1)
                        if routed:
                            e.wait_ge(scw.h, c0cw + 8 + tc + 1)
                            e.tensor_tensor(u_sb[qi % 2][:, :], ps_u[qi % 2][:, :],
                                            cw_b[:, tc * 512:(tc + 1) * 512], ALU.mult)
                        else:
                            e.tensor_copy(u_sb[qi % 2][:, :], ps_u[qi % 2][:, :])
                        e.wait_ge(ssl.h, b_sl + qi + 1)
                        e.wait_ge(sdn.h, b_dn + 16 * J)
                        sat.inc(e.tensor_tensor(aT[:, ic * TH + tc * 512:ic * TH + tc * 512 + 512],
                                                g_sb[qi % 2][:, :], u_sb[qi % 2][:, :], ALU.mult), 1)
            p.block.vector(vec_au)

            def act_wd(e, J=J):
                for ic in range(IC):
                    gi = J * IC + ic
                    e.wait_ge(ld.h, ld_wd_done(gi))
                    e.wait_ge(sdn.h, b_dn + 16 * J)
                    swd.inc(e.copy(wd_sb[:, ic * D:(ic + 1) * D], stag_d[:, :]), 1)
            p.block.scalar(act_wd)

            def tens_dn(e, J=J, routed=routed, c0cw=c0cw):
                e.wait_ge(sat.h, b_at + 32 * (J + 1))
                e.wait_ge(swd.h, b_wd + 16 * (J + 1))
                if routed:
                    e.wait_ge(scw.h, c0cw + 10)
                for tc in range(2):
                    for ht in range(KT):
                        di = J * 16 + tc * KT + ht
                        if di >= 4:
                            e.wait_ge(sde.h, b_de + di - 3)
                        ins = None
                        for ic in range(IC):
                            ins = e.matmul(ps_d[di % 4][:, :], wd_sb[:, ic * D + ht * 128:ic * D + ht * 128 + 128],
                                           aT[:, ic * TH + tc * 512:ic * TH + tc * 512 + 512],
                                           start=(ic == 0), stop=(ic == IC - 1))
                        sdn.inc(ins, 1)
            p.block.tensor(tens_dn)
            mm_n[0] = mm.n

            def vec_dn(e, J=J, toff=toff):
                for tc in range(2):
                    for ht in range(KT):
                        di = J * 16 + tc * KT + ht
                        e.wait_ge(sdn.h, b_dn + di + 1)
                        sl = slice(ht * T + toff + tc * 512, ht * T + toff + tc * 512 + 512)
                        sde.inc(e.tensor_tensor(xs[:, sl], xs[:, sl], ps_d[di % 4][:, :], ALU.add), 1)
            p.block.vector(vec_dn)
            ev_n[0] = ev.n

    if os.environ.get("MK_DBG") in ("cw", "cwprobs"):
        def vcw(e):
            e.wait_ge(sde.h, b_de + 18 * 16)
            e.tensor_copy(xs[:, 0:128], cw_tm[:, 0:128])
            e.tensor_copy(xs[:, 128:130], m1[:, 0:2])
            e.tensor_copy(xs[:, 130:154], msk[:, 0:24])
            ln.inc(e.tensor_copy(t["junk"][:, 0:1], t["junk"][:, 1:2]), 1)
        p.block.vector(vcw)
        p.stream_ready = ln.n
        inner.close()
        p._moe_ret = (xs, phase)
        return

    def vec_bridge(e):
        for k in range(KT):
            e.wait_ge(sde.h, b_de + 8 * 16 + 8 + k + 1)
            e.wait_ge(sde.h, b_de + 17 * 16 + 8 + k + 1)
            ev.inc(e.tensor_copy(t["junk"][:, 0:1], t["junk"][:, 1:2]), 1)
    p.block.vector(vec_bridge)
    y_ready = ev.n

    if os.environ.get("MK_DBG") == "moeraw":
        def vraw(e):
            for k in range(KT):
                e.wait_ge(ev.h, y_ready - KT + k + 1)
            ln.inc(e.tensor_copy(t["junk"][:, 0:1], t["junk"][:, 1:2]), 1)
        p.block.vector(vraw)
        p.stream_ready = ln.n
        inner.close()
        p._moe_ret = (xs, phase)
        return
    inner.close()

    _emit_ln(p, xs, y_ready, tag="ln2")
    p._moe_ret = (xs, phase)


def _emit_attn(p):
    nc, s, t = p.nc, p.s, p.t
    ld, mm, ev, act, ln, st = s["ld"], s["mm"], s["ev"], s["act"], s["ln"], s["st"]
    pj = s["pj"]; pg = s["pg"]; pgd = s["pgd"]
    pp = s["pp"]; pa = s["pa"]; po = s["po"]; pi = s["pi"]; pc = s["pc"]
    pS = s["pS"]; pmx = s["pmx"]; pex = s["pex"]; psc = s["psc"]
    ptr = s["ptr"]; pte = s["pte"]; pav = s["pav"]; poe = s["poe"]; tp = s["tp"]

    phase = contextlib.ExitStack()
    OT = phase.enter_context(nc.sbuf_tensor("at_OT", [128, KT * T], BF16))
    qkv = contextlib.ExitStack()
    QT = qkv.enter_context(nc.sbuf_tensor("at_QT", [128, KT * T], BF16))
    KTs = qkv.enter_context(nc.sbuf_tensor("at_KT", [128, KT * T], BF16))
    V = qkv.enter_context(nc.sbuf_tensor("at_V", [128, 16 * D], BF16))
    rext = qkv.enter_context(nc.sbuf_tensor("at_rext", [128, 4096], BF16))
    bvT = qkv.enter_context(nc.sbuf_tensor("at_bvT", [128, KT], F32))

    stream_in_dram = p.stream_in_dram
    mm_fence = mm.n
    b_ld = ld.n
    b_pg = pg.n
    b_pgd = pgd.n
    b_pj = pj.n
    b_mm = mm.n

    proj = contextlib.ExitStack()
    xb = proj.enter_context(nc.sbuf_tensor("at_xb", [128, KT * T], BF16))
    pstg = [proj.enter_context(nc.sbuf_tensor(f"at_stg{i}", [128, D], F32)) for i in range(2)]
    w_sb = proj.enter_context(nc.sbuf_tensor("at_wsb", [128, KT * D], BF16))
    bqkT = proj.enter_context(nc.sbuf_tensor("at_bqk", [128, 2 * KT], F32))
    ps_pr = [proj.enter_context(nc.psum_tensor(f"at_pspr{i}", [128, 512], F32)) for i in range(4)]

    def gp_xb(e):
        e.wait_ge(st.h, stream_in_dram)
        e.wait_ge(mm.h, mm_fence)
        e.wait_ge(ln.h, p.stream_ready)
        for k in range(KT):
            pgd.inc(e.dma_start(out=xb[:, k * T:(k + 1) * T], in_=t["xs_dram"][k * 128:(k + 1) * 128, :]), 16)
    p.block.gpsimd(gp_xb)

    def sync_w(e):
        e.wait_ge(st.h, stream_in_dram)
        e.wait_ge(mm.h, mm_fence)
        for m in range(KT):
            ld.inc(e.dma_start(out=bqkT[:, m:m + 1], in_=t["bq_hbm"][m * 128:(m + 1) * 128].unsqueeze(1)), 16)
            ld.inc(e.dma_start(out=bqkT[:, KT + m:KT + m + 1], in_=t["bk_hbm"][m * 128:(m + 1) * 128].unsqueeze(1)), 16)
            ld.inc(e.dma_start(out=bvT[:, m:m + 1], in_=t["bv_hbm"][m * 128:(m + 1) * 128].unsqueeze(1)), 16)
        for wi, wh in enumerate(["wqT_hbm", "wkT_hbm", "wvT_hbm"]):
            for kt in range(KT):
                si = wi * KT + kt
                if si >= 2:
                    e.wait_ge(pg.h, b_pg + si - 1)
                ld.inc(e.dma_start(out=pstg[si % 2][:, :], in_=t[wh][kt * 128:(kt + 1) * 128, :]), 16)
        for hf in range(4):
            si = 24 + hf
            e.wait_ge(pg.h, b_pg + si - 1)
            ld.inc(e.dma_start(out=pstg[si % 2][0:64, :],
                               in_=t["rextT_hbm"][:, hf * 1024:(hf + 1) * 1024]), 16)
            ld.inc(e.dma_start(out=pstg[si % 2][64:128, :],
                               in_=t["rextT_hbm"][:, hf * 1024:(hf + 1) * 1024]), 16)
    p.block.sync(sync_w)

    def ld_w(si):
        if si < 24:
            return b_ld + (24 + si + 1) * 16
        return b_ld + (48 + 2 * (si - 24 + 1)) * 16

    def gp_cast(e):
        for wi in range(3):
            for kt in range(KT):
                si = wi * KT + kt
                e.wait_ge(ld.h, ld_w(si))
                if wi > 0:
                    e.wait_ge(mm.h, b_mm + 32 * wi)
                pg.inc(e.tensor_copy(w_sb[:, kt * D:(kt + 1) * D], pstg[si % 2][:, :]), 1)
        for hf in range(4):
            si = 24 + hf
            e.wait_ge(ld.h, ld_w(si))
            pg.inc(e.tensor_copy(rext[:, hf * 1024:(hf + 1) * 1024], pstg[si % 2][:, :]), 1)
    p.block.gpsimd(gp_cast)

    def tens_proj(e):
        e.wait_ge(pgd.h, b_pgd + 16 * KT)
        for wi in range(3):
            for gidx in range(32):
                g = wi * 32 + gidx
                e.wait_ge(pg.h, b_pg + wi * KT + KT)
                if g >= 4:
                    e.wait_ge(pj.h, b_pj + g - 3)
                pb = ps_pr[g % 4]
                ins = None
                if wi < 2:
                    mt, tc = divmod(gidx, NTC)
                    for k in range(KT):
                        ins = e.matmul(pb[:, :], w_sb[:, k * D + mt * 128:k * D + mt * 128 + 128],
                                       xb[:, k * T + tc * 512:k * T + tc * 512 + 512],
                                       start=(k == 0), stop=(k == KT - 1))
                else:
                    jt, fh = divmod(gidx, 2)
                    for k in range(KT):
                        ins = e.matmul(pb[:, :], xb[:, k * T + jt * 128:k * T + jt * 128 + 128],
                                       w_sb[:, k * D + fh * 512:k * D + fh * 512 + 512],
                                       start=(k == 0), stop=(k == KT - 1))
                mm.inc(ins, 1)
    p.block.tensor(tens_proj)

    def vec_proj(e):
        for wi in range(3):
            for gidx in range(32):
                g = wi * 32 + gidx
                e.wait_ge(mm.h, b_mm + g + 1)
                pb = ps_pr[g % 4]
                if wi < 2:
                    mt, tc = divmod(gidx, NTC)
                    dst = QT if wi == 0 else KTs
                    pj.inc(e.tensor_scalar(dst[:, mt * T + tc * 512:mt * T + tc * 512 + 512],
                                           pb[:, :], bqkT[:, wi * KT + mt:wi * KT + mt + 1], None, ALU.add), 1)
                else:
                    jt, fh = divmod(gidx, 2)
                    pj.inc(e.tensor_copy(V[:, jt * D + fh * 512:jt * D + fh * 512 + 512], pb[:, :]), 1)
    p.block.vector(vec_proj)
    proj_done = pj.n
    rext_done = b_pg + 28
    proj.close()

    hl = contextlib.ExitStack()
    pstage = [hl.enter_context(nc.sbuf_tensor(f"at_pst{i}", [128, 2176], BF16)) for i in range(2)]
    sk_sb = [hl.enter_context(nc.sbuf_tensor(f"at_sk{i}", [128, T], BF16)) for i in range(2)]
    S_row = [hl.enter_context(nc.sbuf_tensor(f"at_S{i}", [128, T], F32)) for i in range(2)]
    expS = [hl.enter_context(nc.sbuf_tensor(f"at_eS{i}", [128, T], BF16)) for i in range(2)]
    expST = [hl.enter_context(nc.sbuf_tensor(f"at_eST{i}", [128, T], BF16)) for i in range(2)]
    mxb = hl.enter_context(nc.sbuf_tensor("at_mx", [128, 8], F32))
    ps_a = [hl.enter_context(nc.psum_tensor(f"at_psa{i}", [128, 512], F32)) for i in range(4)]
    ps_t = [hl.enter_context(nc.psum_tensor(f"at_ptt{i}", [128, 512], BF16)) for i in range(2)]
    ps_o = [hl.enter_context(nc.psum_tensor(f"at_pso{i}", [64, 128], F32)) for i in range(2)]

    NSL = 256
    b_pp, b_pa, b_po, b_pi, b_pc, b_pS, b_pmx, b_pex, b_psc, b_ptr, b_pte, b_pav, b_poe = (
        x.n for x in (pp, pa, po, pi, pc, pS, pmx, pex, psc, ptr, pte, pav, poe))
    CW = [512, 512, 512, 512, 128]
    COF = [0, 512, 1024, 1536, 2048]

    def qslice(h, ic):
        return QT[64 * (h % 2):64 * (h % 2) + 64, (h // 2) * T + ic * 128:(h // 2) * T + ic * 128 + 128]

    def tens_hl(e):
        e.wait_ge(pj.h, proj_done)
        e.wait_ge(pg.h, rext_done)
        e.wait_ge(s["init"].h, p.init_done)
        for oi in range(NSL + 3):
            if oi < NSL:
                h, ic = divmod(oi, 16)
                k0s = 1920 - ic * 128
                for c in range(5):
                    cn = 5 * oi + c
                    if cn >= 2:
                        e.wait_ge(pa.h, b_pa + cn - 1)
                    pp.inc(e.matmul(ps_a[cn % 2][:, 0:CW[c]], qslice(h, ic),
                                    rext[64 * (h % 2):64 * (h % 2) + 64, k0s + COF[c]:k0s + COF[c] + CW[c]],
                                    start=True, stop=True), 1)
            if 1 <= oi < NSL + 1:
                g1 = oi - 1
                h, ic = divmod(g1, 16)
                for jc in range(NTC):
                    cn = 4 * g1 + jc
                    if cn >= 2:
                        e.wait_ge(pS.h, b_pS + cn - 1)
                    pc.inc(e.matmul(ps_a[2 + cn % 2][:, :], qslice(h, ic),
                                    KTs[64 * (h % 2):64 * (h % 2) + 64,
                                        (h // 2) * T + jc * 512:(h // 2) * T + jc * 512 + 512],
                                    start=True, stop=True), 1)
            if 3 <= oi < NSL + 3:
                g3 = oi - 3
                h, ic = divmod(g3, 16)
                e.wait_ge(psc.h, b_psc + g3 + 1)
                for gq in range(4):
                    gqi = 4 * g3 + gq
                    if gqi >= 2:
                        e.wait_ge(pte.h, b_pte + gqi - 1)
                    ins = None
                    for j4 in range(4):
                        jt = gq * 4 + j4
                        ins = e.matmul(ps_t[gqi % 2][:, j4 * 128:(j4 + 1) * 128],
                                       expS[g3 % 2][:, jt * 128:(jt + 1) * 128],
                                       t["identb"][:, :], is_transpose=True,
                                       start=(j4 == 0), stop=(j4 == 3))
                    ptr.inc(ins, 1)
                if g3 >= 2:
                    e.wait_ge(poe.h, b_poe + g3 - 1)
                ins = None
                for jt in range(16):
                    if jt % 4 == 0:
                        e.wait_ge(pte.h, b_pte + 4 * g3 + jt // 4 + 1)
                    ins = e.matmul(ps_o[g3 % 2][0:64, :], V[:, jt * D + h * 64:jt * D + h * 64 + 64],
                                   expST[g3 % 2][:, jt * 128:(jt + 1) * 128],
                                   start=(jt == 0), stop=(jt == 15))
                pav.inc(ins, 1)
    p.block.tensor(tens_hl)

    def gp_hl(e):
        for oi in range(NSL + 1):
            if oi < NSL:
                h, ic = divmod(oi, 16)
                e.wait_ge(pa.h, b_pa + 5 * (oi + 1))
                po.inc(e.dma_start(out=t["pscr"][h % 2, ic * 128 * 2176:(ic + 1) * 128 * 2176]
                                   .rearrange("(a c) -> a c", a=128),
                                   in_=pstage[oi % 2][:, :]), 16)
            if oi >= 1:
                g1 = oi - 1
                h, ic = divmod(g1, 16)
                e.wait_ge(po.h, b_po + 16 * (g1 + 1))
                if g1 >= 2:
                    e.wait_ge(pS.h, b_pS + 4 * (g1 - 1))
                for jc in range(NTC):
                    sap = bass.AP(tensor=t["pscr_h"],
                                  offset=(h % 2) * 16 * 128 * 2176 + ic * 128 * 2176 + jc * 512 + 127,
                                  ap=[[2175, 128], [1, 512]])
                    pi.inc(e.dma_start(out=sk_sb[g1 % 2][:, jc * 512:(jc + 1) * 512], in_=sap), 16)
    p.block.gpsimd(gp_hl)

    def act_hl(e):
        for oi in range(NSL + 2):
            if oi < NSL:
                for c in range(5):
                    cn = 5 * oi + c
                    e.wait_ge(pp.h, b_pp + cn + 1)
                    if oi >= 2 and c == 0:
                        e.wait_ge(po.h, b_po + 16 * (oi - 1))
                    pa.inc(e.activation(out=pstage[oi % 2][:, COF[c]:COF[c] + CW[c]],
                                        in_=ps_a[cn % 2][:, 0:CW[c]], func=AF.Square)
                           if False else
                           e.copy(pstage[oi % 2][:, COF[c]:COF[c] + CW[c]], ps_a[cn % 2][:, 0:CW[c]]), 1)
            if oi >= 2:
                g2 = oi - 2
                e.wait_ge(pmx.h, b_pmx + g2 + 1)
                if g2 >= 2:
                    e.wait_ge(ptr.h, b_ptr + 4 * (g2 - 1))
                pex.inc(e.activation(out=expS[g2 % 2][:, :], in_=S_row[g2 % 2][:, :], func=AF.Exp,
                                     scale=0.125, bias=mxb[:, (g2 % 2) * 4 + 1:(g2 % 2) * 4 + 2],
                                     accum_out=mxb[:, (g2 % 2) * 4 + 2:(g2 % 2) * 4 + 3]), 1)
    p.block.scalar(act_hl)

    def vec_hl(e):
        def st_(ins):
            tp.inc(ins, 1)
            e.wait_ge(tp.h, tp.n)
            return ins
        for oi in range(NSL + 3):
            if 1 <= oi < NSL + 1:
                g1 = oi - 1
                par = g1 % 2
                if g1 >= 2:
                    e.wait_ge(pex.h, b_pex + g1 - 1)
                for jc in range(NTC):
                    cn = 4 * g1 + jc
                    e.wait_ge(pc.h, b_pc + cn + 1)
                    e.wait_ge(pi.h, b_pi + 64 * g1 + 16 * (jc + 1))
                    pS.inc(e.tensor_tensor(S_row[par][:, jc * 512:(jc + 1) * 512],
                                           ps_a[2 + cn % 2][:, :],
                                           sk_sb[par][:, jc * 512:(jc + 1) * 512], ALU.add), 1)
                st_(e.reduce_max(mxb[:, par * 4:par * 4 + 1], S_row[par][:, :], axis=mybir.AxisListType.X))
                pmx.inc(e.tensor_scalar(mxb[:, par * 4 + 1:par * 4 + 2],
                                        mxb[:, par * 4:par * 4 + 1], -0.125, None, ALU.mult), 1)
            if 2 <= oi < NSL + 2:
                g2 = oi - 2
                par = g2 % 2
                e.wait_ge(pex.h, b_pex + g2 + 1)
                st_(e.reciprocal(mxb[:, par * 4 + 3:par * 4 + 4], mxb[:, par * 4 + 2:par * 4 + 3]))
                psc.inc(e.tensor_scalar(expS[par][:, :], expS[par][:, :],
                                        mxb[:, par * 4 + 3:par * 4 + 4], None, ALU.mult), 1)
            if oi >= 3:
                g3 = oi - 3
                h, ic = divmod(g3, 16)
                par = g3 % 2
                if g3 >= 2:
                    e.wait_ge(pav.h, b_pav + g3 - 1)
                for gq in range(4):
                    gqi = 4 * g3 + gq
                    e.wait_ge(ptr.h, b_ptr + gqi + 1)
                    pte.inc(e.tensor_copy(expST[par][:, gq * 512:(gq + 1) * 512], ps_t[gqi % 2][:, :]), 1)
                e.wait_ge(pav.h, b_pav + g3 + 1)
                poe.inc(e.tensor_scalar(OT[64 * (h % 2):64 * (h % 2) + 64,
                                           (h // 2) * T + ic * 128:(h // 2) * T + ic * 128 + 128],
                                        ps_o[par][0:64, :],
                                        bvT[64 * (h % 2):64 * (h % 2) + 64, h // 2:h // 2 + 1],
                                        None, ALU.add), 1)
    p.block.vector(vec_hl)
    heads_done_poe = poe.n
    heads_done_pav = pav.n
    hl.close()
    qkv.close()

    wo = contextlib.ExitStack()
    xs = phase.enter_context(nc.sbuf_tensor("at_xs", [128, KT * T], F32))
    wstg = [wo.enter_context(nc.sbuf_tensor(f"at_wstg{i}", [128, D], F32)) for i in range(2)]
    wo_sb = wo.enter_context(nc.sbuf_tensor("at_wo", [128, KT * D], BF16))
    boT = wo.enter_context(nc.sbuf_tensor("at_bo", [128, KT], F32))
    t1b = wo.enter_context(nc.sbuf_tensor("at_t1", [128, 512], F32))
    ps_w = [wo.enter_context(nc.psum_tensor(f"at_psw{i}", [128, 512], F32)) for i in range(4)]

    b_ld2 = ld.n
    b_pg2 = pg.n
    b_mm2 = mm.n
    b_ev2 = ev.n

    def sync_wo(e):
        e.wait_ge(poe.h, heads_done_poe)
        e.wait_ge(pav.h, heads_done_pav)
        for k in range(KT):
            ld.inc(e.dma_start(out=xs[:, k * T:(k + 1) * T], in_=t["xs_dram"][k * 128:(k + 1) * 128, :]), 16)
        for m in range(KT):
            ld.inc(e.dma_start(out=boT[:, m:m + 1], in_=t["bo_hbm"][m * 128:(m + 1) * 128].unsqueeze(1)), 16)
        for kt in range(KT):
            if kt >= 2:
                e.wait_ge(pg.h, b_pg2 + kt - 1)
            ld.inc(e.dma_start(out=wstg[kt % 2][:, :], in_=t["woT_hbm"][kt * 128:(kt + 1) * 128, :]), 16)
    p.block.sync(sync_wo)

    def gp_wo(e):
        for kt in range(KT):
            e.wait_ge(ld.h, b_ld2 + 16 * (16 + kt + 1))
            pg.inc(e.tensor_copy(wo_sb[:, kt * D:(kt + 1) * D], wstg[kt % 2][:, :]), 1)
    p.block.gpsimd(gp_wo)

    def tens_wo(e):
        e.wait_ge(pg.h, b_pg2 + KT)
        for g in range(32):
            tc, mt = divmod(g, KT)
            if g >= 4:
                e.wait_ge(ev.h, b_ev2 + g - 3)
            ins = None
            for k in range(KT):
                ins = e.matmul(ps_w[g % 4][:, :], wo_sb[:, k * D + mt * 128:k * D + mt * 128 + 128],
                               OT[:, k * T + tc * 512:k * T + tc * 512 + 512],
                               start=(k == 0), stop=(k == KT - 1))
            mm.inc(ins, 1)
    p.block.tensor(tens_wo)

    def vec_wo(e):
        e.wait_ge(ld.h, b_ld2 + 16 * 16)
        for g in range(32):
            tc, mt = divmod(g, KT)
            e.wait_ge(mm.h, b_mm2 + g + 1)
            sl = slice(mt * T + tc * 512, mt * T + tc * 512 + 512)
            e.tensor_scalar(t1b[:, :], ps_w[g % 4][:, :], boT[:, mt:mt + 1], None, ALU.add)
            ev.inc(e.tensor_tensor(xs[:, sl], xs[:, sl], t1b[:, :], ALU.add), 1)
    p.block.vector(vec_wo)
    y_ready = ev.n
    wo.close()

    _emit_ln(p, xs, y_ready, tag="ln3")
    p._attn_ret = (xs, phase)


def _emit_ffn(p):
    nc, s, t = p.nc, p.s, p.t
    ld, mm, ev, act, ln, st = s["ld"], s["mm"], s["ev"], s["act"], s["ln"], s["st"]
    sgu = s["sgu"]; sdn = s["sdn"]; sat = s["sat"]; hh = s["hh"]
    fev = s["ssl"]; swd = s["swd"]; pmx = s["pmx"]; pgd = s["pgd"]; pj = s["pj"]
    po = s["po"]
    TH = 512
    NF1 = 32

    phase = contextlib.ExitStack()
    xs = phase.enter_context(nc.sbuf_tensor("ffn_xs", [128, KT * T], F32))
    inner = contextlib.ExitStack()
    xb = inner.enter_context(nc.sbuf_tensor("ffn_xb", [128, KT * TH], BF16))
    h1T = inner.enter_context(nc.sbuf_tensor("ffn_h1", [128, NF1 * TH], BF16))
    h2T = inner.enter_context(nc.sbuf_tensor("ffn_h2", [128, NF1 * TH], BF16))
    k0c = [inner.enter_context(nc.sbuf_tensor(f"ffn_k0c{i}", [128, KT * 128], BF16)) for i in range(2)]
    kc = [inner.enter_context(nc.sbuf_tensor(f"ffn_kc{i}", [128, NF1 * 128], BF16)) for i in range(2)]
    s0a = [inner.enter_context(nc.sbuf_tensor(f"ffn_s0a{i}", [128, KT * 128], U8)) for i in range(2)]
    s0b = [inner.enter_context(nc.sbuf_tensor(f"ffn_s0b{i}", [128, KT * 128], U8)) for i in range(2)]
    s1g = [inner.enter_context(nc.sbuf_tensor(f"ffn_s1{i}", [128, NF1 * 128], U8)) for i in range(2)]
    rs = inner.enter_context(nc.sbuf_tensor("ffn_rs", [1, 2 * TH], F32))
    svec = inner.enter_context(nc.sbuf_tensor("ffn_svec", [128, 4], F32))
    t1b = inner.enter_context(nc.sbuf_tensor("ffn_t1", [128, 512], F32))
    ps1 = [inner.enter_context(nc.psum_tensor(f"ffn_ps{i}", [128, 512], F32)) for i in range(4)]
    ps_r = [inner.enter_context(nc.psum_tensor(f"ffn_psr{i}", [1, 512], F32)) for i in range(2)]
    ps_s = inner.enter_context(nc.psum_tensor("ffn_pss", [128, 4], F32))

    stream_in_dram = p.stream_in_dram
    mm_fence = mm.n
    b_ld = ld.n
    b_pgd = pgd.n
    b_po = po.n
    b_gu, b_dn, b_at, b_hh, b_fev, b_wd, b_pmx, b_mm, b_pj = (
        x.n for x in (sgu, sdn, sat, hh, fev, swd, pmx, mm, pj))

    def tens_s(e):
        e.wait_ge(st.h, stream_in_dram)
        e.wait_ge(mm.h, mm_fence)
        e.wait_ge(s["init"].h, p.init_done)
        mm.inc(e.matmul(ps_s[:, 0:3], t["ones_row"][0:1, :], t["s012_sb"][0:1, 0:3],
                        start=True, stop=True), 1)
    p.block.tensor(tens_s)

    def vec_s(e):
        tp = s["tp"]
        e.wait_ge(mm.h, b_mm + 1)
        tp.inc(e.tensor_scalar(svec[:, 0:1], ps_s[:, 0:1], 4.0, None, ALU.mult), 1)
        tp.inc(e.tensor_scalar(svec[:, 1:2], ps_s[:, 1:2], 4.0, None, ALU.mult), 1)
        tp.inc(e.tensor_scalar(svec[:, 2:3], ps_s[:, 2:3], 2.0, None, ALU.mult), 1)
        e.wait_ge(tp.h, tp.n)
        pj.inc(e.tensor_copy(svec[:, 3:4], svec[:, 0:1]), 1)
    p.block.vector(vec_s)
    svec_done = pj.n

    QO = [8, 8 + 104, 8 + 104 + 72, 8 + 104 + 144]

    def lda(q, idx):
        return b_ld + (QO[q] + idx) * 16 if q > 0 else b_ld + (8 + idx) * 16

    def swd_k0(i):
        return b_wd + i + 1

    def swd_k1(i):
        return b_wd + 32 + i + 1

    def swd_k2(i):
        return b_wd + 64 + i + 1

    C0 = 0
    C1 = 32 * 128 * KT * 128
    C2 = C1 + 32 * 128 * NF1 * 128

    def sync(e):
        e.wait_ge(st.h, stream_in_dram)
        e.wait_ge(mm.h, mm_fence)
        for k in range(KT):
            ld.inc(e.dma_start(out=xs[:, k * T:(k + 1) * T], in_=t["xs_dram"][k * 128:(k + 1) * 128, :]), 16)
        for i in range(NF1):
            if i >= 2:
                e.wait_ge(swd.h, swd_k0(i - 2))
            ld.inc(e.dma_start(out=s0a[i % 2][:, :].rearrange("p (k c) -> p k c", k=KT),
                               in_=t["k0a_hbm"].rearrange("(k p) f -> p k f", p=128)[:, :, i * 128:(i + 1) * 128]), 16)
            ld.inc(e.dma_start(out=s0b[i % 2][:, :].rearrange("p (k c) -> p k c", k=KT),
                               in_=t["k0b_hbm"].rearrange("(k p) f -> p k f", p=128)[:, :, i * 128:(i + 1) * 128]), 16)
        for i in range(NF1):
            if i >= 2:
                e.wait_ge(swd.h, swd_k1(i - 2))
            ld.inc(e.dma_start(out=s1g[i % 2][:, :].rearrange("p (k c) -> p k c", k=NF1),
                               in_=t["k1_hbm"].rearrange("(k p) f -> p k f", p=128)[:, :, i * 128:(i + 1) * 128]), 16)
        for i in range(KT):
            if i >= 2:
                e.wait_ge(swd.h, swd_k2(i - 2))
            else:
                e.wait_ge(swd.h, swd_k1(30 + i))
            ld.inc(e.dma_start(out=s1g[i % 2][:, :].rearrange("p (k c) -> p k c", k=NF1),
                               in_=t["k2_hbm"].rearrange("(k p) f -> p k f", p=128)[:, :, i * 128:(i + 1) * 128]), 16)
        for q in range(1, 4):
            for i in range(NF1):
                gi = q * NF1 + i
                e.wait_ge(po.h, b_po + 16 * (i + 1) if q == 1 else b_po)
                e.wait_ge(sgu.h, (b_gu + 32 * q + i - 1) if i >= 2 else (b_gu + 32 * q))
                ld.inc(e.dma_start(out=k0c[gi % 2][:, :],
                                   in_=t["wck"][C0 + i * 128 * KT * 128:C0 + (i + 1) * 128 * KT * 128]
                                   .rearrange("(a c) -> a c", a=128)), 16)
            for i in range(NF1):
                gi = q * NF1 + i
                e.wait_ge(po.h, b_po + 16 * (32 + i + 1) if q == 1 else b_po)
                if i >= 2:
                    e.wait_ge(sdn.h, b_dn + 32 * q + i - 1)
                else:
                    e.wait_ge(sat.h, b_at + 8 * q)
                ld.inc(e.dma_start(out=kc[gi % 2][:, :],
                                   in_=t["wck"][C1 + i * 128 * NF1 * 128:C1 + (i + 1) * 128 * NF1 * 128]
                                   .rearrange("(a c) -> a c", a=128)), 16)
            for i in range(KT):
                gi = q * KT + i
                e.wait_ge(po.h, b_po + 16 * (64 + i + 1) if q == 1 else b_po)
                if i >= 2:
                    e.wait_ge(sat.h, b_at + 8 * q + i - 1)
                else:
                    e.wait_ge(sdn.h, b_dn + 32 * (q + 1))
                ld.inc(e.dma_start(out=kc[gi % 2][:, :],
                                   in_=t["wck"][C2 + i * 128 * NF1 * 128:C2 + (i + 1) * 128 * NF1 * 128]
                                   .rearrange("(a c) -> a c", a=128)), 16)
    p.block.sync(sync)

    def gp(e):
        for q in range(4):
            toff = q * TH
            if q == 0:
                e.wait_ge(st.h, stream_in_dram)
                e.wait_ge(mm.h, mm_fence)
            else:
                e.wait_ge(sgu.h, b_gu + 32 * q)
            for k in range(KT):
                pgd.inc(e.dma_start(out=xb[:, k * TH:(k + 1) * TH],
                                    in_=t["xs_dram"][k * 128:(k + 1) * 128, toff:toff + TH]), 16)
            if q == 0:
                for i in range(NF1):
                    e.wait_ge(ld.h, lda(0, 2 * (i + 1)))
                    if i >= 2:
                        e.wait_ge(sgu.h, b_gu + i - 1)
                    swd.inc(e.tensor_tensor(k0c[i % 2][:, :], s0a[i % 2][:, :], s0b[i % 2][:, :], ALU.subtract), 1)
                    po.inc(e.dma_start(out=t["wck"][C0 + i * 128 * KT * 128:C0 + (i + 1) * 128 * KT * 128]
                                       .rearrange("(a c) -> a c", a=128), in_=k0c[i % 2][:, :]), 16)
                for i in range(NF1):
                    e.wait_ge(ld.h, lda(0, 64 + i + 1))
                    if i >= 2:
                        e.wait_ge(sdn.h, b_dn + i - 1)
                    if i < 2:
                        e.wait_ge(sat.h, b_at)
                    swd.inc(e.tensor_copy(kc[i % 2][:, :], s1g[i % 2][:, :]), 1)
                    po.inc(e.dma_start(out=t["wck"][C1 + i * 128 * NF1 * 128:C1 + (i + 1) * 128 * NF1 * 128]
                                       .rearrange("(a c) -> a c", a=128), in_=kc[i % 2][:, :]), 16)
                for i in range(KT):
                    e.wait_ge(ld.h, lda(0, 96 + i + 1))
                    if i < 2:
                        e.wait_ge(sdn.h, b_dn + 32)
                    else:
                        e.wait_ge(sat.h, b_at + i - 1)
                    swd.inc(e.tensor_copy(kc[i % 2][:, :], s1g[i % 2][:, :]), 1)
                    po.inc(e.dma_start(out=t["wck"][C2 + i * 128 * NF1 * 128:C2 + (i + 1) * 128 * NF1 * 128]
                                       .rearrange("(a c) -> a c", a=128), in_=kc[i % 2][:, :]), 16)
    p.block.gpsimd(gp)

    def tens(e):
        e.wait_ge(pj.h, svec_done)
        for q in range(4):
            e.wait_ge(pgd.h, b_pgd + 16 * KT * (q + 1))
            for i in range(NF1):
                if q == 0:
                    e.wait_ge(swd.h, swd_k0(i))
                    e.wait_ge(po.h, b_po + 16 * (i + 1))
                else:
                    e.wait_ge(ld.h, lda(q, i + 1))
                gg = 72 * q + i
                if gg >= 4:
                    e.wait_ge(fev.h, b_fev + gg - 3)
                ins = None
                for k in range(KT):
                    ins = e.matmul(ps1[gg % 4][:, :], k0c[(q * NF1 + i) % 2][:, k * 128:(k + 1) * 128],
                                   xb[:, k * TH:k * TH + TH],
                                   start=(k == 0), stop=(k == KT - 1))
                sgu.inc(ins, 1)
            e.wait_ge(fev.h, b_fev + 72 * q + 32)
            e.wait_ge(pmx.h, b_pmx + 2 * q)
            ins = None
            for k in range(NF1):
                ins = e.matmul(ps_r[0][0:1, :], t["ones_col_b"][:, 0:1],
                               h1T[:, k * TH:(k + 1) * TH], start=(k == 0), stop=(k == NF1 - 1))
            hh.inc(ins, 1)
            for i in range(NF1):
                if q == 0:
                    e.wait_ge(swd.h, swd_k1(i))
                else:
                    e.wait_ge(ld.h, lda(q, 32 + i + 1))
                gg = 72 * q + 32 + i
                if gg >= 4:
                    e.wait_ge(fev.h, b_fev + gg - 3)
                if i == 0:
                    e.wait_ge(pmx.h, b_pmx + 2 * q + 1)
                ins = None
                for k in range(NF1):
                    ins = e.matmul(ps1[gg % 4][:, :], kc[(q * NF1 + i) % 2][:, k * 128:(k + 1) * 128],
                                   h1T[:, k * TH:(k + 1) * TH], start=(k == 0), stop=False)
                ins = e.matmul(ps1[gg % 4][:, :], t["ones_row"][0:1, :],
                               rs[0:1, 0:TH], start=False, stop=True)
                sdn.inc(ins, 1)
            e.wait_ge(fev.h, b_fev + 72 * q + 64)
            e.wait_ge(pmx.h, b_pmx + 2 * q)
            ins = None
            for k in range(NF1):
                ins = e.matmul(ps_r[1][0:1, :], t["ones_col_b"][:, 0:1],
                               h2T[:, k * TH:(k + 1) * TH], start=(k == 0), stop=(k == NF1 - 1))
            hh.inc(ins, 1)
            for i in range(KT):
                if q == 0:
                    e.wait_ge(swd.h, swd_k2(i))
                else:
                    e.wait_ge(ld.h, lda(q, 64 + i + 1))
                gg = 72 * q + 64 + i
                if gg >= 4:
                    e.wait_ge(fev.h, b_fev + gg - 3)
                if i == 0:
                    e.wait_ge(pmx.h, b_pmx + 2 * q + 2)
                ins = None
                for k in range(NF1):
                    ins = e.matmul(ps1[gg % 4][:, :], kc[(q * KT + i) % 2][:, k * 128:(k + 1) * 128],
                                   h2T[:, k * TH:(k + 1) * TH], start=(k == 0), stop=False)
                ins = e.matmul(ps1[gg % 4][:, :], t["ones_row"][0:1, :],
                               rs[0:1, TH:2 * TH], start=False, stop=True)
                sat.inc(ins, 1)
    p.block.tensor(tens)

    def vec(e):
        e.wait_ge(ld.h, b_ld + 16 * KT)
        for q in range(4):
            for i in range(NF1):
                gg = 72 * q + i
                e.wait_ge(sgu.h, b_gu + 32 * q + i + 1)
                if q >= 1 and i == 0:
                    e.wait_ge(sdn.h, b_dn + 32 * q)
                    e.wait_ge(hh.h, b_hh + 2 * q - 1)
                fev.inc(e.tensor_scalar(h1T[:, i * TH:(i + 1) * TH],
                                        ps1[gg % 4][:, :], svec[:, 0:1], 0.0, ALU.mult, ALU.max), 1)
            e.wait_ge(hh.h, b_hh + 2 * q + 1)
            pmx.inc(e.tensor_scalar(rs[0:1, 0:TH], ps_r[0][0:1, :], -0.5, None, ALU.mult), 1)
            for i in range(NF1):
                gg = 72 * q + 32 + i
                e.wait_ge(sdn.h, b_dn + 32 * q + i + 1)
                if q >= 1 and i == 0:
                    e.wait_ge(sat.h, b_at + 8 * q)
                    e.wait_ge(hh.h, b_hh + 2 * q)
                fev.inc(e.tensor_scalar(h2T[:, i * TH:(i + 1) * TH],
                                        ps1[gg % 4][:, :], svec[:, 1:2], 0.0, ALU.mult, ALU.max), 1)
            e.wait_ge(hh.h, b_hh + 2 * q + 2)
            pmx.inc(e.tensor_scalar(rs[0:1, TH:2 * TH], ps_r[1][0:1, :], -0.5, None, ALU.mult), 1)
            for i in range(KT):
                gg = 72 * q + 64 + i
                e.wait_ge(sat.h, b_at + 8 * q + i + 1)
                e.tensor_scalar(t1b[:, :], ps1[gg % 4][:, :], svec[:, 2:3], None, ALU.mult)
                sl = slice(i * T + q * TH, i * T + q * TH + TH)
                fev.inc(e.tensor_tensor(xs[:, sl], xs[:, sl], t1b[:, :], ALU.add), 1)
    p.block.vector(vec)

    def vec_bridge(e):
        for k in range(KT):
            e.wait_ge(fev.h, b_fev + 72 * 3 + 64 + k + 1)
            ev.inc(e.tensor_copy(t["junk"][:, 0:1], t["junk"][:, 1:2]), 1)
    p.block.vector(vec_bridge)
    y_ready = ev.n
    inner.close()

    _emit_ln(p, xs, y_ready, tag="ln4")
    p._ffn_ret = (xs, phase)


def build_program(phases=("l0",)):
    nc = bass.Bass("TRN2", target_bir_lowering=False, debug=False)
    t = {}
    t["xT_hbm"] = nc.dram_tensor("xT", [D, T], F32, kind="ExternalInput").ap()
    t["ident_hbm"] = nc.dram_tensor("ident", [128, 128], F32, kind="ExternalInput").ap()
    t["s012_hbm"] = nc.dram_tensor("s012", [3], F32, kind="ExternalInput").ap()
    if "l0" in phases:
        t["wiT_hbm"] = nc.dram_tensor("wiT", [D, D], F32, kind="ExternalInput").ap()
        t["whT_hbm"] = nc.dram_tensor("whT", [D, D], F32, kind="ExternalInput").ap()
        t["bi_hbm"] = nc.dram_tensor("bi", [D], F32, kind="ExternalInput").ap()
        t["bh_hbm"] = nc.dram_tensor("bh", [D], F32, kind="ExternalInput").ap()
    if "moe" in phases:
        for e in range(8):
            t[f"wg{e}T_hbm"] = nc.dram_tensor(f"wg{e}T", [D, 2048], F32, kind="ExternalInput").ap()
            t[f"wu{e}T_hbm"] = nc.dram_tensor(f"wu{e}T", [D, 2048], F32, kind="ExternalInput").ap()
            t[f"wd{e}T_hbm"] = nc.dram_tensor(f"wd{e}T", [2048, D], F32, kind="ExternalInput").ap()
        t["wgsT_hbm"] = nc.dram_tensor("wgsT", [D, 2048], F32, kind="ExternalInput").ap()
        t["wusT_hbm"] = nc.dram_tensor("wusT", [D, 2048], F32, kind="ExternalInput").ap()
        t["wdsT_hbm"] = nc.dram_tensor("wdsT", [2048, D], F32, kind="ExternalInput").ap()
        t["wrT_hbm"] = nc.dram_tensor("wrT", [D, 8], F32, kind="ExternalInput").ap()
        t["rb_hbm"] = nc.dram_tensor("rb", [8], F32, kind="ExternalInput").ap()
    if "attn" in phases:
        for w in ["wq", "wk", "wv", "wo"]:
            t[w + "T_hbm"] = nc.dram_tensor(w + "T", [D, D], F32, kind="ExternalInput").ap()
        for b in ["bq", "bk", "bv", "bo"]:
            t[b + "_hbm"] = nc.dram_tensor(b, [D], F32, kind="ExternalInput").ap()
        t["rextT_hbm"] = nc.dram_tensor("rextT", [64, 4096], F32, kind="ExternalInput").ap()
        t["pscr_h"] = nc.dram_tensor("pscr", [2, 16 * 128 * 2176], BF16, kind="Internal")
        t["pscr"] = t["pscr_h"].ap()
    if "ffn" in phases:
        t["k0a_hbm"] = nc.dram_tensor("k0a", [D, 4096], U8, kind="ExternalInput").ap()
        t["k0b_hbm"] = nc.dram_tensor("k0b", [D, 4096], U8, kind="ExternalInput").ap()
        t["k1_hbm"] = nc.dram_tensor("k1", [4096, 4096], U8, kind="ExternalInput").ap()
        t["k2_hbm"] = nc.dram_tensor("k2", [4096, D], U8, kind="ExternalInput").ap()
        t["wck"] = nc.dram_tensor("wck", [32 * 128 * KT * 128 + 40 * 128 * 32 * 128], BF16, kind="Internal").ap()
    t["outT_hbm"] = nc.dram_tensor("outT", [D, T], F32, kind="ExternalOutput").ap()
    t["xs_dram"] = nc.dram_tensor("xs_dram", [D, T], F32, kind="Internal").ap()

    with contextlib.ExitStack() as es:
        t["ident"] = es.enter_context(nc.sbuf_tensor("identsb", [128, 128], F32))
        t["identb"] = es.enter_context(nc.sbuf_tensor("identbsb", [128, 128], BF16))
        t["ones_col"] = es.enter_context(nc.sbuf_tensor("ones_col", [128, 1], F32))
        t["ones_col_b"] = es.enter_context(nc.sbuf_tensor("ones_col_b", [128, 1], BF16))
        t["ones_row"] = es.enter_context(nc.sbuf_tensor("ones_row", [1, 128], F32))
        t["ones_row_b"] = es.enter_context(nc.sbuf_tensor("ones_row_b", [1, 128], BF16))
        t["s012_sb"] = es.enter_context(nc.sbuf_tensor("s012_sb", [1, 3], F32))
        t["junk"] = es.enter_context(nc.sbuf_tensor("junk", [128, 2], BF16))

        sems = {}
        for name in ["ld", "mm", "ev", "act", "hh", "init", "ln", "st",
                     "sgu", "sdn", "ssl", "sat", "sde", "swu", "swd", "scw", "tp",
                     "pj", "pg", "pgd", "pp", "pa", "po", "pi", "pc",
                     "pS", "pmx", "pex", "psc", "ptr", "pte", "pav", "poe"]:
            sems[name] = Sem(es.enter_context(nc.semaphore(name)))
        block = es.enter_context(nc.Block())
        p = P(nc, block, es, sems, t)

        _emit_init(p)
        p.stream_in_dram = 0
        last = phases[-1]

        def store(xs, scope, is_last):
            st = sems["st"]
            stream_ready = p.stream_ready
            st0 = st.n
            dst = t["outT_hbm"] if is_last else t["xs_dram"]

            def gp(e):
                e.wait_ge(sems["ln"].h, stream_ready)
                for k in range(KT):
                    st.inc(e.dma_start(out=dst[k * 128:(k + 1) * 128, :], in_=xs[:, k * T:(k + 1) * T]), 16)
                if is_last:
                    e.wait_ge(st.h, st0 + 16 * KT)
            p.block.gpsimd(gp)
            p.stream_in_dram = st.n
            scope.close()

        if "l0" in phases:
            _emit_l0(p, t["xT_hbm"])
            store(*p._l0_ret, last == "l0")
        if "moe" in phases:
            _emit_moe(p)
            store(*p._moe_ret, last == "moe")
        if "attn" in phases:
            _emit_attn(p)
            store(*p._attn_ret, last == "attn")
        if "ffn" in phases:
            _emit_ffn(p)
            store(*p._ffn_ret, last == "ffn")
    return nc


def _host_inputs_per_core(inputs, b, phases):
    x = np.asarray(inputs["L0_attn_x"])[b]
    m = {
        "xT": np.ascontiguousarray(x.T),
        "ident": np.eye(128, dtype=np.float32),
        "s012": np.stack([np.float32(inputs["L1_ffn_s0"]), np.float32(inputs["L1_ffn_s1"]),
                          np.float32(inputs["L1_ffn_s2"])]).astype(np.float32),
    }
    if "l0" in phases:
        m["wiT"] = np.ascontiguousarray(np.asarray(inputs["L0_attn_Wi"]).T)
        m["whT"] = np.ascontiguousarray(np.asarray(inputs["L0_attn_Wh"]).T)
        m["bi"] = np.asarray(inputs["L0_attn_bi"])
        m["bh"] = np.asarray(inputs["L0_attn_bh"])
    if "moe" in phases:
        for e in range(8):
            m[f"wg{e}T"] = np.ascontiguousarray(np.asarray(inputs["L0_ffn_Wg"])[e].T)
            m[f"wu{e}T"] = np.ascontiguousarray(np.asarray(inputs["L0_ffn_Wu"])[e].T)
            m[f"wd{e}T"] = np.ascontiguousarray(np.asarray(inputs["L0_ffn_Wd"])[e].T)
        m["wgsT"] = np.ascontiguousarray(np.asarray(inputs["L0_ffn_Wg_s"]).T)
        m["wusT"] = np.ascontiguousarray(np.asarray(inputs["L0_ffn_Wu_s"]).T)
        m["wdsT"] = np.ascontiguousarray(np.asarray(inputs["L0_ffn_Wd_s"]).T)
        m["wrT"] = np.ascontiguousarray(np.asarray(inputs["L0_ffn_Wr"]).T)
        m["rb"] = np.asarray(inputs["L0_ffn_rb"])
    if "attn" in phases:
        m["wqT"] = np.ascontiguousarray(np.asarray(inputs["L1_attn_Wq"]).T)
        m["wkT"] = np.ascontiguousarray(np.asarray(inputs["L1_attn_Wk"]).T)
        m["wvT"] = np.ascontiguousarray(np.asarray(inputs["L1_attn_Wv"]).T)
        m["woT"] = np.ascontiguousarray(np.asarray(inputs["L1_attn_Wo"]).T)
        m["bq"] = np.asarray(inputs["L1_attn_bq"])
        m["bk"] = np.asarray(inputs["L1_attn_bk"])
        m["bv"] = np.asarray(inputs["L1_attn_bv"])
        m["bo"] = np.asarray(inputs["L1_attn_bo"])
        rt = np.asarray(inputs["L1_attn_rel_table"])
        rext = rt[np.clip(3070 - np.arange(4096), 0, 2046)]
        m["rextT"] = np.ascontiguousarray(rext.T)
    if "ffn" in phases:
        k0 = np.asarray(inputs["L1_ffn_k0"])
        m["k0a"] = np.ascontiguousarray(k0[:D]).view(np.uint8)
        m["k0b"] = np.ascontiguousarray(k0[D:]).view(np.uint8)
        m["k1"] = np.ascontiguousarray(np.asarray(inputs["L1_ffn_k1"])).view(np.uint8)
        m["k2"] = np.ascontiguousarray(np.asarray(inputs["L1_ffn_k2"])).view(np.uint8)
    return m


def kernel(**inputs):
    phases = ("l0", "moe", "attn", "ffn")
    nc = build_program(phases)
    in_maps = [_host_inputs_per_core(inputs, b, phases) for b in range(8)]
    res = run_bass_kernel_spmd(nc, in_maps, core_ids=list(range(8)))
    return np.stack([r["outT"].T for r in res.results], axis=0)


# version 04
# speedup vs baseline: 1.0218x; 1.0218x over previous
import contextlib
import os
import numpy as np
import concourse.bass as bass
import concourse.mybir as mybir
from concourse.bass_utils import run_bass_kernel_spmd

F32 = mybir.dt.float32
BF16 = mybir.dt.bfloat16
U8 = mybir.dt.uint8
AF = mybir.ActivationFunctionType
ALU = mybir.AluOpType

D = 1024
T = 2048
KT = D // 128
NTC = T // 512
LN_EPS = 1e-5


class Sem:
    def __init__(self, handle):
        self.h = handle
        self.n = 0

    def inc(self, instr, k=1):
        instr.then_inc(self.h, k)
        self.n += k
        return instr


class P:
    def __init__(self, nc, block, es, sems, tensors):
        self.nc = nc
        self.block = block
        self.es = es
        self.s = sems
        self.t = tensors
        self.init_done = 0
        self.stream_ready = 0


def _emit_init(p):
    nc, s, t = p.nc, p.s, p.t
    ld, init = s["ld"], s["init"]
    ld0 = ld.n

    def sync(e):
        ld.inc(e.dma_start(out=t["ident"][:], in_=t["ident_hbm"][:, :]), 16)
        ld.inc(e.dma_start(out=t["s012_sb"][0:1, :], in_=t["s012_hbm"].unsqueeze(0)), 16)
    p.block.sync(sync)

    def vec(e):
        init.inc(e.memset(t["ones_col"][:], 1.0), 1)
        init.inc(e.memset(t["ones_col_b"][:], 1.0), 1)
        init.inc(e.memset(t["ones_row"][:], 1.0), 1)
        init.inc(e.memset(t["ones_row_b"][:], 1.0), 1)
        e.wait_ge(ld.h, ld0 + 32)
        init.inc(e.tensor_copy(t["identb"][:], t["ident"][:]), 1)
    p.block.vector(vec)
    p.init_done = init.n


def _emit_l0(p, stream_src):
    nc, s, t = p.nc, p.s, p.t
    ld, mm, ev, act, hh = s["ld"], s["mm"], s["ev"], s["act"], s["hh"]

    phase = contextlib.ExitStack()
    xs = phase.enter_context(nc.sbuf_tensor("l0_xs", [128, KT * T], F32))
    xib_scope = contextlib.ExitStack()
    xibT = xib_scope.enter_context(nc.sbuf_tensor("xibT", [128, KT * T], F32))
    inner = contextlib.ExitStack()
    xb = inner.enter_context(nc.sbuf_tensor("l0_xb", [128, KT * T], BF16))
    stag = inner.enter_context(nc.sbuf_tensor("l0_stag", [128, 2 * D], F32))
    wiT_sb = inner.enter_context(nc.sbuf_tensor("wiT_sb", [128, KT * D], BF16))
    bihT = inner.enter_context(nc.sbuf_tensor("bihT", [128, 2 * KT], F32))
    ps_x = [inner.enter_context(nc.psum_tensor(f"l0ps{i}", [128, 512], F32)) for i in range(4)]

    ld0 = ld.n
    ev0 = ev.n

    def sync(e):
        for k in range(KT):
            ld.inc(e.dma_start(out=xs[:, k * T:(k + 1) * T], in_=stream_src[k * 128:(k + 1) * 128, :]), 16)
        for m in range(KT):
            ld.inc(e.dma_start(out=bihT[:, m:m + 1], in_=t["bi_hbm"][m * 128:(m + 1) * 128].unsqueeze(1)), 16)
            ld.inc(e.dma_start(out=bihT[:, KT + m:KT + m + 1], in_=t["bh_hbm"][m * 128:(m + 1) * 128].unsqueeze(1)), 16)
        for q in range(4):
            if q > 0:
                e.wait_ge(ev.h, ev0 + 1 + q)
            ld.inc(e.dma_start(out=stag[:, :].rearrange("p (k d) -> p k d", k=2),
                               in_=t["wiT_hbm"].rearrange("(k p) d -> p k d", p=128)[:, 2 * q:2 * q + 2, :]), 16)
    p.block.sync(sync)
    after_stream = ld0 + 16 * KT
    after_bias = after_stream + 16 * 2 * KT

    def vec(e):
        e.wait_ge(s["init"].h, p.init_done)
        e.wait_ge(ld.h, after_stream)
        for k in range(KT):
            e.tensor_copy(xb[:, k * T:(k + 1) * T], xs[:, k * T:(k + 1) * T])
        e.wait_ge(ld.h, after_bias)
        ev.inc(e.tensor_tensor(bihT[:, 0:KT], bihT[:, 0:KT], bihT[:, KT:2 * KT], ALU.add), 1)
        for q in range(4):
            e.wait_ge(ld.h, after_bias + 16 * (q + 1))
            ev.inc(e.tensor_copy(wiT_sb[:, q * 2 * D:(q + 1) * 2 * D], stag[:, :]), 1)
    p.block.vector(vec)
    wi_ready = ev.n

    mm0 = mm.n
    ev_xi0 = ev.n

    def tens(e):
        e.wait_ge(ev.h, wi_ready)
        for idx in range(KT * NTC):
            mt, tc = divmod(idx, NTC)
            if idx >= 4:
                e.wait_ge(ev.h, ev_xi0 + (idx - 4) + 1)
            pb = ps_x[idx % 4]
            ins = None
            for k in range(KT):
                ins = e.matmul(pb[:, :],
                               wiT_sb[:, k * D + mt * 128: k * D + mt * 128 + 128],
                               xb[:, k * T + tc * 512: k * T + tc * 512 + 512],
                               start=(k == 0), stop=(k == KT - 1))
            mm.inc(ins, 1)
    p.block.tensor(tens)

    def vec2(e):
        for idx in range(KT * NTC):
            mt, tc = divmod(idx, NTC)
            e.wait_ge(mm.h, mm0 + idx + 1)
            pb = ps_x[idx % 4]
            ev.inc(e.tensor_scalar(xibT[:, mt * T + tc * 512: mt * T + tc * 512 + 512],
                                   pb[:, :], bihT[:, mt:mt + 1], None, ALU.add), 1)
    p.block.vector(vec2)
    xib_done = ev.n
    mm_xi_done = mm.n
    inner.close()

    if os.environ.get("MK_DBG") == "xib":
        ln = s["ln"]
        def vend(e):
            e.wait_ge(ev.h, xib_done)
            for k in range(KT):
                e.tensor_copy(xs[:, k * T:(k + 1) * T], xibT[:, k * T:(k + 1) * T])
            ln.inc(e.tensor_copy(t["junk"][:, 0:1], t["junk"][:, 1:2]), 1)
        p.block.vector(vend)
        p.stream_ready = ln.n
        p._l0_ret = (xs, phase)
        return

    inner2 = contextlib.ExitStack()
    whT_sb = inner2.enter_context(nc.sbuf_tensor("whT_sb", [128, KT * D], BF16))
    stag_scope = contextlib.ExitStack()
    stag2 = stag_scope.enter_context(nc.sbuf_tensor("l0_stag2", [128, 2 * D], F32))

    ld_wh0 = ld.n
    ev_wh0 = ev.n

    def sync2(e):
        e.wait_ge(mm.h, mm_xi_done)
        for q in range(4):
            if q > 0:
                e.wait_ge(ev.h, ev_wh0 + q)
            ld.inc(e.dma_start(out=stag2[:, :].rearrange("p (k d) -> p k d", k=2),
                               in_=t["whT_hbm"].rearrange("(k p) d -> p k d", p=128)[:, 2 * q:2 * q + 2, :]), 16)
    p.block.sync(sync2)

    def vec3(e):
        for q in range(4):
            e.wait_ge(ld.h, ld_wh0 + 16 * (q + 1))
            ev.inc(e.tensor_copy(whT_sb[:, q * 2 * D:(q + 1) * 2 * D], stag2[:, :]), 1)
    p.block.vector(vec3)
    wh_ready = ev.n
    stag_scope.close()
    hsT = inner2.enter_context(nc.sbuf_tensor("hsT", [128, KT * (T + 1)], BF16))
    ps_h = [inner2.enter_context(nc.psum_tensor(f"rnnps{m}", [128, 512], F32)) for m in range(KT)]

    hh0 = hh.n

    def vec4(e):
        for m in range(KT):
            hh.inc(e.memset(hsT[:, m * (T + 1):m * (T + 1) + 1], 0.0), 1)
    p.block.vector(vec4)

    mm_rnn0 = mm.n

    def tens2(e):
        e.wait_ge(ev.h, wh_ready)
        for tt in range(1, T + 1):
            e.wait_ge(hh.h, hh0 + KT * tt)
            for m in range(KT):
                ins = None
                for k in range(KT):
                    ins = e.matmul(ps_h[m][:, 0:1],
                                   whT_sb[:, k * D + m * 128:k * D + m * 128 + 128],
                                   hsT[:, k * (T + 1) + (tt - 1):k * (T + 1) + tt],
                                   start=(k == 0), stop=(k == KT - 1))
                mm.inc(ins, 1)
    p.block.tensor(tens2)

    def scal(e):
        for tt in range(1, T + 1):
            for m in range(KT):
                e.wait_ge(mm.h, mm_rnn0 + KT * (tt - 1) + m + 1)
                hh.inc(e.activation(out=hsT[:, m * (T + 1) + tt:m * (T + 1) + tt + 1],
                                    in_=ps_h[m][:, 0:1], func=AF.Tanh,
                                    bias=xibT[:, m * T + (tt - 1):m * T + tt], scale=1.0), 1)
    p.block.scalar(scal)
    rnn_done = hh.n

    if os.environ.get("MK_DBG") == "whs":
        ln = s["ln"]
        def vwhs(e):
            e.wait_ge(ev.h, wh_ready)
            for q in range(4):
                e.tensor_copy(xs[:, q * 2 * D:(q + 1) * 2 * D], whT_sb[:, q * 2 * D:(q + 1) * 2 * D])
            e.wait_ge(hh.h, hh0 + KT)
            for m in range(KT):
                e.tensor_copy(xs[:, 4 * 2 * D + m: 4 * 2 * D + m + 1], hsT[:, m * (T + 1):m * (T + 1) + 1])
            ln.inc(e.tensor_copy(t["junk"][:, 0:1], t["junk"][:, 1:2]), 1)
        p.block.vector(vwhs)
        p.stream_ready = ln.n
        inner2.close()
        p._l0_ret = (xs, phase)
        return

    if os.environ.get("MK_DBG") == "rnn":
        ln = s["ln"]
        def vrnn(e):
            e.wait_ge(hh.h, rnn_done)
            for k in range(KT):
                e.tensor_copy(xs[:, k * T:(k + 1) * T], hsT[:, k * (T + 1) + 1:k * (T + 1) + T + 1])
            ln.inc(e.tensor_copy(t["junk"][:, 0:1], t["junk"][:, 1:2]), 1)
        p.block.vector(vrnn)
        p.stream_ready = ln.n
        inner2.close()
        p._l0_ret = (xs, phase)
        return

    def vec5(e):
        e.wait_ge(hh.h, rnn_done)
        for k in range(KT):
            ev.inc(e.tensor_tensor(xs[:, k * T:(k + 1) * T], xs[:, k * T:(k + 1) * T],
                                   hsT[:, k * (T + 1) + 1:k * (T + 1) + T + 1], ALU.add), 1)
    p.block.vector(vec5)
    y_ready = ev.n
    inner2.close()
    xib_scope.close()
    p._l0_ret = (xs, phase)

    if os.environ.get("MK_DBG") == "noln":
        ln = s["ln"]
        def vend(e):
            e.wait_ge(ev.h, y_ready)
            ln.inc(e.tensor_copy(t["junk"][:, 0:1], t["junk"][:, 1:2]), 1)
        p.block.vector(vend)
        p.stream_ready = ln.n
    else:
        _emit_ln(p, xs, y_ready, tag="ln1")


def _emit_ln(p, xs, ev_y_ready, tag):
    nc, s, t = p.nc, p.s, p.t
    mm, ev, act, ln = s["mm"], s["ev"], s["act"], s["ln"]

    inner = contextlib.ExitStack()
    ysq = inner.enter_context(nc.sbuf_tensor(f"{tag}_ysq", [128, T], F32))
    rows = inner.enter_context(nc.sbuf_tensor(f"{tag}_rows", [1, 4 * T], F32))
    MU, VB, RA, TMP = (i * T for i in range(4))

    act0 = act.n
    mm0 = mm.n

    stats = contextlib.ExitStack()
    ps_s = stats.enter_context(nc.psum_tensor(f"{tag}_ps_s", [1, T], F32))
    ps_q = stats.enter_context(nc.psum_tensor(f"{tag}_ps_q", [1, T], F32))

    def scal(e):
        for k in range(KT):
            e.wait_ge(ev.h, ev_y_ready - KT + k + 1)
            if k >= 1:
                e.wait_ge(mm.h, mm0 + k * NTC)
            act.inc(e.activation(out=ysq[:, :], in_=xs[:, k * T:(k + 1) * T], func=AF.Square), 1)
    p.block.scalar(scal)

    def tens(e):
        e.wait_ge(s["init"].h, p.init_done)
        e.wait_ge(ev.h, ev_y_ready)
        for k in range(KT):
            e.wait_ge(act.h, act0 + k + 1)
            for tc in range(NTC):
                e.matmul(ps_s[0:1, tc * 512:(tc + 1) * 512], t["ones_col"][:, 0:1],
                         xs[:, k * T + tc * 512: k * T + tc * 512 + 512],
                         start=(k == 0), stop=(k == KT - 1))
                mm.inc(e.matmul(ps_q[0:1, tc * 512:(tc + 1) * 512], t["ones_col"][:, 0:1],
                                ysq[:, tc * 512:(tc + 1) * 512],
                                start=(k == 0), stop=(k == KT - 1)), 1)
    p.block.tensor(tens)
    mm_stats = mm.n

    ev_v0 = ev.n

    def vec(e):
        e.wait_ge(mm.h, mm_stats)
        e.tensor_scalar(rows[0:1, MU:MU + T], ps_s[0:1, :], 1.0 / D, None, ALU.mult)
        e.tensor_scalar(rows[0:1, TMP:TMP + T], ps_q[0:1, :], 1.0 / D, None, ALU.mult)
        e.tensor_tensor(rows[0:1, RA:RA + T], rows[0:1, MU:MU + T], rows[0:1, MU:MU + T], ALU.mult)
        ev.inc(e.tensor_tensor(rows[0:1, VB:VB + T], rows[0:1, TMP:TMP + T], rows[0:1, RA:RA + T], ALU.subtract), 1)
    p.block.vector(vec)
    ev_var = ev.n

    def vec1b(e):
        e.wait_ge(ev.h, ev_var)
        ev.inc(e.tensor_scalar(rows[0:1, VB:VB + T], rows[0:1, VB:VB + T], LN_EPS, None, ALU.add), 1)
    p.block.vector(vec1b)
    ev_vareps = ev.n

    def scal2(e):
        e.wait_ge(ev.h, ev_vareps)
        e.activation(out=rows[0:1, TMP:TMP + T], in_=rows[0:1, VB:VB + T], func=AF.Ln, scale=1.0)
        act.inc(e.activation(out=rows[0:1, RA:RA + T], in_=rows[0:1, TMP:TMP + T], func=AF.Exp, scale=-0.5), 1)
    p.block.scalar(scal2)
    act_rstd = act.n

    def vec2(e):
        e.wait_ge(act.h, act_rstd)
        e.tensor_tensor(rows[0:1, TMP:TMP + T], rows[0:1, MU:MU + T], rows[0:1, RA:RA + T], ALU.mult)
        ev.inc(e.tensor_scalar(rows[0:1, VB:VB + T], rows[0:1, TMP:TMP + T], -1.0, None, ALU.mult), 1)
    p.block.vector(vec2)
    ev_rows = ev.n
    stats.close()

    bc = contextlib.ExitStack()
    ps_bc = [bc.enter_context(nc.psum_tensor(f"{tag}_bc{i}", [128, 512], F32)) for i in range(2)]
    mm_bc0 = mm.n

    def tens2(e):
        e.wait_ge(ev.h, ev_rows)
        for tc in range(NTC):
            if tc >= 1:
                e.wait_ge(ev.h, ev_rows + tc * KT)
            e.matmul(ps_bc[0][:, :], t["ones_row"][0:1, :], rows[0:1, RA + tc * 512:RA + tc * 512 + 512],
                     start=True, stop=True)
            mm.inc(e.matmul(ps_bc[1][:, :], t["ones_row"][0:1, :], rows[0:1, VB + tc * 512:VB + tc * 512 + 512],
                            start=True, stop=True), 1)
    p.block.tensor(tens2)

    def vec3(e):
        for tc in range(NTC):
            e.wait_ge(mm.h, mm_bc0 + tc + 1)
            for k in range(KT):
                sl = slice(k * T + tc * 512, k * T + tc * 512 + 512)
                e.tensor_tensor(xs[:, sl], xs[:, sl], ps_bc[0][:, :], ALU.mult)
                ev.inc(e.tensor_tensor(xs[:, sl], xs[:, sl], ps_bc[1][:, :], ALU.add), 1)
        ln.inc(e.tensor_copy(t["junk"][:, 0:1], t["junk"][:, 1:2]), 1)
    p.block.vector(vec3)
    p.stream_ready = ln.n
    bc.close()
    inner.close()


def _emit_moe(p):
    nc, s, t = p.nc, p.s, p.t
    ld, mm, ev, act, ln = s["ld"], s["mm"], s["ev"], s["act"], s["ln"]
    IC = 16
    TH = 1024
    NE = 9

    phase = contextlib.ExitStack()
    xs = phase.enter_context(nc.sbuf_tensor("moe_xs", [128, KT * T], F32))
    inner = contextlib.ExitStack()
    xb = inner.enter_context(nc.sbuf_tensor("moe_xb", [128, KT * TH], BF16))
    aT = inner.enter_context(nc.sbuf_tensor("moe_aT", [128, IC * TH], BF16))
    wd_sb = inner.enter_context(nc.sbuf_tensor("moe_wd", [128, IC * D], BF16))
    stag_gu = [inner.enter_context(nc.sbuf_tensor(f"moe_sgu{i}", [128, D], F32)) for i in range(2)]
    wg_sb = [inner.enter_context(nc.sbuf_tensor(f"moe_wg{i}", [128, D], BF16)) for i in range(2)]
    wu_sb = [inner.enter_context(nc.sbuf_tensor(f"moe_wu{i}", [128, D], BF16)) for i in range(2)]
    stag_d = inner.enter_context(nc.sbuf_tensor("moe_sd", [128, D], F32))
    cw_tm = inner.enter_context(nc.sbuf_tensor("moe_cwtm", [128, 2 * 64], F32))
    cw_row = inner.enter_context(nc.sbuf_tensor("moe_cwrow", [1, TH], BF16))
    cw_b = inner.enter_context(nc.sbuf_tensor("moe_cwb", [128, TH], BF16))
    g_sb = [inner.enter_context(nc.sbuf_tensor(f"moe_g{i}", [128, 512], BF16)) for i in range(2)]
    u_sb = [inner.enter_context(nc.sbuf_tensor(f"moe_u{i}", [128, 512], BF16)) for i in range(2)]
    wr_sb = inner.enter_context(nc.sbuf_tensor("moe_wr", [128, KT * 8], F32))
    rb_sb = inner.enter_context(nc.sbuf_tensor("moe_rb", [8, 1], F32))
    m1 = inner.enter_context(nc.sbuf_tensor("moe_m1", [128, 2], F32))
    msk = inner.enter_context(nc.sbuf_tensor("moe_msk", [128, 24], F32))
    probsT = inner.enter_context(nc.sbuf_tensor("moe_probs", [8, TH], F32))

    ps_g = [inner.enter_context(nc.psum_tensor(f"moe_psg{i}", [128, 512], F32)) for i in range(2)]
    ps_u = [inner.enter_context(nc.psum_tensor(f"moe_psu{i}", [128, 512], F32)) for i in range(2)]
    ps_d = [inner.enter_context(nc.psum_tensor(f"moe_psd{i}", [128, 512], F32)) for i in range(4)]

    stream_ready = p.stream_ready
    mm_fence = mm.n

    wgs = [t["wgsT_hbm"]] + [t[f"wg{e}T_hbm"] for e in range(8)]
    wus = [t["wusT_hbm"]] + [t[f"wu{e}T_hbm"] for e in range(8)]
    wds = [t["wdsT_hbm"]] + [t[f"wd{e}T_hbm"] for e in range(8)]

    sgu = s["sgu"]; sdn = s["sdn"]; ssl = s["ssl"]; sat = s["sat"]
    sde = s["sde"]; swu = s["swu"]; swd = s["swd"]; scw = s["scw"]
    b_gu, b_dn, b_sl, b_at, b_de, b_wu, b_wd, b_cw = (x.n for x in (sgu, sdn, ssl, sat, sde, swu, swd, scw))
    b_ld = ld.n
    b_ev = ev.n
    b_act = act.n

    NLD0 = 17

    def ld_wg_done(gi):
        J, ic = divmod(gi, IC)
        return b_ld + (NLD0 + J * 48 + 2 * ic + 1) * 16

    def ld_wu_done(gi):
        J, ic = divmod(gi, IC)
        return b_ld + (NLD0 + J * 48 + 2 * ic + 2) * 16

    def ld_wd_done(gi):
        J, ic = divmod(gi, IC)
        return b_ld + (NLD0 + J * 48 + 32 + (ic + 1)) * 16

    st = s["st"]
    stream_in_dram = p.stream_in_dram

    def sync(e):
        e.wait_ge(st.h, stream_in_dram)
        e.wait_ge(ln.h, stream_ready)
        e.wait_ge(mm.h, mm_fence)
        for k in range(KT):
            ld.inc(e.dma_start(out=xs[:, k * T:(k + 1) * T], in_=t["xs_dram"][k * 128:(k + 1) * 128, :]), 16)
        for k in range(KT):
            ld.inc(e.dma_start(out=wr_sb[:, k * 8:(k + 1) * 8], in_=t["wrT_hbm"][k * 128:(k + 1) * 128, :]), 16)
        ld.inc(e.dma_start(out=rb_sb[0:8, 0:1], in_=t["rb_hbm"].unsqueeze(1)), 16)
        for th in range(2):
            for j in range(NE):
                J = th * NE + j
                for ic in range(IC):
                    gi = J * IC + ic
                    if gi >= 1:
                        e.wait_ge(swu.h, b_wu + 2 * gi)
                    ld.inc(e.dma_start(out=stag_gu[0][:, :].rearrange("p (k c) -> p k c", k=KT),
                                       in_=wgs[j].rearrange("(k p) i -> p k i", p=128)[:, :, ic * 128:(ic + 1) * 128]), 16)
                    ld.inc(e.dma_start(out=stag_gu[1][:, :].rearrange("p (k c) -> p k c", k=KT),
                                       in_=wus[j].rearrange("(k p) i -> p k i", p=128)[:, :, ic * 128:(ic + 1) * 128]), 16)
                for ic in range(IC):
                    gi = J * IC + ic
                    if gi >= 1:
                        e.wait_ge(swd.h, b_wd + gi)
                    ld.inc(e.dma_start(out=stag_d[:, :], in_=wds[j][ic * 128:(ic + 1) * 128, :]), 16)
    p.block.sync(sync)

    ev_n = [b_ev]
    act_n = [b_act]
    mm_n = [mm_fence]

    for th in range(2):
        toff = th * TH

        ev_xb0 = ev_n[0]

        def vec_xb(e, th=th, toff=toff, ev_xb0=ev_xb0):
            if th == 0:
                e.wait_ge(ld.h, b_ld + 8 * 16)
            for k in range(KT):
                ev.inc(e.tensor_copy(xb[:, k * TH:(k + 1) * TH], xs[:, k * T + toff:k * T + toff + TH]), 1)
        p.block.vector(vec_xb)
        ev_n[0] = ev.n
        xb_ready = ev.n

        mm_r0 = mm_n[0]

        def tens_r(e, th=th, toff=toff, mm_r0=mm_r0, xb_ready=xb_ready):
            e.wait_ge(ev.h, xb_ready)
            e.wait_ge(ld.h, b_ld + NLD0 * 16)
            e.wait_ge(sde.h, b_de + 16 * (th * NE))
            for tc in range(2):
                ins = None
                for k in range(KT):
                    ins = e.matmul(ps_d[tc % 2][0:8, :], wr_sb[:, k * 8:(k + 1) * 8],
                                   xs[:, k * T + toff + tc * 512: k * T + toff + tc * 512 + 512],
                                   start=(k == 0), stop=(k == KT - 1))
                mm.inc(ins, 1)
        p.block.tensor(tens_r)
        mm_router = mm.n
        act_r0 = act_n[0]

        def act_r(e, th=th, mm_r0=mm_r0):
            for tc in range(2):
                e.wait_ge(mm.h, mm_r0 + tc + 1)
                act.inc(e.activation(out=probsT[0:8, tc * 512:(tc + 1) * 512], in_=ps_d[tc % 2][0:8, :],
                                     func=AF.Sigmoid, bias=rb_sb[0:8, 0:1], scale=1.0), 1)
        p.block.scalar(act_r)
        act_n[0] = act.n
        act_probs = act.n
        ev_r0 = ev.n

        def tens_r2(e, th=th, act_probs=act_probs, ev_r0=ev_r0):
            e.wait_ge(act.h, act_probs)
            for tt in range(8):
                if tt >= 2:
                    e.wait_ge(ev.h, ev_r0 + 7 * (tt - 1))
                mm.inc(e.transpose(ps_d[2 + tt % 2][0:128, 0:8], probsT[0:8, tt * 128:(tt + 1) * 128],
                                   t["ident"][0:8, 0:8]), 1)
        p.block.tensor(tens_r2)
        mm_tr0 = mm_router

        def vec_r(e, th=th, mm_tr0=mm_tr0):
            tp = s["tp"]

            def st_(ins):
                tp.inc(ins, 1)
                e.wait_ge(tp.h, tp.n)
                return ins
            for tt in range(8):
                e.wait_ge(mm.h, mm_tr0 + tt + 1)
                pb = ps_d[2 + tt % 2]
                c0 = th * 64 + tt * 8
                st_(e.tensor_copy(cw_tm[:, c0:c0 + 8], pb[0:128, 0:8]))
                st_(e.reduce_max(m1[:, 0:1], cw_tm[:, c0:c0 + 8], axis=mybir.AxisListType.X))
                st_(e.tensor_scalar(msk[:, 0:8], cw_tm[:, c0:c0 + 8], m1[:, 0:1], None, ALU.is_ge))
                st_(e.tensor_scalar(msk[:, 8:16], msk[:, 0:8], -1e30, None, ALU.mult))
                st_(e.tensor_tensor(msk[:, 16:24], cw_tm[:, c0:c0 + 8], msk[:, 8:16], ALU.add))
                st_(e.reduce_max(m1[:, 1:2], msk[:, 16:24], axis=mybir.AxisListType.X))
                st_(e.tensor_scalar(msk[:, 0:8], cw_tm[:, c0:c0 + 8], m1[:, 1:2], None, ALU.is_ge))
                if os.environ.get("MK_DBG") == "cwprobs":
                    ev.inc(e.tensor_copy(cw_tm[:, c0:c0 + 8], cw_tm[:, c0:c0 + 8]) if False else e.tensor_scalar(cw_tm[:, c0:c0 + 8], cw_tm[:, c0:c0 + 8], 1.0, None, ALU.mult), 7)
                else:
                    ev.inc(e.tensor_tensor(cw_tm[:, c0:c0 + 8], cw_tm[:, c0:c0 + 8], msk[:, 0:8], ALU.mult), 7)
        p.block.vector(vec_r)
        ev_n[0] = ev.n
        ev_cwtm = ev.n
        mm_n[0] = mm.n

        for j in range(NE):
            J = th * NE + j
            routed = j >= 1
            R = th * 8 + (j - 1)
            c0cw = b_cw + R * 10

            if routed:
                mm_cw0 = mm.n

                def tens_cw(e, th=th, j=j, J=J, R=R, c0cw=c0cw, ev_cwtm=ev_cwtm):
                    e.wait_ge(ev.h, ev_cwtm)
                    e.wait_ge(sde.h, b_de + 16 * J)
                    for tt in range(8):
                        if tt >= 4:
                            e.wait_ge(scw.h, c0cw + (tt - 4) + 1)
                        mm.inc(e.transpose(ps_d[tt % 4][0:1, 0:128],
                                           cw_tm[:, th * 64 + tt * 8 + (j - 1):th * 64 + tt * 8 + j],
                                           t["ident"][:, :]), 1)
                    for tc in range(2):
                        e.wait_ge(scw.h, c0cw + max(4 * (tc + 1), 5 + tc))
                        mm.inc(e.matmul(ps_d[tc % 4][:, :], t["ones_row_b"][0:1, :],
                                        cw_row[0:1, tc * 512:tc * 512 + 512], start=True, stop=True), 1)
                p.block.tensor(tens_cw)

                def vec_cw(e, mm_cw0=mm_cw0):
                    for tt in range(8):
                        e.wait_ge(mm.h, mm_cw0 + tt + 1)
                        scw.inc(e.tensor_copy(cw_row[0:1, tt * 128:(tt + 1) * 128], ps_d[tt % 4][0:1, 0:128]), 1)
                    for tc in range(2):
                        e.wait_ge(mm.h, mm_cw0 + 8 + tc + 1)
                        scw.inc(e.tensor_copy(cw_b[:, tc * 512:(tc + 1) * 512], ps_d[tc % 4][:, :]), 1)
                p.block.vector(vec_cw)
                mm_n[0] = mm.n

            def tens_gu(e, J=J, xb_ready=xb_ready):
                e.wait_ge(ev.h, xb_ready)
                for ic in range(IC):
                    gi = J * IC + ic
                    db = gi % 2
                    e.wait_ge(swu.h, b_wu + 2 * (gi + 1))
                    for tc in range(2):
                        qi = gi * 2 + tc
                        if qi >= 2:
                            e.wait_ge(ssl.h, b_sl + qi - 1)
                            e.wait_ge(sat.h, b_at + qi - 1)
                        for k in range(KT):
                            e.matmul(ps_g[qi % 2][:, :], wg_sb[db][:, k * 128:(k + 1) * 128],
                                     xb[:, k * TH + tc * 512:k * TH + tc * 512 + 512],
                                     start=(k == 0), stop=(k == KT - 1))
                        ins = None
                        for k in range(KT):
                            ins = e.matmul(ps_u[qi % 2][:, :], wu_sb[db][:, k * 128:(k + 1) * 128],
                                           xb[:, k * TH + tc * 512:k * TH + tc * 512 + 512],
                                           start=(k == 0), stop=(k == KT - 1))
                        sgu.inc(ins, 1)
            p.block.tensor(tens_gu)


            def gp_wgu(e, J=J):
                for ic in range(IC):
                    gi = J * IC + ic
                    db = gi % 2
                    e.wait_ge(ld.h, ld_wg_done(gi))
                    if gi >= 2:
                        e.wait_ge(sgu.h, b_gu + 2 * (gi - 1))
                    swu.inc(e.tensor_copy(wg_sb[db][:, :], stag_gu[0][:, :]), 1)
                    e.wait_ge(ld.h, ld_wu_done(gi))
                    swu.inc(e.tensor_copy(wu_sb[db][:, :], stag_gu[1][:, :]), 1)
            p.block.gpsimd(gp_wgu)

            def act_sl(e, J=J):
                for ic in range(IC):
                    for tc in range(2):
                        qi = (J * IC + ic) * 2 + tc
                        e.wait_ge(sgu.h, b_gu + qi + 1)
                        if qi >= 2:
                            e.wait_ge(sat.h, b_at + qi - 1)
                        ssl.inc(e.activation(out=g_sb[qi % 2][:, :], in_=ps_g[qi % 2][:, :], func=AF.Silu), 1)
            p.block.scalar(act_sl)

            def vec_au(e, J=J, routed=routed, c0cw=c0cw):
                for ic in range(IC):
                    for tc in range(2):
                        qi = (J * IC + ic) * 2 + tc
                        e.wait_ge(sgu.h, b_gu + qi + 1)
                        if routed:
                            e.wait_ge(scw.h, c0cw + 8 + tc + 1)
                            e.tensor_tensor(u_sb[qi % 2][:, :], ps_u[qi % 2][:, :],
                                            cw_b[:, tc * 512:(tc + 1) * 512], ALU.mult)
                        else:
                            e.tensor_copy(u_sb[qi % 2][:, :], ps_u[qi % 2][:, :])
                        e.wait_ge(ssl.h, b_sl + qi + 1)
                        e.wait_ge(sdn.h, b_dn + 16 * J)
                        sat.inc(e.tensor_tensor(aT[:, ic * TH + tc * 512:ic * TH + tc * 512 + 512],
                                                g_sb[qi % 2][:, :], u_sb[qi % 2][:, :], ALU.mult), 1)
            p.block.vector(vec_au)

            def act_wd(e, J=J):
                for ic in range(IC):
                    gi = J * IC + ic
                    e.wait_ge(ld.h, ld_wd_done(gi))
                    e.wait_ge(sdn.h, b_dn + 16 * J)
                    swd.inc(e.copy(wd_sb[:, ic * D:(ic + 1) * D], stag_d[:, :]), 1)
            p.block.scalar(act_wd)

            def tens_dn(e, J=J, routed=routed, c0cw=c0cw):
                e.wait_ge(sat.h, b_at + 32 * (J + 1))
                e.wait_ge(swd.h, b_wd + 16 * (J + 1))
                if routed:
                    e.wait_ge(scw.h, c0cw + 10)
                for tc in range(2):
                    for ht in range(KT):
                        di = J * 16 + tc * KT + ht
                        if di >= 4:
                            e.wait_ge(sde.h, b_de + di - 3)
                        ins = None
                        for ic in range(IC):
                            ins = e.matmul(ps_d[di % 4][:, :], wd_sb[:, ic * D + ht * 128:ic * D + ht * 128 + 128],
                                           aT[:, ic * TH + tc * 512:ic * TH + tc * 512 + 512],
                                           start=(ic == 0), stop=(ic == IC - 1))
                        sdn.inc(ins, 1)
            p.block.tensor(tens_dn)
            mm_n[0] = mm.n

            def vec_dn(e, J=J, toff=toff):
                for tc in range(2):
                    for ht in range(KT):
                        di = J * 16 + tc * KT + ht
                        e.wait_ge(sdn.h, b_dn + di + 1)
                        sl = slice(ht * T + toff + tc * 512, ht * T + toff + tc * 512 + 512)
                        sde.inc(e.tensor_tensor(xs[:, sl], xs[:, sl], ps_d[di % 4][:, :], ALU.add), 1)
            p.block.vector(vec_dn)
            ev_n[0] = ev.n

    if os.environ.get("MK_DBG") in ("cw", "cwprobs"):
        def vcw(e):
            e.wait_ge(sde.h, b_de + 18 * 16)
            e.tensor_copy(xs[:, 0:128], cw_tm[:, 0:128])
            e.tensor_copy(xs[:, 128:130], m1[:, 0:2])
            e.tensor_copy(xs[:, 130:154], msk[:, 0:24])
            ln.inc(e.tensor_copy(t["junk"][:, 0:1], t["junk"][:, 1:2]), 1)
        p.block.vector(vcw)
        p.stream_ready = ln.n
        inner.close()
        p._moe_ret = (xs, phase)
        return

    def vec_bridge(e):
        for k in range(KT):
            e.wait_ge(sde.h, b_de + 8 * 16 + 8 + k + 1)
            e.wait_ge(sde.h, b_de + 17 * 16 + 8 + k + 1)
            ev.inc(e.tensor_copy(t["junk"][:, 0:1], t["junk"][:, 1:2]), 1)
    p.block.vector(vec_bridge)
    y_ready = ev.n

    if os.environ.get("MK_DBG") == "moeraw":
        def vraw(e):
            for k in range(KT):
                e.wait_ge(ev.h, y_ready - KT + k + 1)
            ln.inc(e.tensor_copy(t["junk"][:, 0:1], t["junk"][:, 1:2]), 1)
        p.block.vector(vraw)
        p.stream_ready = ln.n
        inner.close()
        p._moe_ret = (xs, phase)
        return
    inner.close()

    _emit_ln(p, xs, y_ready, tag="ln2")
    p._moe_ret = (xs, phase)


def _emit_attn(p):
    nc, s, t = p.nc, p.s, p.t
    ld, mm, ev, act, ln, st = s["ld"], s["mm"], s["ev"], s["act"], s["ln"], s["st"]
    pj = s["pj"]; pg = s["pg"]; pgd = s["pgd"]
    pp = s["pp"]; pa = s["pa"]; po = s["po"]; pi = s["pi"]; pc = s["pc"]
    pS = s["pS"]; pmx = s["pmx"]; pex = s["pex"]; psc = s["psc"]
    ptr = s["ptr"]; pte = s["pte"]; pav = s["pav"]; poe = s["poe"]; tp = s["tp"]

    phase = contextlib.ExitStack()
    OT = phase.enter_context(nc.sbuf_tensor("at_OT", [128, KT * T], BF16))
    qkv = contextlib.ExitStack()
    QT = qkv.enter_context(nc.sbuf_tensor("at_QT", [128, KT * T], BF16))
    KTs = qkv.enter_context(nc.sbuf_tensor("at_KT", [128, KT * T], BF16))
    V = qkv.enter_context(nc.sbuf_tensor("at_V", [128, 16 * D], BF16))
    rext = qkv.enter_context(nc.sbuf_tensor("at_rext", [128, 4096], BF16))
    bvT = qkv.enter_context(nc.sbuf_tensor("at_bvT", [128, KT], F32))

    stream_in_dram = p.stream_in_dram
    mm_fence = mm.n
    b_ld = ld.n
    b_pg = pg.n
    b_pgd = pgd.n
    b_pj = pj.n
    b_mm = mm.n

    proj = contextlib.ExitStack()
    xb = proj.enter_context(nc.sbuf_tensor("at_xb", [128, KT * T], BF16))
    pstg = [proj.enter_context(nc.sbuf_tensor(f"at_stg{i}", [128, D], F32)) for i in range(2)]
    w_sb = proj.enter_context(nc.sbuf_tensor("at_wsb", [128, KT * D], BF16))
    bqkT = proj.enter_context(nc.sbuf_tensor("at_bqk", [128, 2 * KT], F32))
    ps_pr = [proj.enter_context(nc.psum_tensor(f"at_pspr{i}", [128, 512], F32)) for i in range(4)]

    def gp_xb(e):
        e.wait_ge(st.h, stream_in_dram)
        e.wait_ge(mm.h, mm_fence)
        e.wait_ge(ln.h, p.stream_ready)
        for k in range(KT):
            pgd.inc(e.dma_start(out=xb[:, k * T:(k + 1) * T], in_=t["xs_dram"][k * 128:(k + 1) * 128, :]), 16)
    p.block.gpsimd(gp_xb)

    def sync_w(e):
        e.wait_ge(st.h, stream_in_dram)
        e.wait_ge(mm.h, mm_fence)
        for m in range(KT):
            ld.inc(e.dma_start(out=bqkT[:, m:m + 1], in_=t["bq_hbm"][m * 128:(m + 1) * 128].unsqueeze(1)), 16)
            ld.inc(e.dma_start(out=bqkT[:, KT + m:KT + m + 1], in_=t["bk_hbm"][m * 128:(m + 1) * 128].unsqueeze(1)), 16)
            ld.inc(e.dma_start(out=bvT[:, m:m + 1], in_=t["bv_hbm"][m * 128:(m + 1) * 128].unsqueeze(1)), 16)
        for wi, wh in enumerate(["wqT_hbm", "wkT_hbm", "wvT_hbm"]):
            for kt in range(KT):
                si = wi * KT + kt
                if si >= 2:
                    e.wait_ge(pg.h, b_pg + si - 1)
                ld.inc(e.dma_start(out=pstg[si % 2][:, :], in_=t[wh][kt * 128:(kt + 1) * 128, :]), 16)
        for hf in range(4):
            si = 24 + hf
            e.wait_ge(pg.h, b_pg + si - 1)
            ld.inc(e.dma_start(out=pstg[si % 2][0:64, :],
                               in_=t["rextT_hbm"][:, hf * 1024:(hf + 1) * 1024]), 16)
            ld.inc(e.dma_start(out=pstg[si % 2][64:128, :],
                               in_=t["rextT_hbm"][:, hf * 1024:(hf + 1) * 1024]), 16)
    p.block.sync(sync_w)

    def ld_w(si):
        if si < 24:
            return b_ld + (24 + si + 1) * 16
        return b_ld + (48 + 2 * (si - 24 + 1)) * 16

    def gp_cast(e):
        for wi in range(3):
            for kt in range(KT):
                si = wi * KT + kt
                e.wait_ge(ld.h, ld_w(si))
                if wi > 0:
                    e.wait_ge(mm.h, b_mm + 32 * wi)
                pg.inc(e.tensor_copy(w_sb[:, kt * D:(kt + 1) * D], pstg[si % 2][:, :]), 1)
        for hf in range(4):
            si = 24 + hf
            e.wait_ge(ld.h, ld_w(si))
            pg.inc(e.tensor_copy(rext[:, hf * 1024:(hf + 1) * 1024], pstg[si % 2][:, :]), 1)
    p.block.gpsimd(gp_cast)

    def tens_proj(e):
        e.wait_ge(pgd.h, b_pgd + 16 * KT)
        for wi in range(3):
            for gidx in range(32):
                g = wi * 32 + gidx
                e.wait_ge(pg.h, b_pg + wi * KT + KT)
                if g >= 4:
                    e.wait_ge(pj.h, b_pj + g - 3)
                pb = ps_pr[g % 4]
                ins = None
                if wi < 2:
                    mt, tc = divmod(gidx, NTC)
                    for k in range(KT):
                        ins = e.matmul(pb[:, :], w_sb[:, k * D + mt * 128:k * D + mt * 128 + 128],
                                       xb[:, k * T + tc * 512:k * T + tc * 512 + 512],
                                       start=(k == 0), stop=(k == KT - 1))
                else:
                    jt, fh = divmod(gidx, 2)
                    for k in range(KT):
                        ins = e.matmul(pb[:, :], xb[:, k * T + jt * 128:k * T + jt * 128 + 128],
                                       w_sb[:, k * D + fh * 512:k * D + fh * 512 + 512],
                                       start=(k == 0), stop=(k == KT - 1))
                mm.inc(ins, 1)
    p.block.tensor(tens_proj)

    def vec_proj(e):
        for wi in range(3):
            for gidx in range(32):
                g = wi * 32 + gidx
                e.wait_ge(mm.h, b_mm + g + 1)
                pb = ps_pr[g % 4]
                if wi < 2:
                    mt, tc = divmod(gidx, NTC)
                    dst = QT if wi == 0 else KTs
                    pj.inc(e.tensor_scalar(dst[:, mt * T + tc * 512:mt * T + tc * 512 + 512],
                                           pb[:, :], bqkT[:, wi * KT + mt:wi * KT + mt + 1], None, ALU.add), 1)
                else:
                    jt, fh = divmod(gidx, 2)
                    pj.inc(e.tensor_copy(V[:, jt * D + fh * 512:jt * D + fh * 512 + 512], pb[:, :]), 1)
    p.block.vector(vec_proj)
    proj_done = pj.n
    rext_done = b_pg + 28
    proj.close()

    hl = contextlib.ExitStack()
    pstage = [hl.enter_context(nc.sbuf_tensor(f"at_pst{i}", [128, 2176], BF16)) for i in range(2)]
    sk_sb = [hl.enter_context(nc.sbuf_tensor(f"at_sk{i}", [128, T], BF16)) for i in range(2)]
    S_row = [hl.enter_context(nc.sbuf_tensor(f"at_S{i}", [128, T], F32)) for i in range(2)]
    expS = [hl.enter_context(nc.sbuf_tensor(f"at_eS{i}", [128, T], BF16)) for i in range(2)]
    expST = [hl.enter_context(nc.sbuf_tensor(f"at_eST{i}", [128, T], BF16)) for i in range(2)]
    mxb = hl.enter_context(nc.sbuf_tensor("at_mx", [128, 8], F32))
    ps_a = [hl.enter_context(nc.psum_tensor(f"at_psa{i}", [128, 512], F32)) for i in range(4)]
    ps_t = [hl.enter_context(nc.psum_tensor(f"at_ptt{i}", [128, 512], BF16)) for i in range(2)]
    ps_o = [hl.enter_context(nc.psum_tensor(f"at_pso{i}", [64, 128], F32)) for i in range(2)]

    NSL = 256
    b_pp, b_pa, b_po, b_pi, b_pc, b_pS, b_pmx, b_pex, b_psc, b_ptr, b_pte, b_pav, b_poe = (
        x.n for x in (pp, pa, po, pi, pc, pS, pmx, pex, psc, ptr, pte, pav, poe))
    CW = [512, 512, 512, 512, 128]
    COF = [0, 512, 1024, 1536, 2048]

    def qslice(h, ic):
        return QT[64 * (h % 2):64 * (h % 2) + 64, (h // 2) * T + ic * 128:(h // 2) * T + ic * 128 + 128]

    def tens_hl(e):
        e.wait_ge(pj.h, proj_done)
        e.wait_ge(pg.h, rext_done)
        e.wait_ge(s["init"].h, p.init_done)
        for oi in range(NSL + 4):
            if oi < NSL:
                h, ic = divmod(oi, 16)
                k0s = 1920 - ic * 128
                for c in range(5):
                    cn = 5 * oi + c
                    if cn >= 2:
                        e.wait_ge(pa.h, b_pa + cn - 1)
                    pp.inc(e.matmul(ps_a[cn % 2][:, 0:CW[c]], qslice(h, ic),
                                    rext[64 * (h % 2):64 * (h % 2) + 64, k0s + COF[c]:k0s + COF[c] + CW[c]],
                                    start=True, stop=True), 1)
            if 2 <= oi < NSL + 2:
                g1 = oi - 2
                h, ic = divmod(g1, 16)
                for jc in range(NTC):
                    cn = 4 * g1 + jc
                    if cn >= 2:
                        e.wait_ge(pS.h, b_pS + cn - 1)
                    pc.inc(e.matmul(ps_a[2 + cn % 2][:, :], qslice(h, ic),
                                    KTs[64 * (h % 2):64 * (h % 2) + 64,
                                        (h // 2) * T + jc * 512:(h // 2) * T + jc * 512 + 512],
                                    start=True, stop=True), 1)
            if 4 <= oi < NSL + 4:
                g3 = oi - 4
                h, ic = divmod(g3, 16)
                e.wait_ge(psc.h, b_psc + g3 + 1)
                for gq in range(4):
                    gqi = 4 * g3 + gq
                    if gqi >= 2:
                        e.wait_ge(pte.h, b_pte + gqi - 1)
                    ins = None
                    for j4 in range(4):
                        jt = gq * 4 + j4
                        ins = e.matmul(ps_t[gqi % 2][:, j4 * 128:(j4 + 1) * 128],
                                       expS[g3 % 2][:, jt * 128:(jt + 1) * 128],
                                       t["identb"][:, :], is_transpose=True,
                                       start=(j4 == 0), stop=(j4 == 3))
                    ptr.inc(ins, 1)
                if g3 >= 2:
                    e.wait_ge(poe.h, b_poe + g3 - 1)
                ins = None
                for jt in range(16):
                    if jt % 4 == 0:
                        e.wait_ge(pte.h, b_pte + 4 * g3 + jt // 4 + 1)
                    ins = e.matmul(ps_o[g3 % 2][0:64, :], V[:, jt * D + h * 64:jt * D + h * 64 + 64],
                                   expST[g3 % 2][:, jt * 128:(jt + 1) * 128],
                                   start=(jt == 0), stop=(jt == 15))
                pav.inc(ins, 1)
    p.block.tensor(tens_hl)

    def gp_hl(e):
        for oi in range(NSL + 1):
            if oi < NSL:
                h, ic = divmod(oi, 16)
                e.wait_ge(pa.h, b_pa + 5 * (oi + 1))
                po.inc(e.dma_start(out=t["pscr"][h % 2, ic * 128 * 2176:(ic + 1) * 128 * 2176]
                                   .rearrange("(a c) -> a c", a=128),
                                   in_=pstage[oi % 2][:, :]), 16)
            if oi >= 1:
                g1 = oi - 1
                h, ic = divmod(g1, 16)
                e.wait_ge(po.h, b_po + 16 * (g1 + 1))
                if g1 >= 2:
                    e.wait_ge(pS.h, b_pS + 4 * (g1 - 1))
                for jc in range(NTC):
                    sap = bass.AP(tensor=t["pscr_h"],
                                  offset=(h % 2) * 16 * 128 * 2176 + ic * 128 * 2176 + jc * 512 + 127,
                                  ap=[[2175, 128], [1, 512]])
                    pi.inc(e.dma_start(out=sk_sb[g1 % 2][:, jc * 512:(jc + 1) * 512], in_=sap), 16)
    p.block.gpsimd(gp_hl)

    def act_hl(e):
        for oi in range(NSL + 3):
            if oi < NSL:
                for c in range(5):
                    cn = 5 * oi + c
                    e.wait_ge(pp.h, b_pp + cn + 1)
                    if oi >= 2 and c == 0:
                        e.wait_ge(po.h, b_po + 16 * (oi - 1))
                    pa.inc(e.copy(pstage[oi % 2][:, COF[c]:COF[c] + CW[c]], ps_a[cn % 2][:, 0:CW[c]]), 1)
            if oi >= 3:
                g2 = oi - 3
                e.wait_ge(pmx.h, b_pmx + g2 + 1)
                if g2 >= 2:
                    e.wait_ge(ptr.h, b_ptr + 4 * (g2 - 1))
                pex.inc(e.activation(out=expS[g2 % 2][:, :], in_=S_row[g2 % 2][:, :], func=AF.Exp,
                                     scale=0.125, bias=mxb[:, (g2 % 2) * 4 + 1:(g2 % 2) * 4 + 2],
                                     accum_out=mxb[:, (g2 % 2) * 4 + 2:(g2 % 2) * 4 + 3]), 1)
    p.block.scalar(act_hl)

    def vec_hl(e):
        def st_(ins):
            tp.inc(ins, 1)
            e.wait_ge(tp.h, tp.n)
            return ins
        for oi in range(NSL + 4):
            if 2 <= oi < NSL + 2:
                g1 = oi - 2
                par = g1 % 2
                if g1 >= 2:
                    e.wait_ge(pex.h, b_pex + g1 - 1)
                for jc in range(NTC):
                    cn = 4 * g1 + jc
                    e.wait_ge(pc.h, b_pc + cn + 1)
                    e.wait_ge(pi.h, b_pi + 64 * g1 + 16 * (jc + 1))
                    pS.inc(e.tensor_tensor(S_row[par][:, jc * 512:(jc + 1) * 512],
                                           ps_a[2 + cn % 2][:, :],
                                           sk_sb[par][:, jc * 512:(jc + 1) * 512], ALU.add), 1)
                st_(e.reduce_max(mxb[:, par * 4:par * 4 + 1], S_row[par][:, :], axis=mybir.AxisListType.X))
                pmx.inc(e.tensor_scalar(mxb[:, par * 4 + 1:par * 4 + 2],
                                        mxb[:, par * 4:par * 4 + 1], -0.125, None, ALU.mult), 1)
            if 3 <= oi < NSL + 3:
                g2 = oi - 3
                par = g2 % 2
                e.wait_ge(pex.h, b_pex + g2 + 1)
                st_(e.reciprocal(mxb[:, par * 4 + 3:par * 4 + 4], mxb[:, par * 4 + 2:par * 4 + 3]))
                psc.inc(e.tensor_scalar(expS[par][:, :], expS[par][:, :],
                                        mxb[:, par * 4 + 3:par * 4 + 4], None, ALU.mult), 1)
            if oi >= 4:
                g3 = oi - 4
                h, ic = divmod(g3, 16)
                par = g3 % 2
                if g3 >= 2:
                    e.wait_ge(pav.h, b_pav + g3 - 1)
                for gq in range(4):
                    gqi = 4 * g3 + gq
                    e.wait_ge(ptr.h, b_ptr + gqi + 1)
                    pte.inc(e.tensor_copy(expST[par][:, gq * 512:(gq + 1) * 512], ps_t[gqi % 2][:, :]), 1)
                e.wait_ge(pav.h, b_pav + g3 + 1)
                poe.inc(e.tensor_scalar(OT[64 * (h % 2):64 * (h % 2) + 64,
                                           (h // 2) * T + ic * 128:(h // 2) * T + ic * 128 + 128],
                                        ps_o[par][0:64, :],
                                        bvT[64 * (h % 2):64 * (h % 2) + 64, h // 2:h // 2 + 1],
                                        None, ALU.add), 1)
    p.block.vector(vec_hl)
    heads_done_poe = poe.n
    heads_done_pav = pav.n
    hl.close()
    qkv.close()

    wo = contextlib.ExitStack()
    xs = phase.enter_context(nc.sbuf_tensor("at_xs", [128, KT * T], F32))
    wstg = [wo.enter_context(nc.sbuf_tensor(f"at_wstg{i}", [128, D], F32)) for i in range(2)]
    wo_sb = wo.enter_context(nc.sbuf_tensor("at_wo", [128, KT * D], BF16))
    boT = wo.enter_context(nc.sbuf_tensor("at_bo", [128, KT], F32))
    t1b = wo.enter_context(nc.sbuf_tensor("at_t1", [128, 512], F32))
    ps_w = [wo.enter_context(nc.psum_tensor(f"at_psw{i}", [128, 512], F32)) for i in range(4)]

    b_ld2 = ld.n
    b_pg2 = pg.n
    b_mm2 = mm.n
    b_ev2 = ev.n

    def sync_wo(e):
        e.wait_ge(poe.h, heads_done_poe)
        e.wait_ge(pav.h, heads_done_pav)
        for k in range(KT):
            ld.inc(e.dma_start(out=xs[:, k * T:(k + 1) * T], in_=t["xs_dram"][k * 128:(k + 1) * 128, :]), 16)
        for m in range(KT):
            ld.inc(e.dma_start(out=boT[:, m:m + 1], in_=t["bo_hbm"][m * 128:(m + 1) * 128].unsqueeze(1)), 16)
        for kt in range(KT):
            if kt >= 2:
                e.wait_ge(pg.h, b_pg2 + kt - 1)
            ld.inc(e.dma_start(out=wstg[kt % 2][:, :], in_=t["woT_hbm"][kt * 128:(kt + 1) * 128, :]), 16)
    p.block.sync(sync_wo)

    def gp_wo(e):
        for kt in range(KT):
            e.wait_ge(ld.h, b_ld2 + 16 * (16 + kt + 1))
            pg.inc(e.tensor_copy(wo_sb[:, kt * D:(kt + 1) * D], wstg[kt % 2][:, :]), 1)
    p.block.gpsimd(gp_wo)

    def tens_wo(e):
        e.wait_ge(pg.h, b_pg2 + KT)
        for g in range(32):
            tc, mt = divmod(g, KT)
            if g >= 4:
                e.wait_ge(ev.h, b_ev2 + g - 3)
            ins = None
            for k in range(KT):
                ins = e.matmul(ps_w[g % 4][:, :], wo_sb[:, k * D + mt * 128:k * D + mt * 128 + 128],
                               OT[:, k * T + tc * 512:k * T + tc * 512 + 512],
                               start=(k == 0), stop=(k == KT - 1))
            mm.inc(ins, 1)
    p.block.tensor(tens_wo)

    def vec_wo(e):
        e.wait_ge(ld.h, b_ld2 + 16 * 16)
        for g in range(32):
            tc, mt = divmod(g, KT)
            e.wait_ge(mm.h, b_mm2 + g + 1)
            sl = slice(mt * T + tc * 512, mt * T + tc * 512 + 512)
            e.tensor_scalar(t1b[:, :], ps_w[g % 4][:, :], boT[:, mt:mt + 1], None, ALU.add)
            ev.inc(e.tensor_tensor(xs[:, sl], xs[:, sl], t1b[:, :], ALU.add), 1)
    p.block.vector(vec_wo)
    y_ready = ev.n
    wo.close()

    _emit_ln(p, xs, y_ready, tag="ln3")
    p._attn_ret = (xs, phase)


def _emit_ffn(p):
    nc, s, t = p.nc, p.s, p.t
    ld, mm, ev, act, ln, st = s["ld"], s["mm"], s["ev"], s["act"], s["ln"], s["st"]
    sgu = s["sgu"]; sdn = s["sdn"]; sat = s["sat"]; hh = s["hh"]
    fev = s["ssl"]; swd = s["swd"]; pmx = s["pmx"]; pgd = s["pgd"]; pj = s["pj"]
    po = s["po"]
    TH = 512
    NF1 = 32

    phase = contextlib.ExitStack()
    xs = phase.enter_context(nc.sbuf_tensor("ffn_xs", [128, KT * T], F32))
    inner = contextlib.ExitStack()
    xb = inner.enter_context(nc.sbuf_tensor("ffn_xb", [128, KT * TH], BF16))
    h1T = inner.enter_context(nc.sbuf_tensor("ffn_h1", [128, NF1 * TH], BF16))
    h2T = inner.enter_context(nc.sbuf_tensor("ffn_h2", [128, NF1 * TH], BF16))
    k0c = [inner.enter_context(nc.sbuf_tensor(f"ffn_k0c{i}", [128, KT * 128], BF16)) for i in range(2)]
    kc = [inner.enter_context(nc.sbuf_tensor(f"ffn_kc{i}", [128, NF1 * 128], BF16)) for i in range(2)]
    s0a = [inner.enter_context(nc.sbuf_tensor(f"ffn_s0a{i}", [128, KT * 128], U8)) for i in range(2)]
    s0b = [inner.enter_context(nc.sbuf_tensor(f"ffn_s0b{i}", [128, KT * 128], U8)) for i in range(2)]
    s1g = [inner.enter_context(nc.sbuf_tensor(f"ffn_s1{i}", [128, NF1 * 128], U8)) for i in range(2)]
    rs = inner.enter_context(nc.sbuf_tensor("ffn_rs", [1, 2 * TH], F32))
    svec = inner.enter_context(nc.sbuf_tensor("ffn_svec", [128, 4], F32))
    t1b = inner.enter_context(nc.sbuf_tensor("ffn_t1", [128, 512], F32))
    ps1 = [inner.enter_context(nc.psum_tensor(f"ffn_ps{i}", [128, 512], F32)) for i in range(4)]
    ps_r = [inner.enter_context(nc.psum_tensor(f"ffn_psr{i}", [1, 512], F32)) for i in range(2)]
    ps_s = inner.enter_context(nc.psum_tensor("ffn_pss", [128, 4], F32))

    stream_in_dram = p.stream_in_dram
    mm_fence = mm.n
    b_ld = ld.n
    b_pgd = pgd.n
    b_po = po.n
    b_gu, b_dn, b_at, b_hh, b_fev, b_wd, b_pmx, b_mm, b_pj = (
        x.n for x in (sgu, sdn, sat, hh, fev, swd, pmx, mm, pj))

    def tens_s(e):
        e.wait_ge(st.h, stream_in_dram)
        e.wait_ge(mm.h, mm_fence)
        e.wait_ge(s["init"].h, p.init_done)
        mm.inc(e.matmul(ps_s[:, 0:3], t["ones_row"][0:1, :], t["s012_sb"][0:1, 0:3],
                        start=True, stop=True), 1)
    p.block.tensor(tens_s)

    def vec_s(e):
        tp = s["tp"]
        e.wait_ge(mm.h, b_mm + 1)
        tp.inc(e.tensor_scalar(svec[:, 0:1], ps_s[:, 0:1], 4.0, None, ALU.mult), 1)
        tp.inc(e.tensor_scalar(svec[:, 1:2], ps_s[:, 1:2], 4.0, None, ALU.mult), 1)
        tp.inc(e.tensor_scalar(svec[:, 2:3], ps_s[:, 2:3], 2.0, None, ALU.mult), 1)
        e.wait_ge(tp.h, tp.n)
        pj.inc(e.tensor_copy(svec[:, 3:4], svec[:, 0:1]), 1)
    p.block.vector(vec_s)
    svec_done = pj.n

    QO = [8, 8 + 104, 8 + 104 + 72, 8 + 104 + 144]

    def lda(q, idx):
        return b_ld + (QO[q] + idx) * 16 if q > 0 else b_ld + (8 + idx) * 16

    def swd_k0(i):
        return b_wd + i + 1

    def swd_k1(i):
        return b_wd + 32 + i + 1

    def swd_k2(i):
        return b_wd + 64 + i + 1

    C0 = 0
    C1 = 32 * 128 * KT * 128
    C2 = C1 + 32 * 128 * NF1 * 128

    def sync(e):
        e.wait_ge(st.h, stream_in_dram)
        e.wait_ge(mm.h, mm_fence)
        for k in range(KT):
            ld.inc(e.dma_start(out=xs[:, k * T:(k + 1) * T], in_=t["xs_dram"][k * 128:(k + 1) * 128, :]), 16)
        for i in range(NF1):
            if i >= 2:
                e.wait_ge(swd.h, swd_k0(i - 2))
            ld.inc(e.dma_start(out=s0a[i % 2][:, :].rearrange("p (k c) -> p k c", k=KT),
                               in_=t["k0a_hbm"].rearrange("(k p) f -> p k f", p=128)[:, :, i * 128:(i + 1) * 128]), 16)
            ld.inc(e.dma_start(out=s0b[i % 2][:, :].rearrange("p (k c) -> p k c", k=KT),
                               in_=t["k0b_hbm"].rearrange("(k p) f -> p k f", p=128)[:, :, i * 128:(i + 1) * 128]), 16)
        for i in range(NF1):
            if i >= 2:
                e.wait_ge(swd.h, swd_k1(i - 2))
            ld.inc(e.dma_start(out=s1g[i % 2][:, :].rearrange("p (k c) -> p k c", k=NF1),
                               in_=t["k1_hbm"].rearrange("(k p) f -> p k f", p=128)[:, :, i * 128:(i + 1) * 128]), 16)
        for i in range(KT):
            if i >= 2:
                e.wait_ge(swd.h, swd_k2(i - 2))
            else:
                e.wait_ge(swd.h, swd_k1(30 + i))
            ld.inc(e.dma_start(out=s1g[i % 2][:, :].rearrange("p (k c) -> p k c", k=NF1),
                               in_=t["k2_hbm"].rearrange("(k p) f -> p k f", p=128)[:, :, i * 128:(i + 1) * 128]), 16)
        for q in range(1, 4):
            for i in range(NF1):
                gi = q * NF1 + i
                e.wait_ge(po.h, b_po + 16 * (i + 1) if q == 1 else b_po)
                e.wait_ge(sgu.h, (b_gu + 32 * q + i - 1) if i >= 2 else (b_gu + 32 * q))
                ld.inc(e.dma_start(out=k0c[gi % 2][:, :],
                                   in_=t["wck"][C0 + i * 128 * KT * 128:C0 + (i + 1) * 128 * KT * 128]
                                   .rearrange("(a c) -> a c", a=128)), 16)
            for i in range(NF1):
                gi = q * NF1 + i
                e.wait_ge(po.h, b_po + 16 * (32 + i + 1) if q == 1 else b_po)
                if i >= 2:
                    e.wait_ge(sdn.h, b_dn + 32 * q + i - 1)
                else:
                    e.wait_ge(sat.h, b_at + 8 * q)
                ld.inc(e.dma_start(out=kc[gi % 2][:, :],
                                   in_=t["wck"][C1 + i * 128 * NF1 * 128:C1 + (i + 1) * 128 * NF1 * 128]
                                   .rearrange("(a c) -> a c", a=128)), 16)
            for i in range(KT):
                gi = q * KT + i
                e.wait_ge(po.h, b_po + 16 * (64 + i + 1) if q == 1 else b_po)
                if i >= 2:
                    e.wait_ge(sat.h, b_at + 8 * q + i - 1)
                else:
                    e.wait_ge(sdn.h, b_dn + 32 * (q + 1))
                ld.inc(e.dma_start(out=kc[gi % 2][:, :],
                                   in_=t["wck"][C2 + i * 128 * NF1 * 128:C2 + (i + 1) * 128 * NF1 * 128]
                                   .rearrange("(a c) -> a c", a=128)), 16)
    p.block.sync(sync)

    def gp(e):
        for q in range(4):
            toff = q * TH
            if q == 0:
                e.wait_ge(st.h, stream_in_dram)
                e.wait_ge(mm.h, mm_fence)
            else:
                e.wait_ge(sgu.h, b_gu + 32 * q)
            for k in range(KT):
                pgd.inc(e.dma_start(out=xb[:, k * TH:(k + 1) * TH],
                                    in_=t["xs_dram"][k * 128:(k + 1) * 128, toff:toff + TH]), 16)
            if q == 0:
                for i in range(NF1):
                    e.wait_ge(ld.h, lda(0, 2 * (i + 1)))
                    if i >= 2:
                        e.wait_ge(sgu.h, b_gu + i - 1)
                    swd.inc(e.tensor_tensor(k0c[i % 2][:, :], s0a[i % 2][:, :], s0b[i % 2][:, :], ALU.subtract), 1)
                    po.inc(e.dma_start(out=t["wck"][C0 + i * 128 * KT * 128:C0 + (i + 1) * 128 * KT * 128]
                                       .rearrange("(a c) -> a c", a=128), in_=k0c[i % 2][:, :]), 16)
                for i in range(NF1):
                    e.wait_ge(ld.h, lda(0, 64 + i + 1))
                    if i >= 2:
                        e.wait_ge(sdn.h, b_dn + i - 1)
                    if i < 2:
                        e.wait_ge(sat.h, b_at)
                    swd.inc(e.tensor_copy(kc[i % 2][:, :], s1g[i % 2][:, :]), 1)
                    po.inc(e.dma_start(out=t["wck"][C1 + i * 128 * NF1 * 128:C1 + (i + 1) * 128 * NF1 * 128]
                                       .rearrange("(a c) -> a c", a=128), in_=kc[i % 2][:, :]), 16)
                for i in range(KT):
                    e.wait_ge(ld.h, lda(0, 96 + i + 1))
                    if i < 2:
                        e.wait_ge(sdn.h, b_dn + 32)
                    else:
                        e.wait_ge(sat.h, b_at + i - 1)
                    swd.inc(e.tensor_copy(kc[i % 2][:, :], s1g[i % 2][:, :]), 1)
                    po.inc(e.dma_start(out=t["wck"][C2 + i * 128 * NF1 * 128:C2 + (i + 1) * 128 * NF1 * 128]
                                       .rearrange("(a c) -> a c", a=128), in_=kc[i % 2][:, :]), 16)
    p.block.gpsimd(gp)

    def tens(e):
        e.wait_ge(pj.h, svec_done)
        for q in range(4):
            e.wait_ge(pgd.h, b_pgd + 16 * KT * (q + 1))
            for i in range(NF1):
                if q == 0:
                    e.wait_ge(swd.h, swd_k0(i))
                    e.wait_ge(po.h, b_po + 16 * (i + 1))
                else:
                    e.wait_ge(ld.h, lda(q, i + 1))
                gg = 72 * q + i
                if gg >= 4:
                    e.wait_ge(fev.h, b_fev + gg - 3)
                ins = None
                for k in range(KT):
                    ins = e.matmul(ps1[gg % 4][:, :], k0c[(q * NF1 + i) % 2][:, k * 128:(k + 1) * 128],
                                   xb[:, k * TH:k * TH + TH],
                                   start=(k == 0), stop=(k == KT - 1))
                sgu.inc(ins, 1)
            e.wait_ge(fev.h, b_fev + 72 * q + 32)
            e.wait_ge(pmx.h, b_pmx + 2 * q)
            ins = None
            for k in range(NF1):
                ins = e.matmul(ps_r[0][0:1, :], t["ones_col_b"][:, 0:1],
                               h1T[:, k * TH:(k + 1) * TH], start=(k == 0), stop=(k == NF1 - 1))
            hh.inc(ins, 1)
            for i in range(NF1):
                if q == 0:
                    e.wait_ge(swd.h, swd_k1(i))
                else:
                    e.wait_ge(ld.h, lda(q, 32 + i + 1))
                gg = 72 * q + 32 + i
                if gg >= 4:
                    e.wait_ge(fev.h, b_fev + gg - 3)
                if i == 0:
                    e.wait_ge(pmx.h, b_pmx + 2 * q + 1)
                ins = None
                for k in range(NF1):
                    ins = e.matmul(ps1[gg % 4][:, :], kc[(q * NF1 + i) % 2][:, k * 128:(k + 1) * 128],
                                   h1T[:, k * TH:(k + 1) * TH], start=(k == 0), stop=False)
                ins = e.matmul(ps1[gg % 4][:, :], t["ones_row"][0:1, :],
                               rs[0:1, 0:TH], start=False, stop=True)
                sdn.inc(ins, 1)
            e.wait_ge(fev.h, b_fev + 72 * q + 64)
            e.wait_ge(pmx.h, b_pmx + 2 * q)
            ins = None
            for k in range(NF1):
                ins = e.matmul(ps_r[1][0:1, :], t["ones_col_b"][:, 0:1],
                               h2T[:, k * TH:(k + 1) * TH], start=(k == 0), stop=(k == NF1 - 1))
            hh.inc(ins, 1)
            for i in range(KT):
                if q == 0:
                    e.wait_ge(swd.h, swd_k2(i))
                else:
                    e.wait_ge(ld.h, lda(q, 64 + i + 1))
                gg = 72 * q + 64 + i
                if gg >= 4:
                    e.wait_ge(fev.h, b_fev + gg - 3)
                if i == 0:
                    e.wait_ge(pmx.h, b_pmx + 2 * q + 2)
                ins = None
                for k in range(NF1):
                    ins = e.matmul(ps1[gg % 4][:, :], kc[(q * KT + i) % 2][:, k * 128:(k + 1) * 128],
                                   h2T[:, k * TH:(k + 1) * TH], start=(k == 0), stop=False)
                ins = e.matmul(ps1[gg % 4][:, :], t["ones_row"][0:1, :],
                               rs[0:1, TH:2 * TH], start=False, stop=True)
                sat.inc(ins, 1)
    p.block.tensor(tens)

    def vec(e):
        e.wait_ge(ld.h, b_ld + 16 * KT)
        for q in range(4):
            for i in range(NF1):
                gg = 72 * q + i
                e.wait_ge(sgu.h, b_gu + 32 * q + i + 1)
                if q >= 1 and i == 0:
                    e.wait_ge(sdn.h, b_dn + 32 * q)
                    e.wait_ge(hh.h, b_hh + 2 * q - 1)
                fev.inc(e.tensor_scalar(h1T[:, i * TH:(i + 1) * TH],
                                        ps1[gg % 4][:, :], svec[:, 0:1], 0.0, ALU.mult, ALU.max), 1)
            e.wait_ge(hh.h, b_hh + 2 * q + 1)
            pmx.inc(e.tensor_scalar(rs[0:1, 0:TH], ps_r[0][0:1, :], -0.5, None, ALU.mult), 1)
            for i in range(NF1):
                gg = 72 * q + 32 + i
                e.wait_ge(sdn.h, b_dn + 32 * q + i + 1)
                if q >= 1 and i == 0:
                    e.wait_ge(sat.h, b_at + 8 * q)
                    e.wait_ge(hh.h, b_hh + 2 * q)
                fev.inc(e.tensor_scalar(h2T[:, i * TH:(i + 1) * TH],
                                        ps1[gg % 4][:, :], svec[:, 1:2], 0.0, ALU.mult, ALU.max), 1)
            e.wait_ge(hh.h, b_hh + 2 * q + 2)
            pmx.inc(e.tensor_scalar(rs[0:1, TH:2 * TH], ps_r[1][0:1, :], -0.5, None, ALU.mult), 1)
            for i in range(KT):
                gg = 72 * q + 64 + i
                e.wait_ge(sat.h, b_at + 8 * q + i + 1)
                e.tensor_scalar(t1b[:, :], ps1[gg % 4][:, :], svec[:, 2:3], None, ALU.mult)
                sl = slice(i * T + q * TH, i * T + q * TH + TH)
                fev.inc(e.tensor_tensor(xs[:, sl], xs[:, sl], t1b[:, :], ALU.add), 1)
    p.block.vector(vec)

    def vec_bridge(e):
        for k in range(KT):
            e.wait_ge(fev.h, b_fev + 72 * 3 + 64 + k + 1)
            ev.inc(e.tensor_copy(t["junk"][:, 0:1], t["junk"][:, 1:2]), 1)
    p.block.vector(vec_bridge)
    y_ready = ev.n
    inner.close()

    _emit_ln(p, xs, y_ready, tag="ln4")
    p._ffn_ret = (xs, phase)


def build_program(phases=("l0",)):
    nc = bass.Bass("TRN2", target_bir_lowering=False, debug=False)
    t = {}
    t["xT_hbm"] = nc.dram_tensor("xT", [D, T], F32, kind="ExternalInput").ap()
    t["ident_hbm"] = nc.dram_tensor("ident", [128, 128], F32, kind="ExternalInput").ap()
    t["s012_hbm"] = nc.dram_tensor("s012", [3], F32, kind="ExternalInput").ap()
    if "l0" in phases:
        t["wiT_hbm"] = nc.dram_tensor("wiT", [D, D], F32, kind="ExternalInput").ap()
        t["whT_hbm"] = nc.dram_tensor("whT", [D, D], F32, kind="ExternalInput").ap()
        t["bi_hbm"] = nc.dram_tensor("bi", [D], F32, kind="ExternalInput").ap()
        t["bh_hbm"] = nc.dram_tensor("bh", [D], F32, kind="ExternalInput").ap()
    if "moe" in phases:
        for e in range(8):
            t[f"wg{e}T_hbm"] = nc.dram_tensor(f"wg{e}T", [D, 2048], F32, kind="ExternalInput").ap()
            t[f"wu{e}T_hbm"] = nc.dram_tensor(f"wu{e}T", [D, 2048], F32, kind="ExternalInput").ap()
            t[f"wd{e}T_hbm"] = nc.dram_tensor(f"wd{e}T", [2048, D], F32, kind="ExternalInput").ap()
        t["wgsT_hbm"] = nc.dram_tensor("wgsT", [D, 2048], F32, kind="ExternalInput").ap()
        t["wusT_hbm"] = nc.dram_tensor("wusT", [D, 2048], F32, kind="ExternalInput").ap()
        t["wdsT_hbm"] = nc.dram_tensor("wdsT", [2048, D], F32, kind="ExternalInput").ap()
        t["wrT_hbm"] = nc.dram_tensor("wrT", [D, 8], F32, kind="ExternalInput").ap()
        t["rb_hbm"] = nc.dram_tensor("rb", [8], F32, kind="ExternalInput").ap()
    if "attn" in phases:
        for w in ["wq", "wk", "wv", "wo"]:
            t[w + "T_hbm"] = nc.dram_tensor(w + "T", [D, D], F32, kind="ExternalInput").ap()
        for b in ["bq", "bk", "bv", "bo"]:
            t[b + "_hbm"] = nc.dram_tensor(b, [D], F32, kind="ExternalInput").ap()
        t["rextT_hbm"] = nc.dram_tensor("rextT", [64, 4096], F32, kind="ExternalInput").ap()
        t["pscr_h"] = nc.dram_tensor("pscr", [2, 16 * 128 * 2176], BF16, kind="Internal")
        t["pscr"] = t["pscr_h"].ap()
    if "ffn" in phases:
        t["k0a_hbm"] = nc.dram_tensor("k0a", [D, 4096], U8, kind="ExternalInput").ap()
        t["k0b_hbm"] = nc.dram_tensor("k0b", [D, 4096], U8, kind="ExternalInput").ap()
        t["k1_hbm"] = nc.dram_tensor("k1", [4096, 4096], U8, kind="ExternalInput").ap()
        t["k2_hbm"] = nc.dram_tensor("k2", [4096, D], U8, kind="ExternalInput").ap()
        t["wck"] = nc.dram_tensor("wck", [32 * 128 * KT * 128 + 40 * 128 * 32 * 128], BF16, kind="Internal").ap()
    t["outT_hbm"] = nc.dram_tensor("outT", [D, T], F32, kind="ExternalOutput").ap()
    t["xs_dram"] = nc.dram_tensor("xs_dram", [D, T], F32, kind="Internal").ap()

    with contextlib.ExitStack() as es:
        t["ident"] = es.enter_context(nc.sbuf_tensor("identsb", [128, 128], F32))
        t["identb"] = es.enter_context(nc.sbuf_tensor("identbsb", [128, 128], BF16))
        t["ones_col"] = es.enter_context(nc.sbuf_tensor("ones_col", [128, 1], F32))
        t["ones_col_b"] = es.enter_context(nc.sbuf_tensor("ones_col_b", [128, 1], BF16))
        t["ones_row"] = es.enter_context(nc.sbuf_tensor("ones_row", [1, 128], F32))
        t["ones_row_b"] = es.enter_context(nc.sbuf_tensor("ones_row_b", [1, 128], BF16))
        t["s012_sb"] = es.enter_context(nc.sbuf_tensor("s012_sb", [1, 3], F32))
        t["junk"] = es.enter_context(nc.sbuf_tensor("junk", [128, 2], BF16))

        sems = {}
        for name in ["ld", "mm", "ev", "act", "hh", "init", "ln", "st",
                     "sgu", "sdn", "ssl", "sat", "sde", "swu", "swd", "scw", "tp",
                     "pj", "pg", "pgd", "pp", "pa", "po", "pi", "pc",
                     "pS", "pmx", "pex", "psc", "ptr", "pte", "pav", "poe"]:
            sems[name] = Sem(es.enter_context(nc.semaphore(name)))
        block = es.enter_context(nc.Block())
        p = P(nc, block, es, sems, t)

        _emit_init(p)
        p.stream_in_dram = 0
        last = phases[-1]

        def store(xs, scope, is_last):
            st = sems["st"]
            stream_ready = p.stream_ready
            st0 = st.n
            dst = t["outT_hbm"] if is_last else t["xs_dram"]

            def gp(e):
                e.wait_ge(sems["ln"].h, stream_ready)
                for k in range(KT):
                    st.inc(e.dma_start(out=dst[k * 128:(k + 1) * 128, :], in_=xs[:, k * T:(k + 1) * T]), 16)
                if is_last:
                    e.wait_ge(st.h, st0 + 16 * KT)
            p.block.gpsimd(gp)
            p.stream_in_dram = st.n
            scope.close()

        if "l0" in phases:
            _emit_l0(p, t["xT_hbm"])
            store(*p._l0_ret, last == "l0")
        if "moe" in phases:
            _emit_moe(p)
            store(*p._moe_ret, last == "moe")
        if "attn" in phases:
            _emit_attn(p)
            store(*p._attn_ret, last == "attn")
        if "ffn" in phases:
            _emit_ffn(p)
            store(*p._ffn_ret, last == "ffn")
    return nc


def _host_inputs_per_core(inputs, b, phases):
    x = np.asarray(inputs["L0_attn_x"])[b]
    m = {
        "xT": np.ascontiguousarray(x.T),
        "ident": np.eye(128, dtype=np.float32),
        "s012": np.stack([np.float32(inputs["L1_ffn_s0"]), np.float32(inputs["L1_ffn_s1"]),
                          np.float32(inputs["L1_ffn_s2"])]).astype(np.float32),
    }
    if "l0" in phases:
        m["wiT"] = np.ascontiguousarray(np.asarray(inputs["L0_attn_Wi"]).T)
        m["whT"] = np.ascontiguousarray(np.asarray(inputs["L0_attn_Wh"]).T)
        m["bi"] = np.asarray(inputs["L0_attn_bi"])
        m["bh"] = np.asarray(inputs["L0_attn_bh"])
    if "moe" in phases:
        for e in range(8):
            m[f"wg{e}T"] = np.ascontiguousarray(np.asarray(inputs["L0_ffn_Wg"])[e].T)
            m[f"wu{e}T"] = np.ascontiguousarray(np.asarray(inputs["L0_ffn_Wu"])[e].T)
            m[f"wd{e}T"] = np.ascontiguousarray(np.asarray(inputs["L0_ffn_Wd"])[e].T)
        m["wgsT"] = np.ascontiguousarray(np.asarray(inputs["L0_ffn_Wg_s"]).T)
        m["wusT"] = np.ascontiguousarray(np.asarray(inputs["L0_ffn_Wu_s"]).T)
        m["wdsT"] = np.ascontiguousarray(np.asarray(inputs["L0_ffn_Wd_s"]).T)
        m["wrT"] = np.ascontiguousarray(np.asarray(inputs["L0_ffn_Wr"]).T)
        m["rb"] = np.asarray(inputs["L0_ffn_rb"])
    if "attn" in phases:
        m["wqT"] = np.ascontiguousarray(np.asarray(inputs["L1_attn_Wq"]).T)
        m["wkT"] = np.ascontiguousarray(np.asarray(inputs["L1_attn_Wk"]).T)
        m["wvT"] = np.ascontiguousarray(np.asarray(inputs["L1_attn_Wv"]).T)
        m["woT"] = np.ascontiguousarray(np.asarray(inputs["L1_attn_Wo"]).T)
        m["bq"] = np.asarray(inputs["L1_attn_bq"])
        m["bk"] = np.asarray(inputs["L1_attn_bk"])
        m["bv"] = np.asarray(inputs["L1_attn_bv"])
        m["bo"] = np.asarray(inputs["L1_attn_bo"])
        rt = np.asarray(inputs["L1_attn_rel_table"])
        rext = rt[np.clip(3070 - np.arange(4096), 0, 2046)]
        m["rextT"] = np.ascontiguousarray(rext.T)
    if "ffn" in phases:
        k0 = np.asarray(inputs["L1_ffn_k0"])
        m["k0a"] = np.ascontiguousarray(k0[:D]).view(np.uint8)
        m["k0b"] = np.ascontiguousarray(k0[D:]).view(np.uint8)
        m["k1"] = np.ascontiguousarray(np.asarray(inputs["L1_ffn_k1"])).view(np.uint8)
        m["k2"] = np.ascontiguousarray(np.asarray(inputs["L1_ffn_k2"])).view(np.uint8)
    return m


def kernel(**inputs):
    phases = ("l0", "moe", "attn", "ffn")
    nc = build_program(phases)
    in_maps = [_host_inputs_per_core(inputs, b, phases) for b in range(8)]
    res = run_bass_kernel_spmd(nc, in_maps, core_ids=list(range(8)))
    return np.stack([r["outT"].T for r in res.results], axis=0)


# version 07
# speedup vs baseline: 1.2029x; 1.1772x over previous
import contextlib
import os
import numpy as np
import concourse.bass as bass
import concourse.mybir as mybir
from concourse.bass_utils import run_bass_kernel_spmd

F32 = mybir.dt.float32
BF16 = mybir.dt.bfloat16
U8 = mybir.dt.uint8
AF = mybir.ActivationFunctionType
ALU = mybir.AluOpType

D = 1024
T = 2048
KT = D // 128
NTC = T // 512
LN_EPS = 1e-5


class Sem:
    def __init__(self, handle):
        self.h = handle
        self.n = 0

    def inc(self, instr, k=1):
        instr.then_inc(self.h, k)
        self.n += k
        return instr


class P:
    def __init__(self, nc, block, es, sems, tensors):
        self.nc = nc
        self.block = block
        self.es = es
        self.s = sems
        self.t = tensors
        self.init_done = 0
        self.stream_ready = 0


def _emit_init(p):
    nc, s, t = p.nc, p.s, p.t
    ld, init = s["ld"], s["init"]
    ld0 = ld.n

    def sync(e):
        ld.inc(e.dma_start(out=t["ident"][:], in_=t["ident_hbm"][:, :]), 16)
        ld.inc(e.dma_start(out=t["s012_sb"][0:1, :], in_=t["s012_hbm"].unsqueeze(0)), 16)
    p.block.sync(sync)

    def vec(e):
        init.inc(e.memset(t["ones_col"][:], 1.0), 1)
        init.inc(e.memset(t["ones_col_b"][:], 1.0), 1)
        init.inc(e.memset(t["ones_row"][:], 1.0), 1)
        init.inc(e.memset(t["ones_row_b"][:], 1.0), 1)
        e.wait_ge(ld.h, ld0 + 32)
        init.inc(e.tensor_copy(t["identb"][:], t["ident"][:]), 1)
    p.block.vector(vec)
    p.init_done = init.n


def _emit_l0(p, stream_src):
    nc, s, t = p.nc, p.s, p.t
    ld, mm, ev, act, hh = s["ld"], s["mm"], s["ev"], s["act"], s["hh"]

    phase = contextlib.ExitStack()
    xs = phase.enter_context(nc.sbuf_tensor("l0_xs", [128, KT * T], F32))
    xib_scope = contextlib.ExitStack()
    xibT = xib_scope.enter_context(nc.sbuf_tensor("xibT", [128, KT * T], F32))
    inner = contextlib.ExitStack()
    xb = inner.enter_context(nc.sbuf_tensor("l0_xb", [128, KT * T], BF16))
    stag = inner.enter_context(nc.sbuf_tensor("l0_stag", [128, 2 * D], F32))
    wiT_sb = inner.enter_context(nc.sbuf_tensor("wiT_sb", [128, KT * D], BF16))
    bihT = inner.enter_context(nc.sbuf_tensor("bihT", [128, 2 * KT], F32))
    ps_x = [inner.enter_context(nc.psum_tensor(f"l0ps{i}", [128, 512], F32)) for i in range(4)]

    ld0 = ld.n
    ev0 = ev.n

    def sync(e):
        for k in range(KT):
            ld.inc(e.dma_start(out=xs[:, k * T:(k + 1) * T], in_=stream_src[k * 128:(k + 1) * 128, :]), 16)
        for m in range(KT):
            ld.inc(e.dma_start(out=bihT[:, m:m + 1], in_=t["bi_hbm"][m * 128:(m + 1) * 128].unsqueeze(1)), 16)
            ld.inc(e.dma_start(out=bihT[:, KT + m:KT + m + 1], in_=t["bh_hbm"][m * 128:(m + 1) * 128].unsqueeze(1)), 16)
        for q in range(4):
            if q > 0:
                e.wait_ge(ev.h, ev0 + 1 + q)
            ld.inc(e.dma_start(out=stag[:, :].rearrange("p (k d) -> p k d", k=2),
                               in_=t["wiT_hbm"].rearrange("(k p) d -> p k d", p=128)[:, 2 * q:2 * q + 2, :]), 16)
    p.block.sync(sync)
    after_stream = ld0 + 16 * KT
    after_bias = after_stream + 16 * 2 * KT

    def vec(e):
        e.wait_ge(s["init"].h, p.init_done)
        e.wait_ge(ld.h, after_stream)
        for k in range(KT):
            e.tensor_copy(xb[:, k * T:(k + 1) * T], xs[:, k * T:(k + 1) * T])
        e.wait_ge(ld.h, after_bias)
        ev.inc(e.tensor_tensor(bihT[:, 0:KT], bihT[:, 0:KT], bihT[:, KT:2 * KT], ALU.add), 1)
        for q in range(4):
            e.wait_ge(ld.h, after_bias + 16 * (q + 1))
            ev.inc(e.tensor_copy(wiT_sb[:, q * 2 * D:(q + 1) * 2 * D], stag[:, :]), 1)
    p.block.vector(vec)
    wi_ready = ev.n

    mm0 = mm.n
    ev_xi0 = ev.n

    def tens(e):
        e.wait_ge(ev.h, wi_ready)
        for idx in range(KT * NTC):
            mt, tc = divmod(idx, NTC)
            if idx >= 4:
                e.wait_ge(ev.h, ev_xi0 + (idx - 4) + 1)
            pb = ps_x[idx % 4]
            ins = None
            for k in range(KT):
                ins = e.matmul(pb[:, :],
                               wiT_sb[:, k * D + mt * 128: k * D + mt * 128 + 128],
                               xb[:, k * T + tc * 512: k * T + tc * 512 + 512],
                               start=(k == 0), stop=(k == KT - 1))
            mm.inc(ins, 1)
    p.block.tensor(tens)

    def vec2(e):
        for idx in range(KT * NTC):
            mt, tc = divmod(idx, NTC)
            e.wait_ge(mm.h, mm0 + idx + 1)
            pb = ps_x[idx % 4]
            ev.inc(e.tensor_scalar(xibT[:, mt * T + tc * 512: mt * T + tc * 512 + 512],
                                   pb[:, :], bihT[:, mt:mt + 1], None, ALU.add), 1)
    p.block.vector(vec2)
    xib_done = ev.n
    mm_xi_done = mm.n
    inner.close()

    if os.environ.get("MK_DBG") == "xib":
        ln = s["ln"]
        def vend(e):
            e.wait_ge(ev.h, xib_done)
            for k in range(KT):
                e.tensor_copy(xs[:, k * T:(k + 1) * T], xibT[:, k * T:(k + 1) * T])
            ln.inc(e.tensor_copy(t["junk"][:, 0:1], t["junk"][:, 1:2]), 1)
        p.block.vector(vend)
        p.stream_ready = ln.n
        p._l0_ret = (xs, phase)
        return

    inner2 = contextlib.ExitStack()
    whT_sb = inner2.enter_context(nc.sbuf_tensor("whT_sb", [128, KT * D], BF16))
    stag_scope = contextlib.ExitStack()
    stag2 = stag_scope.enter_context(nc.sbuf_tensor("l0_stag2", [128, 2 * D], F32))

    ld_wh0 = ld.n
    ev_wh0 = ev.n

    def sync2(e):
        e.wait_ge(mm.h, mm_xi_done)
        for q in range(4):
            if q > 0:
                e.wait_ge(ev.h, ev_wh0 + q)
            ld.inc(e.dma_start(out=stag2[:, :].rearrange("p (k d) -> p k d", k=2),
                               in_=t["whT_hbm"].rearrange("(k p) d -> p k d", p=128)[:, 2 * q:2 * q + 2, :]), 16)
    p.block.sync(sync2)

    def vec3(e):
        for q in range(4):
            e.wait_ge(ld.h, ld_wh0 + 16 * (q + 1))
            ev.inc(e.tensor_copy(whT_sb[:, q * 2 * D:(q + 1) * 2 * D], stag2[:, :]), 1)
    p.block.vector(vec3)
    wh_ready = ev.n
    stag_scope.close()
    hsT = inner2.enter_context(nc.sbuf_tensor("hsT", [128, KT * (T + 1)], BF16))
    ps_h = [inner2.enter_context(nc.psum_tensor(f"rnnps{m}", [128, 512], F32)) for m in range(KT)]

    hh0 = hh.n

    def vec4(e):
        for m in range(KT):
            hh.inc(e.memset(hsT[:, m * (T + 1):m * (T + 1) + 1], 0.0), 1)
    p.block.vector(vec4)

    mm_rnn0 = mm.n

    def tens2(e):
        e.wait_ge(ev.h, wh_ready)
        for tt in range(1, T + 1):
            e.wait_ge(hh.h, hh0 + KT * tt)
            for m in range(KT):
                ins = None
                for k in range(KT):
                    ins = e.matmul(ps_h[m][:, 0:1],
                                   whT_sb[:, k * D + m * 128:k * D + m * 128 + 128],
                                   hsT[:, k * (T + 1) + (tt - 1):k * (T + 1) + tt],
                                   start=(k == 0), stop=(k == KT - 1))
                mm.inc(ins, 1)
    p.block.tensor(tens2)

    def scal(e):
        for tt in range(1, T + 1):
            for m in range(KT):
                e.wait_ge(mm.h, mm_rnn0 + KT * (tt - 1) + m + 1)
                hh.inc(e.activation(out=hsT[:, m * (T + 1) + tt:m * (T + 1) + tt + 1],
                                    in_=ps_h[m][:, 0:1], func=AF.Tanh,
                                    bias=xibT[:, m * T + (tt - 1):m * T + tt], scale=1.0), 1)
    p.block.scalar(scal)
    rnn_done = hh.n

    if os.environ.get("MK_DBG") == "whs":
        ln = s["ln"]
        def vwhs(e):
            e.wait_ge(ev.h, wh_ready)
            for q in range(4):
                e.tensor_copy(xs[:, q * 2 * D:(q + 1) * 2 * D], whT_sb[:, q * 2 * D:(q + 1) * 2 * D])
            e.wait_ge(hh.h, hh0 + KT)
            for m in range(KT):
                e.tensor_copy(xs[:, 4 * 2 * D + m: 4 * 2 * D + m + 1], hsT[:, m * (T + 1):m * (T + 1) + 1])
            ln.inc(e.tensor_copy(t["junk"][:, 0:1], t["junk"][:, 1:2]), 1)
        p.block.vector(vwhs)
        p.stream_ready = ln.n
        inner2.close()
        p._l0_ret = (xs, phase)
        return

    if os.environ.get("MK_DBG") == "rnn":
        ln = s["ln"]
        def vrnn(e):
            e.wait_ge(hh.h, rnn_done)
            for k in range(KT):
                e.tensor_copy(xs[:, k * T:(k + 1) * T], hsT[:, k * (T + 1) + 1:k * (T + 1) + T + 1])
            ln.inc(e.tensor_copy(t["junk"][:, 0:1], t["junk"][:, 1:2]), 1)
        p.block.vector(vrnn)
        p.stream_ready = ln.n
        inner2.close()
        p._l0_ret = (xs, phase)
        return

    def vec5(e):
        e.wait_ge(hh.h, rnn_done)
        for k in range(KT):
            ev.inc(e.tensor_tensor(xs[:, k * T:(k + 1) * T], xs[:, k * T:(k + 1) * T],
                                   hsT[:, k * (T + 1) + 1:k * (T + 1) + T + 1], ALU.add), 1)
    p.block.vector(vec5)
    y_ready = ev.n
    inner2.close()
    xib_scope.close()
    p._l0_ret = (xs, phase)

    if os.environ.get("MK_DBG") == "noln":
        ln = s["ln"]
        def vend(e):
            e.wait_ge(ev.h, y_ready)
            ln.inc(e.tensor_copy(t["junk"][:, 0:1], t["junk"][:, 1:2]), 1)
        p.block.vector(vend)
        p.stream_ready = ln.n
    else:
        _emit_ln(p, xs, y_ready, tag="ln1")


def _emit_ln(p, xs, ev_y_ready, tag):
    nc, s, t = p.nc, p.s, p.t
    mm, ev, act, ln = s["mm"], s["ev"], s["act"], s["ln"]

    inner = contextlib.ExitStack()
    ysq = inner.enter_context(nc.sbuf_tensor(f"{tag}_ysq", [128, T], F32))
    rows = inner.enter_context(nc.sbuf_tensor(f"{tag}_rows", [1, 4 * T], F32))
    MU, VB, RA, TMP = (i * T for i in range(4))

    act0 = act.n
    mm0 = mm.n

    stats = contextlib.ExitStack()
    ps_s = stats.enter_context(nc.psum_tensor(f"{tag}_ps_s", [1, T], F32))
    ps_q = stats.enter_context(nc.psum_tensor(f"{tag}_ps_q", [1, T], F32))

    def scal(e):
        for k in range(KT):
            e.wait_ge(ev.h, ev_y_ready - KT + k + 1)
            if k >= 1:
                e.wait_ge(mm.h, mm0 + k * NTC)
            act.inc(e.activation(out=ysq[:, :], in_=xs[:, k * T:(k + 1) * T], func=AF.Square), 1)
    p.block.scalar(scal)

    def tens(e):
        e.wait_ge(s["init"].h, p.init_done)
        e.wait_ge(ev.h, ev_y_ready)
        for k in range(KT):
            e.wait_ge(act.h, act0 + k + 1)
            for tc in range(NTC):
                e.matmul(ps_s[0:1, tc * 512:(tc + 1) * 512], t["ones_col"][:, 0:1],
                         xs[:, k * T + tc * 512: k * T + tc * 512 + 512],
                         start=(k == 0), stop=(k == KT - 1))
                mm.inc(e.matmul(ps_q[0:1, tc * 512:(tc + 1) * 512], t["ones_col"][:, 0:1],
                                ysq[:, tc * 512:(tc + 1) * 512],
                                start=(k == 0), stop=(k == KT - 1)), 1)
    p.block.tensor(tens)
    mm_stats = mm.n

    ev_v0 = ev.n

    def vec(e):
        e.wait_ge(mm.h, mm_stats)
        e.tensor_scalar(rows[0:1, MU:MU + T], ps_s[0:1, :], 1.0 / D, None, ALU.mult)
        e.tensor_scalar(rows[0:1, TMP:TMP + T], ps_q[0:1, :], 1.0 / D, None, ALU.mult)
        e.tensor_tensor(rows[0:1, RA:RA + T], rows[0:1, MU:MU + T], rows[0:1, MU:MU + T], ALU.mult)
        ev.inc(e.tensor_tensor(rows[0:1, VB:VB + T], rows[0:1, TMP:TMP + T], rows[0:1, RA:RA + T], ALU.subtract), 1)
    p.block.vector(vec)
    ev_var = ev.n

    def vec1b(e):
        e.wait_ge(ev.h, ev_var)
        ev.inc(e.tensor_scalar(rows[0:1, VB:VB + T], rows[0:1, VB:VB + T], LN_EPS, None, ALU.add), 1)
    p.block.vector(vec1b)
    ev_vareps = ev.n

    def scal2(e):
        e.wait_ge(ev.h, ev_vareps)
        e.activation(out=rows[0:1, TMP:TMP + T], in_=rows[0:1, VB:VB + T], func=AF.Ln, scale=1.0)
        act.inc(e.activation(out=rows[0:1, RA:RA + T], in_=rows[0:1, TMP:TMP + T], func=AF.Exp, scale=-0.5), 1)
    p.block.scalar(scal2)
    act_rstd = act.n

    def vec2(e):
        e.wait_ge(act.h, act_rstd)
        e.tensor_tensor(rows[0:1, TMP:TMP + T], rows[0:1, MU:MU + T], rows[0:1, RA:RA + T], ALU.mult)
        ev.inc(e.tensor_scalar(rows[0:1, VB:VB + T], rows[0:1, TMP:TMP + T], -1.0, None, ALU.mult), 1)
    p.block.vector(vec2)
    ev_rows = ev.n
    stats.close()

    bc = contextlib.ExitStack()
    ps_bc = [bc.enter_context(nc.psum_tensor(f"{tag}_bc{i}", [128, 512], F32)) for i in range(2)]
    mm_bc0 = mm.n

    def tens2(e):
        e.wait_ge(ev.h, ev_rows)
        for tc in range(NTC):
            if tc >= 1:
                e.wait_ge(ev.h, ev_rows + tc * KT)
            e.matmul(ps_bc[0][:, :], t["ones_row"][0:1, :], rows[0:1, RA + tc * 512:RA + tc * 512 + 512],
                     start=True, stop=True)
            mm.inc(e.matmul(ps_bc[1][:, :], t["ones_row"][0:1, :], rows[0:1, VB + tc * 512:VB + tc * 512 + 512],
                            start=True, stop=True), 1)
    p.block.tensor(tens2)

    def vec3(e):
        for tc in range(NTC):
            e.wait_ge(mm.h, mm_bc0 + tc + 1)
            for k in range(KT):
                sl = slice(k * T + tc * 512, k * T + tc * 512 + 512)
                e.tensor_tensor(xs[:, sl], xs[:, sl], ps_bc[0][:, :], ALU.mult)
                ev.inc(e.tensor_tensor(xs[:, sl], xs[:, sl], ps_bc[1][:, :], ALU.add), 1)
        ln.inc(e.tensor_copy(t["junk"][:, 0:1], t["junk"][:, 1:2]), 1)
    p.block.vector(vec3)
    p.stream_ready = ln.n
    bc.close()
    inner.close()


def _emit_moe(p):
    nc, s, t = p.nc, p.s, p.t
    ld, mm, ev, act, ln = s["ld"], s["mm"], s["ev"], s["act"], s["ln"]
    IC = 16
    TH = 1024
    NE = 9

    phase = contextlib.ExitStack()
    xs = phase.enter_context(nc.sbuf_tensor("moe_xs", [128, KT * T], F32))
    inner = contextlib.ExitStack()
    xb = inner.enter_context(nc.sbuf_tensor("moe_xb", [128, KT * TH], BF16))
    aT = inner.enter_context(nc.sbuf_tensor("moe_aT", [128, IC * TH], BF16))
    wd_sb = inner.enter_context(nc.sbuf_tensor("moe_wd", [128, IC * D], BF16))
    wg_sb = [inner.enter_context(nc.sbuf_tensor(f"moe_wg{i}", [128, D], BF16)) for i in range(2)]
    wu_sb = [inner.enter_context(nc.sbuf_tensor(f"moe_wu{i}", [128, D], BF16)) for i in range(2)]
    cw_tm = inner.enter_context(nc.sbuf_tensor("moe_cwtm", [128, 2 * 64], F32))
    cw_row = inner.enter_context(nc.sbuf_tensor("moe_cwrow", [1, TH], BF16))
    cw_b = inner.enter_context(nc.sbuf_tensor("moe_cwb", [128, TH], BF16))
    g_sb = [inner.enter_context(nc.sbuf_tensor(f"moe_g{i}", [128, 512], BF16)) for i in range(2)]
    u_sb = [inner.enter_context(nc.sbuf_tensor(f"moe_u{i}", [128, 512], BF16)) for i in range(2)]
    wr_sb = inner.enter_context(nc.sbuf_tensor("moe_wr", [128, KT * 8], F32))
    rb_sb = inner.enter_context(nc.sbuf_tensor("moe_rb", [8, 1], F32))
    m1 = inner.enter_context(nc.sbuf_tensor("moe_m1", [128, 2], F32))
    msk = inner.enter_context(nc.sbuf_tensor("moe_msk", [128, 24], F32))
    probsT = inner.enter_context(nc.sbuf_tensor("moe_probs", [8, TH], F32))

    ps_g = [inner.enter_context(nc.psum_tensor(f"moe_psg{i}", [128, 512], F32)) for i in range(2)]
    ps_u = [inner.enter_context(nc.psum_tensor(f"moe_psu{i}", [128, 512], F32)) for i in range(2)]
    ps_d = [inner.enter_context(nc.psum_tensor(f"moe_psd{i}", [128, 512], F32)) for i in range(4)]

    stream_ready = p.stream_ready
    mm_fence = mm.n

    wgs = [t["wgsT_hbm"]] + [t[f"wg{e}T_hbm"] for e in range(8)]
    wus = [t["wusT_hbm"]] + [t[f"wu{e}T_hbm"] for e in range(8)]
    wds = [t["wdsT_hbm"]] + [t[f"wd{e}T_hbm"] for e in range(8)]

    sgu = s["sgu"]; sdn = s["sdn"]; ssl = s["ssl"]; sat = s["sat"]
    sde = s["sde"]; swu = s["swu"]; swd = s["swd"]; scw = s["scw"]
    b_gu, b_dn, b_sl, b_at, b_de, b_wu, b_wd, b_cw = (x.n for x in (sgu, sdn, ssl, sat, sde, swu, swd, scw))
    b_ld = ld.n
    b_ev = ev.n
    b_act = act.n

    st = s["st"]
    pgd = s["pgd"]
    b_pgd = pgd.n
    stream_in_dram = p.stream_in_dram

    def sync(e):
        e.wait_ge(st.h, stream_in_dram)
        e.wait_ge(ln.h, stream_ready)
        e.wait_ge(mm.h, mm_fence)
        for k in range(KT):
            ld.inc(e.dma_start(out=xs[:, k * T:(k + 1) * T], in_=t["xs_dram"][k * 128:(k + 1) * 128, :]), 16)
        for k in range(KT):
            ld.inc(e.dma_start(out=wr_sb[:, k * 8:(k + 1) * 8], in_=t["wrT_hbm"][k * 128:(k + 1) * 128, :]), 16)
        ld.inc(e.dma_start(out=rb_sb[0:8, 0:1], in_=t["rb_hbm"].unsqueeze(1)), 16)
    p.block.sync(sync)

    def gp_w(e):
        e.wait_ge(st.h, stream_in_dram)
        e.wait_ge(mm.h, mm_fence)
        for th in range(2):
            for j in range(NE):
                J = th * NE + j
                for ic in range(IC):
                    gi = J * IC + ic
                    if gi >= 2:
                        e.wait_ge(sgu.h, b_gu + 2 * (gi - 1))
                    pgd.inc(e.dma_start(out=wg_sb[gi % 2][:, :].rearrange("p (k c) -> p k c", k=KT),
                                        in_=wgs[j].rearrange("(k p) i -> p k i", p=128)[:, :, ic * 128:(ic + 1) * 128]), 16)
                    pgd.inc(e.dma_start(out=wu_sb[gi % 2][:, :].rearrange("p (k c) -> p k c", k=KT),
                                        in_=wus[j].rearrange("(k p) i -> p k i", p=128)[:, :, ic * 128:(ic + 1) * 128]), 16)
                for ic in range(IC):
                    if ic == 0:
                        e.wait_ge(sdn.h, b_dn + 16 * J)
                    pgd.inc(e.dma_start(out=wd_sb[:, ic * D:(ic + 1) * D], in_=wds[j][ic * 128:(ic + 1) * 128, :]), 16)
    p.block.gpsimd(gp_w)

    def pgd_wgu_done(gi):
        J, ic = divmod(gi, IC)
        return b_pgd + (J * 48 + 2 * (ic + 1)) * 16

    def pgd_wd_done(J):
        return b_pgd + (J * 48 + 48) * 16

    ev_n = [b_ev]
    act_n = [b_act]
    mm_n = [mm_fence]

    for th in range(2):
        toff = th * TH

        ev_xb0 = ev_n[0]

        def vec_xb(e, th=th, toff=toff, ev_xb0=ev_xb0):
            if th == 0:
                e.wait_ge(ld.h, b_ld + 8 * 16)
            for k in range(KT):
                ev.inc(e.tensor_copy(xb[:, k * TH:(k + 1) * TH], xs[:, k * T + toff:k * T + toff + TH]), 1)
        p.block.vector(vec_xb)
        ev_n[0] = ev.n
        xb_ready = ev.n

        mm_r0 = mm_n[0]

        def tens_r(e, th=th, toff=toff, mm_r0=mm_r0, xb_ready=xb_ready):
            e.wait_ge(ev.h, xb_ready)
            e.wait_ge(ld.h, b_ld + 17 * 16)
            e.wait_ge(sde.h, b_de + 16 * (th * NE))
            for tc in range(2):
                ins = None
                for k in range(KT):
                    ins = e.matmul(ps_d[tc % 2][0:8, :], wr_sb[:, k * 8:(k + 1) * 8],
                                   xs[:, k * T + toff + tc * 512: k * T + toff + tc * 512 + 512],
                                   start=(k == 0), stop=(k == KT - 1))
                mm.inc(ins, 1)
        p.block.tensor(tens_r)
        mm_router = mm.n
        act_r0 = act_n[0]

        def act_r(e, th=th, mm_r0=mm_r0):
            for tc in range(2):
                e.wait_ge(mm.h, mm_r0 + tc + 1)
                act.inc(e.activation(out=probsT[0:8, tc * 512:(tc + 1) * 512], in_=ps_d[tc % 2][0:8, :],
                                     func=AF.Sigmoid, bias=rb_sb[0:8, 0:1], scale=1.0), 1)
        p.block.scalar(act_r)
        act_n[0] = act.n
        act_probs = act.n
        ev_r0 = ev.n

        def tens_r2(e, th=th, act_probs=act_probs, ev_r0=ev_r0):
            e.wait_ge(act.h, act_probs)
            for tt in range(8):
                if tt >= 2:
                    e.wait_ge(ev.h, ev_r0 + 7 * (tt - 1))
                mm.inc(e.transpose(ps_d[2 + tt % 2][0:128, 0:8], probsT[0:8, tt * 128:(tt + 1) * 128],
                                   t["ident"][0:8, 0:8]), 1)
        p.block.tensor(tens_r2)
        mm_tr0 = mm_router

        def vec_r(e, th=th, mm_tr0=mm_tr0):
            tp = s["tp"]

            def st_(ins):
                tp.inc(ins, 1)
                e.wait_ge(tp.h, tp.n)
                return ins
            for tt in range(8):
                e.wait_ge(mm.h, mm_tr0 + tt + 1)
                pb = ps_d[2 + tt % 2]
                c0 = th * 64 + tt * 8
                st_(e.tensor_copy(cw_tm[:, c0:c0 + 8], pb[0:128, 0:8]))
                st_(e.reduce_max(m1[:, 0:1], cw_tm[:, c0:c0 + 8], axis=mybir.AxisListType.X))
                st_(e.tensor_scalar(msk[:, 0:8], cw_tm[:, c0:c0 + 8], m1[:, 0:1], None, ALU.is_ge))
                st_(e.tensor_scalar(msk[:, 8:16], msk[:, 0:8], -1e30, None, ALU.mult))
                st_(e.tensor_tensor(msk[:, 16:24], cw_tm[:, c0:c0 + 8], msk[:, 8:16], ALU.add))
                st_(e.reduce_max(m1[:, 1:2], msk[:, 16:24], axis=mybir.AxisListType.X))
                st_(e.tensor_scalar(msk[:, 0:8], cw_tm[:, c0:c0 + 8], m1[:, 1:2], None, ALU.is_ge))
                if os.environ.get("MK_DBG") == "cwprobs":
                    ev.inc(e.tensor_copy(cw_tm[:, c0:c0 + 8], cw_tm[:, c0:c0 + 8]) if False else e.tensor_scalar(cw_tm[:, c0:c0 + 8], cw_tm[:, c0:c0 + 8], 1.0, None, ALU.mult), 7)
                else:
                    ev.inc(e.tensor_tensor(cw_tm[:, c0:c0 + 8], cw_tm[:, c0:c0 + 8], msk[:, 0:8], ALU.mult), 7)
        p.block.vector(vec_r)
        ev_n[0] = ev.n
        ev_cwtm = ev.n
        mm_n[0] = mm.n

        for j in range(NE):
            J = th * NE + j
            routed = j >= 1
            R = th * 8 + (j - 1)
            c0cw = b_cw + R * 10

            if routed:
                mm_cw0 = mm.n

                def tens_cw(e, th=th, j=j, J=J, R=R, c0cw=c0cw, ev_cwtm=ev_cwtm):
                    e.wait_ge(ev.h, ev_cwtm)
                    e.wait_ge(sde.h, b_de + 16 * J)
                    for tt in range(8):
                        if tt >= 4:
                            e.wait_ge(scw.h, c0cw + (tt - 4) + 1)
                        mm.inc(e.transpose(ps_d[tt % 4][0:1, 0:128],
                                           cw_tm[:, th * 64 + tt * 8 + (j - 1):th * 64 + tt * 8 + j],
                                           t["ident"][:, :]), 1)
                    for tc in range(2):
                        e.wait_ge(scw.h, c0cw + max(4 * (tc + 1), 5 + tc))
                        mm.inc(e.matmul(ps_d[tc % 4][:, :], t["ones_row_b"][0:1, :],
                                        cw_row[0:1, tc * 512:tc * 512 + 512], start=True, stop=True), 1)
                p.block.tensor(tens_cw)

                def vec_cw(e, mm_cw0=mm_cw0):
                    for tt in range(8):
                        e.wait_ge(mm.h, mm_cw0 + tt + 1)
                        scw.inc(e.tensor_copy(cw_row[0:1, tt * 128:(tt + 1) * 128], ps_d[tt % 4][0:1, 0:128]), 1)
                    for tc in range(2):
                        e.wait_ge(mm.h, mm_cw0 + 8 + tc + 1)
                        scw.inc(e.tensor_copy(cw_b[:, tc * 512:(tc + 1) * 512], ps_d[tc % 4][:, :]), 1)
                p.block.vector(vec_cw)
                mm_n[0] = mm.n

            def tens_gu(e, J=J, xb_ready=xb_ready):
                e.wait_ge(ev.h, xb_ready)
                for ic in range(IC):
                    gi = J * IC + ic
                    db = gi % 2
                    e.wait_ge(pgd.h, pgd_wgu_done(gi))
                    for tc in range(2):
                        qi = gi * 2 + tc
                        if qi >= 2:
                            e.wait_ge(ssl.h, b_sl + qi - 1)
                            e.wait_ge(sat.h, b_at + qi - 1)
                        for k in range(KT):
                            e.matmul(ps_g[qi % 2][:, :], wg_sb[db][:, k * 128:(k + 1) * 128],
                                     xb[:, k * TH + tc * 512:k * TH + tc * 512 + 512],
                                     start=(k == 0), stop=(k == KT - 1))
                        ins = None
                        for k in range(KT):
                            ins = e.matmul(ps_u[qi % 2][:, :], wu_sb[db][:, k * 128:(k + 1) * 128],
                                           xb[:, k * TH + tc * 512:k * TH + tc * 512 + 512],
                                           start=(k == 0), stop=(k == KT - 1))
                        sgu.inc(ins, 1)
            p.block.tensor(tens_gu)


            def act_sl(e, J=J):
                for ic in range(IC):
                    for tc in range(2):
                        qi = (J * IC + ic) * 2 + tc
                        e.wait_ge(sgu.h, b_gu + qi + 1)
                        if qi >= 2:
                            e.wait_ge(sat.h, b_at + qi - 1)
                        ssl.inc(e.activation(out=g_sb[qi % 2][:, :], in_=ps_g[qi % 2][:, :], func=AF.Silu), 1)
            p.block.scalar(act_sl)

            def vec_au(e, J=J, routed=routed, c0cw=c0cw):
                for ic in range(IC):
                    for tc in range(2):
                        qi = (J * IC + ic) * 2 + tc
                        e.wait_ge(sgu.h, b_gu + qi + 1)
                        if routed:
                            e.wait_ge(scw.h, c0cw + 8 + tc + 1)
                            e.tensor_tensor(u_sb[qi % 2][:, :], ps_u[qi % 2][:, :],
                                            cw_b[:, tc * 512:(tc + 1) * 512], ALU.mult)
                        else:
                            e.tensor_copy(u_sb[qi % 2][:, :], ps_u[qi % 2][:, :])
                        e.wait_ge(ssl.h, b_sl + qi + 1)
                        e.wait_ge(sdn.h, b_dn + 16 * J)
                        sat.inc(e.tensor_tensor(aT[:, ic * TH + tc * 512:ic * TH + tc * 512 + 512],
                                                g_sb[qi % 2][:, :], u_sb[qi % 2][:, :], ALU.mult), 1)
            p.block.vector(vec_au)

            def tens_dn(e, J=J, routed=routed, c0cw=c0cw):
                e.wait_ge(sat.h, b_at + 32 * (J + 1))
                e.wait_ge(pgd.h, pgd_wd_done(J))
                if routed:
                    e.wait_ge(scw.h, c0cw + 10)
                for tc in range(2):
                    for ht in range(KT):
                        di = J * 16 + tc * KT + ht
                        if di >= 4:
                            e.wait_ge(sde.h, b_de + di - 3)
                        ins = None
                        for ic in range(IC):
                            ins = e.matmul(ps_d[di % 4][:, :], wd_sb[:, ic * D + ht * 128:ic * D + ht * 128 + 128],
                                           aT[:, ic * TH + tc * 512:ic * TH + tc * 512 + 512],
                                           start=(ic == 0), stop=(ic == IC - 1))
                        sdn.inc(ins, 1)
            p.block.tensor(tens_dn)
            mm_n[0] = mm.n

            def vec_dn(e, J=J, toff=toff):
                for tc in range(2):
                    for ht in range(KT):
                        di = J * 16 + tc * KT + ht
                        e.wait_ge(sdn.h, b_dn + di + 1)
                        sl = slice(ht * T + toff + tc * 512, ht * T + toff + tc * 512 + 512)
                        sde.inc(e.tensor_tensor(xs[:, sl], xs[:, sl], ps_d[di % 4][:, :], ALU.add), 1)
            p.block.vector(vec_dn)
            ev_n[0] = ev.n

    if os.environ.get("MK_DBG") in ("cw", "cwprobs"):
        def vcw(e):
            e.wait_ge(sde.h, b_de + 18 * 16)
            e.tensor_copy(xs[:, 0:128], cw_tm[:, 0:128])
            e.tensor_copy(xs[:, 128:130], m1[:, 0:2])
            e.tensor_copy(xs[:, 130:154], msk[:, 0:24])
            ln.inc(e.tensor_copy(t["junk"][:, 0:1], t["junk"][:, 1:2]), 1)
        p.block.vector(vcw)
        p.stream_ready = ln.n
        inner.close()
        p._moe_ret = (xs, phase)
        return

    def vec_bridge(e):
        for k in range(KT):
            e.wait_ge(sde.h, b_de + 8 * 16 + 8 + k + 1)
            e.wait_ge(sde.h, b_de + 17 * 16 + 8 + k + 1)
            ev.inc(e.tensor_copy(t["junk"][:, 0:1], t["junk"][:, 1:2]), 1)
    p.block.vector(vec_bridge)
    y_ready = ev.n

    if os.environ.get("MK_DBG") == "moeraw":
        def vraw(e):
            for k in range(KT):
                e.wait_ge(ev.h, y_ready - KT + k + 1)
            ln.inc(e.tensor_copy(t["junk"][:, 0:1], t["junk"][:, 1:2]), 1)
        p.block.vector(vraw)
        p.stream_ready = ln.n
        inner.close()
        p._moe_ret = (xs, phase)
        return
    inner.close()

    _emit_ln(p, xs, y_ready, tag="ln2")
    p._moe_ret = (xs, phase)


def _emit_attn(p):
    nc, s, t = p.nc, p.s, p.t
    ld, mm, ev, act, ln, st = s["ld"], s["mm"], s["ev"], s["act"], s["ln"], s["st"]
    pj = s["pj"]; pg = s["pg"]; pgd = s["pgd"]
    pp = s["pp"]; pa = s["pa"]; po = s["po"]; pi = s["pi"]; pc = s["pc"]
    pS = s["pS"]; pmx = s["pmx"]; pex = s["pex"]; psc = s["psc"]
    ptr = s["ptr"]; pte = s["pte"]; pav = s["pav"]; poe = s["poe"]; tp = s["tp"]

    phase = contextlib.ExitStack()
    OT = phase.enter_context(nc.sbuf_tensor("at_OT", [128, KT * T], BF16))
    qkv = contextlib.ExitStack()
    QT = qkv.enter_context(nc.sbuf_tensor("at_QT", [128, KT * T], BF16))
    KTs = qkv.enter_context(nc.sbuf_tensor("at_KT", [128, KT * T], BF16))
    V = qkv.enter_context(nc.sbuf_tensor("at_V", [128, 16 * D], BF16))
    rext = qkv.enter_context(nc.sbuf_tensor("at_rext", [128, 4096], BF16))
    bvT = qkv.enter_context(nc.sbuf_tensor("at_bvT", [128, KT], F32))

    stream_in_dram = p.stream_in_dram
    mm_fence = mm.n
    b_ld = ld.n
    b_pg = pg.n
    b_pgd = pgd.n
    b_pj = pj.n
    b_mm = mm.n

    proj = contextlib.ExitStack()
    xb = proj.enter_context(nc.sbuf_tensor("at_xb", [128, KT * T], BF16))
    w_sb = proj.enter_context(nc.sbuf_tensor("at_wsb", [128, KT * D], BF16))
    bqkT = proj.enter_context(nc.sbuf_tensor("at_bqk", [128, 2 * KT], F32))
    ps_pr = [proj.enter_context(nc.psum_tensor(f"at_pspr{i}", [128, 512], F32)) for i in range(4)]

    def gp_xb(e):
        e.wait_ge(st.h, stream_in_dram)
        e.wait_ge(mm.h, mm_fence)
        e.wait_ge(ln.h, p.stream_ready)
        for k in range(KT):
            pgd.inc(e.dma_start(out=xb[:, k * T:(k + 1) * T], in_=t["xs_dram"][k * 128:(k + 1) * 128, :]), 16)
        for wi, wh in enumerate(["wqT_hbm", "wkT_hbm", "wvT_hbm"]):
            if wi > 0:
                e.wait_ge(mm.h, b_mm + 32 * wi)
            for kt in range(KT):
                pgd.inc(e.dma_start(out=w_sb[:, kt * D:(kt + 1) * D], in_=t[wh][kt * 128:(kt + 1) * 128, :]), 16)
        for hf in range(4):
            pgd.inc(e.dma_start(out=rext[0:64, hf * 1024:(hf + 1) * 1024],
                                in_=t["rextT_hbm"][:, hf * 1024:(hf + 1) * 1024]), 16)
            pgd.inc(e.dma_start(out=rext[64:128, hf * 1024:(hf + 1) * 1024],
                                in_=t["rextT_hbm"][:, hf * 1024:(hf + 1) * 1024]), 16)
    p.block.gpsimd(gp_xb)

    def sync_w(e):
        e.wait_ge(st.h, stream_in_dram)
        e.wait_ge(mm.h, mm_fence)
        for m in range(KT):
            ld.inc(e.dma_start(out=bqkT[:, m:m + 1], in_=t["bq_hbm"][m * 128:(m + 1) * 128].unsqueeze(1)), 16)
            ld.inc(e.dma_start(out=bqkT[:, KT + m:KT + m + 1], in_=t["bk_hbm"][m * 128:(m + 1) * 128].unsqueeze(1)), 16)
            ld.inc(e.dma_start(out=bvT[:, m:m + 1], in_=t["bv_hbm"][m * 128:(m + 1) * 128].unsqueeze(1)), 16)
    p.block.sync(sync_w)

    def tens_proj(e):
        e.wait_ge(pgd.h, b_pgd + 16 * KT)
        for wi in range(3):
            for gidx in range(32):
                g = wi * 32 + gidx
                e.wait_ge(pgd.h, b_pgd + 16 * (KT + KT * (wi + 1)))
                if g >= 4:
                    e.wait_ge(pj.h, b_pj + g - 3)
                pb = ps_pr[g % 4]
                ins = None
                if wi < 2:
                    mt, tc = divmod(gidx, NTC)
                    for k in range(KT):
                        ins = e.matmul(pb[:, :], w_sb[:, k * D + mt * 128:k * D + mt * 128 + 128],
                                       xb[:, k * T + tc * 512:k * T + tc * 512 + 512],
                                       start=(k == 0), stop=(k == KT - 1))
                else:
                    jt, fh = divmod(gidx, 2)
                    for k in range(KT):
                        ins = e.matmul(pb[:, :], xb[:, k * T + jt * 128:k * T + jt * 128 + 128],
                                       w_sb[:, k * D + fh * 512:k * D + fh * 512 + 512],
                                       start=(k == 0), stop=(k == KT - 1))
                mm.inc(ins, 1)
    p.block.tensor(tens_proj)

    def vec_proj(e):
        for wi in range(3):
            for gidx in range(32):
                g = wi * 32 + gidx
                e.wait_ge(mm.h, b_mm + g + 1)
                pb = ps_pr[g % 4]
                if wi < 2:
                    mt, tc = divmod(gidx, NTC)
                    dst = QT if wi == 0 else KTs
                    pj.inc(e.tensor_scalar(dst[:, mt * T + tc * 512:mt * T + tc * 512 + 512],
                                           pb[:, :], bqkT[:, wi * KT + mt:wi * KT + mt + 1], None, ALU.add), 1)
                else:
                    jt, fh = divmod(gidx, 2)
                    pj.inc(e.tensor_copy(V[:, jt * D + fh * 512:jt * D + fh * 512 + 512], pb[:, :]), 1)
    p.block.vector(vec_proj)
    proj_done = pj.n
    rext_done = b_pg + 28
    proj.close()

    hl = contextlib.ExitStack()
    pstage = [hl.enter_context(nc.sbuf_tensor(f"at_pst{i}", [128, 2176], BF16)) for i in range(2)]
    sk_sb = [hl.enter_context(nc.sbuf_tensor(f"at_sk{i}", [128, T], BF16)) for i in range(2)]
    S_row = [hl.enter_context(nc.sbuf_tensor(f"at_S{i}", [128, T], F32)) for i in range(2)]
    expS = [hl.enter_context(nc.sbuf_tensor(f"at_eS{i}", [128, T], BF16)) for i in range(2)]
    expST = [hl.enter_context(nc.sbuf_tensor(f"at_eST{i}", [128, T], BF16)) for i in range(2)]
    mxb = hl.enter_context(nc.sbuf_tensor("at_mx", [128, 8], F32))
    ps_a = [hl.enter_context(nc.psum_tensor(f"at_psa{i}", [128, 512], F32)) for i in range(4)]
    ps_t = [hl.enter_context(nc.psum_tensor(f"at_ptt{i}", [128, 512], BF16)) for i in range(2)]
    ps_o = [hl.enter_context(nc.psum_tensor(f"at_pso{i}", [64, 128], F32)) for i in range(2)]

    NSL = 256
    b_pp, b_pa, b_po, b_pi, b_pc, b_pS, b_pmx, b_pex, b_psc, b_ptr, b_pte, b_pav, b_poe = (
        x.n for x in (pp, pa, po, pi, pc, pS, pmx, pex, psc, ptr, pte, pav, poe))
    CW = [512, 512, 512, 512, 128]
    COF = [0, 512, 1024, 1536, 2048]

    def qslice(h, ic):
        return QT[64 * (h % 2):64 * (h % 2) + 64, (h // 2) * T + ic * 128:(h // 2) * T + ic * 128 + 128]

    def tens_hl(e):
        e.wait_ge(pj.h, proj_done)
        e.wait_ge(pgd.h, rext_done)
        e.wait_ge(s["init"].h, p.init_done)
        for oi in range(NSL + 4):
            if oi < NSL:
                h, ic = divmod(oi, 16)
                k0s = 1920 - ic * 128
                for c in range(5):
                    cn = 5 * oi + c
                    if cn >= 2:
                        e.wait_ge(pa.h, b_pa + cn - 1)
                    pp.inc(e.matmul(ps_a[cn % 2][:, 0:CW[c]], qslice(h, ic),
                                    rext[64 * (h % 2):64 * (h % 2) + 64, k0s + COF[c]:k0s + COF[c] + CW[c]],
                                    start=True, stop=True), 1)
            if 2 <= oi < NSL + 2:
                g1 = oi - 2
                h, ic = divmod(g1, 16)
                for jc in range(NTC):
                    cn = 4 * g1 + jc
                    if cn >= 2:
                        e.wait_ge(pS.h, b_pS + cn - 1)
                    pc.inc(e.matmul(ps_a[2 + cn % 2][:, :], qslice(h, ic),
                                    KTs[64 * (h % 2):64 * (h % 2) + 64,
                                        (h // 2) * T + jc * 512:(h // 2) * T + jc * 512 + 512],
                                    start=True, stop=True), 1)
            if 4 <= oi < NSL + 4:
                g3 = oi - 4
                h, ic = divmod(g3, 16)
                e.wait_ge(psc.h, b_psc + g3 + 1)
                for gq in range(4):
                    gqi = 4 * g3 + gq
                    if gqi >= 2:
                        e.wait_ge(pte.h, b_pte + gqi - 1)
                    ins = None
                    for j4 in range(4):
                        jt = gq * 4 + j4
                        ins = e.matmul(ps_t[gqi % 2][:, j4 * 128:(j4 + 1) * 128],
                                       expS[g3 % 2][:, jt * 128:(jt + 1) * 128],
                                       t["identb"][:, :], is_transpose=True,
                                       start=(j4 == 0), stop=(j4 == 3))
                    ptr.inc(ins, 1)
                if g3 >= 2:
                    e.wait_ge(poe.h, b_poe + g3 - 1)
                ins = None
                for jt in range(16):
                    if jt % 4 == 0:
                        e.wait_ge(pte.h, b_pte + 4 * g3 + jt // 4 + 1)
                    ins = e.matmul(ps_o[g3 % 2][0:64, :], V[:, jt * D + h * 64:jt * D + h * 64 + 64],
                                   expST[g3 % 2][:, jt * 128:(jt + 1) * 128],
                                   start=(jt == 0), stop=(jt == 15))
                pav.inc(ins, 1)
    p.block.tensor(tens_hl)

    def gp_hl(e):
        for oi in range(NSL + 1):
            if oi < NSL:
                h, ic = divmod(oi, 16)
                e.wait_ge(pa.h, b_pa + 5 * (oi + 1))
                po.inc(e.dma_start(out=t["pscr"][h % 2, ic * 128 * 2176:(ic + 1) * 128 * 2176]
                                   .rearrange("(a c) -> a c", a=128),
                                   in_=pstage[oi % 2][:, :]), 16)
            if oi >= 1:
                g1 = oi - 1
                h, ic = divmod(g1, 16)
                e.wait_ge(po.h, b_po + 16 * (g1 + 1))
                if g1 >= 2:
                    e.wait_ge(pS.h, b_pS + 4 * (g1 - 1))
                for jc in range(NTC):
                    sap = bass.AP(tensor=t["pscr_h"],
                                  offset=(h % 2) * 16 * 128 * 2176 + ic * 128 * 2176 + jc * 512 + 127,
                                  ap=[[2175, 128], [1, 512]])
                    pi.inc(e.dma_start(out=sk_sb[g1 % 2][:, jc * 512:(jc + 1) * 512], in_=sap), 16)
    p.block.gpsimd(gp_hl)

    def act_hl(e):
        for oi in range(NSL + 3):
            if oi < NSL:
                for c in range(5):
                    cn = 5 * oi + c
                    e.wait_ge(pp.h, b_pp + cn + 1)
                    if oi >= 2 and c == 0:
                        e.wait_ge(po.h, b_po + 16 * (oi - 1))
                    pa.inc(e.copy(pstage[oi % 2][:, COF[c]:COF[c] + CW[c]], ps_a[cn % 2][:, 0:CW[c]]), 1)
            if oi >= 3:
                g2 = oi - 3
                e.wait_ge(pS.h, b_pS + 4 * (g2 + 1))
                if g2 >= 2:
                    e.wait_ge(ptr.h, b_ptr + 4 * (g2 - 1))
                pex.inc(e.activation(out=expS[g2 % 2][:, :], in_=S_row[g2 % 2][:, :], func=AF.Exp,
                                     scale=0.125,
                                     accum_out=mxb[:, (g2 % 2) * 4 + 2:(g2 % 2) * 4 + 3]), 1)
    p.block.scalar(act_hl)

    def vec_hl(e):
        def st_(ins):
            tp.inc(ins, 1)
            e.wait_ge(tp.h, tp.n)
            return ins
        for oi in range(NSL + 4):
            if 2 <= oi < NSL + 2:
                g1 = oi - 2
                par = g1 % 2
                if g1 >= 2:
                    e.wait_ge(pex.h, b_pex + g1 - 1)
                for jc in range(NTC):
                    cn = 4 * g1 + jc
                    e.wait_ge(pc.h, b_pc + cn + 1)
                    e.wait_ge(pi.h, b_pi + 64 * g1 + 16 * (jc + 1))
                    pS.inc(e.tensor_tensor(S_row[par][:, jc * 512:(jc + 1) * 512],
                                           ps_a[2 + cn % 2][:, :],
                                           sk_sb[par][:, jc * 512:(jc + 1) * 512], ALU.add), 1)
            if 3 <= oi < NSL + 3:
                g2 = oi - 3
                par = g2 % 2
                e.wait_ge(pex.h, b_pex + g2 + 1)
                st_(e.reciprocal(mxb[:, par * 4 + 3:par * 4 + 4], mxb[:, par * 4 + 2:par * 4 + 3]))
                psc.inc(e.tensor_scalar(expS[par][:, :], expS[par][:, :],
                                        mxb[:, par * 4 + 3:par * 4 + 4], None, ALU.mult), 1)
            if oi >= 4:
                g3 = oi - 4
                h, ic = divmod(g3, 16)
                par = g3 % 2
                if g3 >= 2:
                    e.wait_ge(pav.h, b_pav + g3 - 1)
                for gq in range(4):
                    gqi = 4 * g3 + gq
                    e.wait_ge(ptr.h, b_ptr + gqi + 1)
                    pte.inc(e.tensor_copy(expST[par][:, gq * 512:(gq + 1) * 512], ps_t[gqi % 2][:, :]), 1)
                e.wait_ge(pav.h, b_pav + g3 + 1)
                poe.inc(e.tensor_scalar(OT[64 * (h % 2):64 * (h % 2) + 64,
                                           (h // 2) * T + ic * 128:(h // 2) * T + ic * 128 + 128],
                                        ps_o[par][0:64, :],
                                        bvT[64 * (h % 2):64 * (h % 2) + 64, h // 2:h // 2 + 1],
                                        None, ALU.add), 1)
    p.block.vector(vec_hl)
    heads_done_poe = poe.n
    heads_done_pav = pav.n
    hl.close()
    qkv.close()

    wo = contextlib.ExitStack()
    xs = phase.enter_context(nc.sbuf_tensor("at_xs", [128, KT * T], F32))
    wstg = [wo.enter_context(nc.sbuf_tensor(f"at_wstg{i}", [128, D], F32)) for i in range(2)]
    wo_sb = wo.enter_context(nc.sbuf_tensor("at_wo", [128, KT * D], BF16))
    boT = wo.enter_context(nc.sbuf_tensor("at_bo", [128, KT], F32))
    t1b = wo.enter_context(nc.sbuf_tensor("at_t1", [128, 512], F32))
    ps_w = [wo.enter_context(nc.psum_tensor(f"at_psw{i}", [128, 512], F32)) for i in range(4)]

    b_ld2 = ld.n
    b_pg2 = pg.n
    b_mm2 = mm.n
    b_ev2 = ev.n

    def sync_wo(e):
        e.wait_ge(poe.h, heads_done_poe)
        e.wait_ge(pav.h, heads_done_pav)
        for k in range(KT):
            ld.inc(e.dma_start(out=xs[:, k * T:(k + 1) * T], in_=t["xs_dram"][k * 128:(k + 1) * 128, :]), 16)
        for m in range(KT):
            ld.inc(e.dma_start(out=boT[:, m:m + 1], in_=t["bo_hbm"][m * 128:(m + 1) * 128].unsqueeze(1)), 16)
        for kt in range(KT):
            if kt >= 2:
                e.wait_ge(pg.h, b_pg2 + kt - 1)
            ld.inc(e.dma_start(out=wstg[kt % 2][:, :], in_=t["woT_hbm"][kt * 128:(kt + 1) * 128, :]), 16)
    p.block.sync(sync_wo)

    def gp_wo(e):
        for kt in range(KT):
            e.wait_ge(ld.h, b_ld2 + 16 * (16 + kt + 1))
            pg.inc(e.tensor_copy(wo_sb[:, kt * D:(kt + 1) * D], wstg[kt % 2][:, :]), 1)
    p.block.gpsimd(gp_wo)

    def tens_wo(e):
        e.wait_ge(pg.h, b_pg2 + KT)
        for g in range(32):
            tc, mt = divmod(g, KT)
            if g >= 4:
                e.wait_ge(ev.h, b_ev2 + g - 3)
            ins = None
            for k in range(KT):
                ins = e.matmul(ps_w[g % 4][:, :], wo_sb[:, k * D + mt * 128:k * D + mt * 128 + 128],
                               OT[:, k * T + tc * 512:k * T + tc * 512 + 512],
                               start=(k == 0), stop=(k == KT - 1))
            mm.inc(ins, 1)
    p.block.tensor(tens_wo)

    def vec_wo(e):
        e.wait_ge(ld.h, b_ld2 + 16 * 16)
        for g in range(32):
            tc, mt = divmod(g, KT)
            e.wait_ge(mm.h, b_mm2 + g + 1)
            sl = slice(mt * T + tc * 512, mt * T + tc * 512 + 512)
            e.tensor_scalar(t1b[:, :], ps_w[g % 4][:, :], boT[:, mt:mt + 1], None, ALU.add)
            ev.inc(e.tensor_tensor(xs[:, sl], xs[:, sl], t1b[:, :], ALU.add), 1)
    p.block.vector(vec_wo)
    y_ready = ev.n
    wo.close()

    _emit_ln(p, xs, y_ready, tag="ln3")
    p._attn_ret = (xs, phase)


def _emit_ffn(p):
    nc, s, t = p.nc, p.s, p.t
    ld, mm, ev, act, ln, st = s["ld"], s["mm"], s["ev"], s["act"], s["ln"], s["st"]
    sgu = s["sgu"]; sdn = s["sdn"]; sat = s["sat"]; hh = s["hh"]
    fev = s["ssl"]; swd = s["swd"]; pmx = s["pmx"]; pgd = s["pgd"]; pj = s["pj"]
    po = s["po"]
    TH = 512
    NF1 = 32

    phase = contextlib.ExitStack()
    xs = phase.enter_context(nc.sbuf_tensor("ffn_xs", [128, KT * T], F32))
    inner = contextlib.ExitStack()
    xb = inner.enter_context(nc.sbuf_tensor("ffn_xb", [128, KT * TH], BF16))
    h1T = inner.enter_context(nc.sbuf_tensor("ffn_h1", [128, NF1 * TH], BF16))
    h2T = inner.enter_context(nc.sbuf_tensor("ffn_h2", [128, NF1 * TH], BF16))
    k0c = [inner.enter_context(nc.sbuf_tensor(f"ffn_k0c{i}", [128, KT * 128], BF16)) for i in range(2)]
    kc = [inner.enter_context(nc.sbuf_tensor(f"ffn_kc{i}", [128, NF1 * 128], BF16)) for i in range(2)]
    s0a = [inner.enter_context(nc.sbuf_tensor(f"ffn_s0a{i}", [128, KT * 128], U8)) for i in range(2)]
    s0b = [inner.enter_context(nc.sbuf_tensor(f"ffn_s0b{i}", [128, KT * 128], U8)) for i in range(2)]
    s1g = [inner.enter_context(nc.sbuf_tensor(f"ffn_s1{i}", [128, NF1 * 128], U8)) for i in range(2)]
    rs = inner.enter_context(nc.sbuf_tensor("ffn_rs", [1, 2 * TH], F32))
    svec = inner.enter_context(nc.sbuf_tensor("ffn_svec", [128, 4], F32))
    t1b = inner.enter_context(nc.sbuf_tensor("ffn_t1", [128, 512], F32))
    ps1 = [inner.enter_context(nc.psum_tensor(f"ffn_ps{i}", [128, 512], F32)) for i in range(4)]
    ps_r = [inner.enter_context(nc.psum_tensor(f"ffn_psr{i}", [1, 512], F32)) for i in range(2)]
    ps_s = inner.enter_context(nc.psum_tensor("ffn_pss", [128, 4], F32))

    stream_in_dram = p.stream_in_dram
    mm_fence = mm.n
    b_ld = ld.n
    b_pgd = pgd.n
    b_po = po.n
    b_gu, b_dn, b_at, b_hh, b_fev, b_wd, b_pmx, b_mm, b_pj = (
        x.n for x in (sgu, sdn, sat, hh, fev, swd, pmx, mm, pj))

    def tens_s(e):
        e.wait_ge(st.h, stream_in_dram)
        e.wait_ge(mm.h, mm_fence)
        e.wait_ge(s["init"].h, p.init_done)
        mm.inc(e.matmul(ps_s[:, 0:3], t["ones_row"][0:1, :], t["s012_sb"][0:1, 0:3],
                        start=True, stop=True), 1)
    p.block.tensor(tens_s)

    def vec_s(e):
        tp = s["tp"]
        e.wait_ge(mm.h, b_mm + 1)
        tp.inc(e.tensor_scalar(svec[:, 0:1], ps_s[:, 0:1], 4.0, None, ALU.mult), 1)
        tp.inc(e.tensor_scalar(svec[:, 1:2], ps_s[:, 1:2], 4.0, None, ALU.mult), 1)
        tp.inc(e.tensor_scalar(svec[:, 2:3], ps_s[:, 2:3], 2.0, None, ALU.mult), 1)
        e.wait_ge(tp.h, tp.n)
        pj.inc(e.tensor_copy(svec[:, 3:4], svec[:, 0:1]), 1)
    p.block.vector(vec_s)
    svec_done = pj.n

    QO = [8, 8 + 104, 8 + 104 + 72, 8 + 104 + 144]

    def lda(q, idx):
        return b_ld + (QO[q] + idx) * 16 if q > 0 else b_ld + (8 + idx) * 16

    def swd_k0(i):
        return b_wd + i + 1

    def swd_k1(i):
        return b_wd + 32 + i + 1

    def swd_k2(i):
        return b_wd + 64 + i + 1

    C0 = 0
    C1 = 32 * 128 * KT * 128
    C2 = C1 + 32 * 128 * NF1 * 128

    def sync(e):
        e.wait_ge(st.h, stream_in_dram)
        e.wait_ge(mm.h, mm_fence)
        for k in range(KT):
            ld.inc(e.dma_start(out=xs[:, k * T:(k + 1) * T], in_=t["xs_dram"][k * 128:(k + 1) * 128, :]), 16)
        for i in range(NF1):
            if i >= 2:
                e.wait_ge(swd.h, swd_k0(i - 2))
            ld.inc(e.dma_start(out=s0a[i % 2][:, :].rearrange("p (k c) -> p k c", k=KT),
                               in_=t["k0a_hbm"].rearrange("(k p) f -> p k f", p=128)[:, :, i * 128:(i + 1) * 128]), 16)
            ld.inc(e.dma_start(out=s0b[i % 2][:, :].rearrange("p (k c) -> p k c", k=KT),
                               in_=t["k0b_hbm"].rearrange("(k p) f -> p k f", p=128)[:, :, i * 128:(i + 1) * 128]), 16)
        for i in range(NF1):
            if i >= 2:
                e.wait_ge(swd.h, swd_k1(i - 2))
            ld.inc(e.dma_start(out=s1g[i % 2][:, :].rearrange("p (k c) -> p k c", k=NF1),
                               in_=t["k1_hbm"].rearrange("(k p) f -> p k f", p=128)[:, :, i * 128:(i + 1) * 128]), 16)
        for i in range(KT):
            if i >= 2:
                e.wait_ge(swd.h, swd_k2(i - 2))
            else:
                e.wait_ge(swd.h, swd_k1(30 + i))
            ld.inc(e.dma_start(out=s1g[i % 2][:, :].rearrange("p (k c) -> p k c", k=NF1),
                               in_=t["k2_hbm"].rearrange("(k p) f -> p k f", p=128)[:, :, i * 128:(i + 1) * 128]), 16)
        for q in range(1, 4):
            for i in range(NF1):
                gi = q * NF1 + i
                e.wait_ge(po.h, b_po + 16 * (i + 1) if q == 1 else b_po)
                e.wait_ge(sgu.h, (b_gu + 32 * q + i - 1) if i >= 2 else (b_gu + 32 * q))
                ld.inc(e.dma_start(out=k0c[gi % 2][:, :],
                                   in_=t["wck"][C0 + i * 128 * KT * 128:C0 + (i + 1) * 128 * KT * 128]
                                   .rearrange("(a c) -> a c", a=128)), 16)
            for i in range(NF1):
                gi = q * NF1 + i
                e.wait_ge(po.h, b_po + 16 * (32 + i + 1) if q == 1 else b_po)
                if i >= 2:
                    e.wait_ge(sdn.h, b_dn + 32 * q + i - 1)
                else:
                    e.wait_ge(sat.h, b_at + 8 * q)
                ld.inc(e.dma_start(out=kc[gi % 2][:, :],
                                   in_=t["wck"][C1 + i * 128 * NF1 * 128:C1 + (i + 1) * 128 * NF1 * 128]
                                   .rearrange("(a c) -> a c", a=128)), 16)
            for i in range(KT):
                gi = q * KT + i
                e.wait_ge(po.h, b_po + 16 * (64 + i + 1) if q == 1 else b_po)
                if i >= 2:
                    e.wait_ge(sat.h, b_at + 8 * q + i - 1)
                else:
                    e.wait_ge(sdn.h, b_dn + 32 * (q + 1))
                ld.inc(e.dma_start(out=kc[gi % 2][:, :],
                                   in_=t["wck"][C2 + i * 128 * NF1 * 128:C2 + (i + 1) * 128 * NF1 * 128]
                                   .rearrange("(a c) -> a c", a=128)), 16)
    p.block.sync(sync)

    def gp(e):
        for q in range(4):
            toff = q * TH
            if q == 0:
                e.wait_ge(st.h, stream_in_dram)
                e.wait_ge(mm.h, mm_fence)
            else:
                e.wait_ge(sgu.h, b_gu + 32 * q)
            for k in range(KT):
                pgd.inc(e.dma_start(out=xb[:, k * TH:(k + 1) * TH],
                                    in_=t["xs_dram"][k * 128:(k + 1) * 128, toff:toff + TH]), 16)
            if q == 0:
                for i in range(NF1):
                    e.wait_ge(ld.h, lda(0, 2 * (i + 1)))
                    if i >= 2:
                        e.wait_ge(sgu.h, b_gu + i - 1)
                    swd.inc(e.tensor_tensor(k0c[i % 2][:, :], s0a[i % 2][:, :], s0b[i % 2][:, :], ALU.subtract), 1)
                    po.inc(e.dma_start(out=t["wck"][C0 + i * 128 * KT * 128:C0 + (i + 1) * 128 * KT * 128]
                                       .rearrange("(a c) -> a c", a=128), in_=k0c[i % 2][:, :]), 16)
                for i in range(NF1):
                    e.wait_ge(ld.h, lda(0, 64 + i + 1))
                    if i >= 2:
                        e.wait_ge(sdn.h, b_dn + i - 1)
                    if i < 2:
                        e.wait_ge(sat.h, b_at)
                    swd.inc(e.tensor_copy(kc[i % 2][:, :], s1g[i % 2][:, :]), 1)
                    po.inc(e.dma_start(out=t["wck"][C1 + i * 128 * NF1 * 128:C1 + (i + 1) * 128 * NF1 * 128]
                                       .rearrange("(a c) -> a c", a=128), in_=kc[i % 2][:, :]), 16)
                for i in range(KT):
                    e.wait_ge(ld.h, lda(0, 96 + i + 1))
                    if i < 2:
                        e.wait_ge(sdn.h, b_dn + 32)
                    else:
                        e.wait_ge(sat.h, b_at + i - 1)
                    swd.inc(e.tensor_copy(kc[i % 2][:, :], s1g[i % 2][:, :]), 1)
                    po.inc(e.dma_start(out=t["wck"][C2 + i * 128 * NF1 * 128:C2 + (i + 1) * 128 * NF1 * 128]
                                       .rearrange("(a c) -> a c", a=128), in_=kc[i % 2][:, :]), 16)
    p.block.gpsimd(gp)

    def tens(e):
        e.wait_ge(pj.h, svec_done)
        for q in range(4):
            e.wait_ge(pgd.h, b_pgd + 16 * KT * (q + 1))
            for i in range(NF1):
                if q == 0:
                    e.wait_ge(swd.h, swd_k0(i))
                    e.wait_ge(po.h, b_po + 16 * (i + 1))
                else:
                    e.wait_ge(ld.h, lda(q, i + 1))
                gg = 72 * q + i
                if gg >= 4:
                    e.wait_ge(fev.h, b_fev + gg - 3)
                ins = None
                for k in range(KT):
                    ins = e.matmul(ps1[gg % 4][:, :], k0c[(q * NF1 + i) % 2][:, k * 128:(k + 1) * 128],
                                   xb[:, k * TH:k * TH + TH],
                                   start=(k == 0), stop=(k == KT - 1))
                sgu.inc(ins, 1)
            e.wait_ge(fev.h, b_fev + 72 * q + 32)
            e.wait_ge(pmx.h, b_pmx + 2 * q)
            ins = None
            for k in range(NF1):
                ins = e.matmul(ps_r[0][0:1, :], t["ones_col_b"][:, 0:1],
                               h1T[:, k * TH:(k + 1) * TH], start=(k == 0), stop=(k == NF1 - 1))
            hh.inc(ins, 1)
            for i in range(NF1):
                if q == 0:
                    e.wait_ge(swd.h, swd_k1(i))
                else:
                    e.wait_ge(ld.h, lda(q, 32 + i + 1))
                gg = 72 * q + 32 + i
                if gg >= 4:
                    e.wait_ge(fev.h, b_fev + gg - 3)
                if i == 0:
                    e.wait_ge(pmx.h, b_pmx + 2 * q + 1)
                ins = None
                for k in range(NF1):
                    ins = e.matmul(ps1[gg % 4][:, :], kc[(q * NF1 + i) % 2][:, k * 128:(k + 1) * 128],
                                   h1T[:, k * TH:(k + 1) * TH], start=(k == 0), stop=False)
                ins = e.matmul(ps1[gg % 4][:, :], t["ones_row"][0:1, :],
                               rs[0:1, 0:TH], start=False, stop=True)
                sdn.inc(ins, 1)
            e.wait_ge(fev.h, b_fev + 72 * q + 64)
            e.wait_ge(pmx.h, b_pmx + 2 * q)
            ins = None
            for k in range(NF1):
                ins = e.matmul(ps_r[1][0:1, :], t["ones_col_b"][:, 0:1],
                               h2T[:, k * TH:(k + 1) * TH], start=(k == 0), stop=(k == NF1 - 1))
            hh.inc(ins, 1)
            for i in range(KT):
                if q == 0:
                    e.wait_ge(swd.h, swd_k2(i))
                else:
                    e.wait_ge(ld.h, lda(q, 64 + i + 1))
                gg = 72 * q + 64 + i
                if gg >= 4:
                    e.wait_ge(fev.h, b_fev + gg - 3)
                if i == 0:
                    e.wait_ge(pmx.h, b_pmx + 2 * q + 2)
                ins = None
                for k in range(NF1):
                    ins = e.matmul(ps1[gg % 4][:, :], kc[(q * KT + i) % 2][:, k * 128:(k + 1) * 128],
                                   h2T[:, k * TH:(k + 1) * TH], start=(k == 0), stop=False)
                ins = e.matmul(ps1[gg % 4][:, :], t["ones_row"][0:1, :],
                               rs[0:1, TH:2 * TH], start=False, stop=True)
                sat.inc(ins, 1)
    p.block.tensor(tens)

    def vec(e):
        e.wait_ge(ld.h, b_ld + 16 * KT)
        for q in range(4):
            for i in range(NF1):
                gg = 72 * q + i
                e.wait_ge(sgu.h, b_gu + 32 * q + i + 1)
                if q >= 1 and i == 0:
                    e.wait_ge(sdn.h, b_dn + 32 * q)
                    e.wait_ge(hh.h, b_hh + 2 * q - 1)
                fev.inc(e.tensor_scalar(h1T[:, i * TH:(i + 1) * TH],
                                        ps1[gg % 4][:, :], svec[:, 0:1], 0.0, ALU.mult, ALU.max), 1)
            e.wait_ge(hh.h, b_hh + 2 * q + 1)
            pmx.inc(e.tensor_scalar(rs[0:1, 0:TH], ps_r[0][0:1, :], -0.5, None, ALU.mult), 1)
            for i in range(NF1):
                gg = 72 * q + 32 + i
                e.wait_ge(sdn.h, b_dn + 32 * q + i + 1)
                if q >= 1 and i == 0:
                    e.wait_ge(sat.h, b_at + 8 * q)
                    e.wait_ge(hh.h, b_hh + 2 * q)
                fev.inc(e.tensor_scalar(h2T[:, i * TH:(i + 1) * TH],
                                        ps1[gg % 4][:, :], svec[:, 1:2], 0.0, ALU.mult, ALU.max), 1)
            e.wait_ge(hh.h, b_hh + 2 * q + 2)
            pmx.inc(e.tensor_scalar(rs[0:1, TH:2 * TH], ps_r[1][0:1, :], -0.5, None, ALU.mult), 1)
            for i in range(KT):
                gg = 72 * q + 64 + i
                e.wait_ge(sat.h, b_at + 8 * q + i + 1)
                e.tensor_scalar(t1b[:, :], ps1[gg % 4][:, :], svec[:, 2:3], None, ALU.mult)
                sl = slice(i * T + q * TH, i * T + q * TH + TH)
                fev.inc(e.tensor_tensor(xs[:, sl], xs[:, sl], t1b[:, :], ALU.add), 1)
    p.block.vector(vec)

    def vec_bridge(e):
        for k in range(KT):
            e.wait_ge(fev.h, b_fev + 72 * 3 + 64 + k + 1)
            ev.inc(e.tensor_copy(t["junk"][:, 0:1], t["junk"][:, 1:2]), 1)
    p.block.vector(vec_bridge)
    y_ready = ev.n
    inner.close()

    _emit_ln(p, xs, y_ready, tag="ln4")
    p._ffn_ret = (xs, phase)


def build_program(phases=("l0",)):
    nc = bass.Bass("TRN2", target_bir_lowering=False, debug=False)
    t = {}
    t["xT_hbm"] = nc.dram_tensor("xT", [D, T], F32, kind="ExternalInput").ap()
    t["ident_hbm"] = nc.dram_tensor("ident", [128, 128], F32, kind="ExternalInput").ap()
    t["s012_hbm"] = nc.dram_tensor("s012", [3], F32, kind="ExternalInput").ap()
    if "l0" in phases:
        t["wiT_hbm"] = nc.dram_tensor("wiT", [D, D], F32, kind="ExternalInput").ap()
        t["whT_hbm"] = nc.dram_tensor("whT", [D, D], F32, kind="ExternalInput").ap()
        t["bi_hbm"] = nc.dram_tensor("bi", [D], F32, kind="ExternalInput").ap()
        t["bh_hbm"] = nc.dram_tensor("bh", [D], F32, kind="ExternalInput").ap()
    if "moe" in phases:
        for e in range(8):
            t[f"wg{e}T_hbm"] = nc.dram_tensor(f"wg{e}T", [D, 2048], F32, kind="ExternalInput").ap()
            t[f"wu{e}T_hbm"] = nc.dram_tensor(f"wu{e}T", [D, 2048], F32, kind="ExternalInput").ap()
            t[f"wd{e}T_hbm"] = nc.dram_tensor(f"wd{e}T", [2048, D], F32, kind="ExternalInput").ap()
        t["wgsT_hbm"] = nc.dram_tensor("wgsT", [D, 2048], F32, kind="ExternalInput").ap()
        t["wusT_hbm"] = nc.dram_tensor("wusT", [D, 2048], F32, kind="ExternalInput").ap()
        t["wdsT_hbm"] = nc.dram_tensor("wdsT", [2048, D], F32, kind="ExternalInput").ap()
        t["wrT_hbm"] = nc.dram_tensor("wrT", [D, 8], F32, kind="ExternalInput").ap()
        t["rb_hbm"] = nc.dram_tensor("rb", [8], F32, kind="ExternalInput").ap()
    if "attn" in phases:
        for w in ["wq", "wk", "wv", "wo"]:
            t[w + "T_hbm"] = nc.dram_tensor(w + "T", [D, D], F32, kind="ExternalInput").ap()
        for b in ["bq", "bk", "bv", "bo"]:
            t[b + "_hbm"] = nc.dram_tensor(b, [D], F32, kind="ExternalInput").ap()
        t["rextT_hbm"] = nc.dram_tensor("rextT", [64, 4096], F32, kind="ExternalInput").ap()
        t["pscr_h"] = nc.dram_tensor("pscr", [2, 16 * 128 * 2176], BF16, kind="Internal")
        t["pscr"] = t["pscr_h"].ap()
    if "ffn" in phases:
        t["k0a_hbm"] = nc.dram_tensor("k0a", [D, 4096], U8, kind="ExternalInput").ap()
        t["k0b_hbm"] = nc.dram_tensor("k0b", [D, 4096], U8, kind="ExternalInput").ap()
        t["k1_hbm"] = nc.dram_tensor("k1", [4096, 4096], U8, kind="ExternalInput").ap()
        t["k2_hbm"] = nc.dram_tensor("k2", [4096, D], U8, kind="ExternalInput").ap()
        t["wck"] = nc.dram_tensor("wck", [32 * 128 * KT * 128 + 40 * 128 * 32 * 128], BF16, kind="Internal").ap()
    t["outT_hbm"] = nc.dram_tensor("outT", [D, T], F32, kind="ExternalOutput").ap()
    t["xs_dram"] = nc.dram_tensor("xs_dram", [D, T], F32, kind="Internal").ap()

    with contextlib.ExitStack() as es:
        t["ident"] = es.enter_context(nc.sbuf_tensor("identsb", [128, 128], F32))
        t["identb"] = es.enter_context(nc.sbuf_tensor("identbsb", [128, 128], BF16))
        t["ones_col"] = es.enter_context(nc.sbuf_tensor("ones_col", [128, 1], F32))
        t["ones_col_b"] = es.enter_context(nc.sbuf_tensor("ones_col_b", [128, 1], BF16))
        t["ones_row"] = es.enter_context(nc.sbuf_tensor("ones_row", [1, 128], F32))
        t["ones_row_b"] = es.enter_context(nc.sbuf_tensor("ones_row_b", [1, 128], BF16))
        t["s012_sb"] = es.enter_context(nc.sbuf_tensor("s012_sb", [1, 3], F32))
        t["junk"] = es.enter_context(nc.sbuf_tensor("junk", [128, 2], BF16))

        sems = {}
        for name in ["ld", "mm", "ev", "act", "hh", "init", "ln", "st",
                     "sgu", "sdn", "ssl", "sat", "sde", "swu", "swd", "scw", "tp",
                     "pj", "pg", "pgd", "pp", "pa", "po", "pi", "pc",
                     "pS", "pmx", "pex", "psc", "ptr", "pte", "pav", "poe"]:
            sems[name] = Sem(es.enter_context(nc.semaphore(name)))
        block = es.enter_context(nc.Block())
        p = P(nc, block, es, sems, t)

        _emit_init(p)
        p.stream_in_dram = 0
        last = phases[-1]

        def store(xs, scope, is_last):
            st = sems["st"]
            stream_ready = p.stream_ready
            st0 = st.n
            dst = t["outT_hbm"] if is_last else t["xs_dram"]

            def gp(e):
                e.wait_ge(sems["ln"].h, stream_ready)
                for k in range(KT):
                    st.inc(e.dma_start(out=dst[k * 128:(k + 1) * 128, :], in_=xs[:, k * T:(k + 1) * T]), 16)
                if is_last:
                    e.wait_ge(st.h, st0 + 16 * KT)
            p.block.gpsimd(gp)
            p.stream_in_dram = st.n
            scope.close()

        if "l0" in phases:
            _emit_l0(p, t["xT_hbm"])
            store(*p._l0_ret, last == "l0")
        if "moe" in phases:
            _emit_moe(p)
            store(*p._moe_ret, last == "moe")
        if "attn" in phases:
            _emit_attn(p)
            store(*p._attn_ret, last == "attn")
        if "ffn" in phases:
            _emit_ffn(p)
            store(*p._ffn_ret, last == "ffn")
    return nc


def _host_inputs_per_core(inputs, b, phases):
    x = np.asarray(inputs["L0_attn_x"])[b]
    m = {
        "xT": np.ascontiguousarray(x.T),
        "ident": np.eye(128, dtype=np.float32),
        "s012": np.stack([np.float32(inputs["L1_ffn_s0"]), np.float32(inputs["L1_ffn_s1"]),
                          np.float32(inputs["L1_ffn_s2"])]).astype(np.float32),
    }
    if "l0" in phases:
        m["wiT"] = np.ascontiguousarray(np.asarray(inputs["L0_attn_Wi"]).T)
        m["whT"] = np.ascontiguousarray(np.asarray(inputs["L0_attn_Wh"]).T)
        m["bi"] = np.asarray(inputs["L0_attn_bi"])
        m["bh"] = np.asarray(inputs["L0_attn_bh"])
    if "moe" in phases:
        for e in range(8):
            m[f"wg{e}T"] = np.ascontiguousarray(np.asarray(inputs["L0_ffn_Wg"])[e].T)
            m[f"wu{e}T"] = np.ascontiguousarray(np.asarray(inputs["L0_ffn_Wu"])[e].T)
            m[f"wd{e}T"] = np.ascontiguousarray(np.asarray(inputs["L0_ffn_Wd"])[e].T)
        m["wgsT"] = np.ascontiguousarray(np.asarray(inputs["L0_ffn_Wg_s"]).T)
        m["wusT"] = np.ascontiguousarray(np.asarray(inputs["L0_ffn_Wu_s"]).T)
        m["wdsT"] = np.ascontiguousarray(np.asarray(inputs["L0_ffn_Wd_s"]).T)
        m["wrT"] = np.ascontiguousarray(np.asarray(inputs["L0_ffn_Wr"]).T)
        m["rb"] = np.asarray(inputs["L0_ffn_rb"])
    if "attn" in phases:
        m["wqT"] = np.ascontiguousarray(np.asarray(inputs["L1_attn_Wq"]).T)
        m["wkT"] = np.ascontiguousarray(np.asarray(inputs["L1_attn_Wk"]).T)
        m["wvT"] = np.ascontiguousarray(np.asarray(inputs["L1_attn_Wv"]).T)
        m["woT"] = np.ascontiguousarray(np.asarray(inputs["L1_attn_Wo"]).T)
        m["bq"] = np.asarray(inputs["L1_attn_bq"])
        m["bk"] = np.asarray(inputs["L1_attn_bk"])
        m["bv"] = np.asarray(inputs["L1_attn_bv"])
        m["bo"] = np.asarray(inputs["L1_attn_bo"])
        rt = np.asarray(inputs["L1_attn_rel_table"])
        rext = rt[np.clip(3070 - np.arange(4096), 0, 2046)]
        m["rextT"] = np.ascontiguousarray(rext.T)
    if "ffn" in phases:
        k0 = np.asarray(inputs["L1_ffn_k0"])
        m["k0a"] = np.ascontiguousarray(k0[:D]).view(np.uint8)
        m["k0b"] = np.ascontiguousarray(k0[D:]).view(np.uint8)
        m["k1"] = np.ascontiguousarray(np.asarray(inputs["L1_ffn_k1"])).view(np.uint8)
        m["k2"] = np.ascontiguousarray(np.asarray(inputs["L1_ffn_k2"])).view(np.uint8)
    return m


def kernel(**inputs):
    phases = ("l0", "moe", "attn", "ffn")
    nc = build_program(phases)
    in_maps = [_host_inputs_per_core(inputs, b, phases) for b in range(8)]
    res = run_bass_kernel_spmd(nc, in_maps, core_ids=list(range(8)))
    return np.stack([r["outT"].T for r in res.results], axis=0)


# version 09
# speedup vs baseline: 1.2527x; 1.0414x over previous
import contextlib
import os
import numpy as np
import concourse.bass as bass
import concourse.mybir as mybir
from concourse.bass_utils import run_bass_kernel_spmd

F32 = mybir.dt.float32
BF16 = mybir.dt.bfloat16
U8 = mybir.dt.uint8
AF = mybir.ActivationFunctionType
ALU = mybir.AluOpType

D = 1024
T = 2048
KT = D // 128
NTC = T // 512
LN_EPS = 1e-5


class Sem:
    def __init__(self, handle):
        self.h = handle
        self.n = 0

    def inc(self, instr, k=1):
        instr.then_inc(self.h, k)
        self.n += k
        return instr


class P:
    def __init__(self, nc, block, es, sems, tensors):
        self.nc = nc
        self.block = block
        self.es = es
        self.s = sems
        self.t = tensors
        self.init_done = 0
        self.stream_ready = 0


def _emit_init(p):
    nc, s, t = p.nc, p.s, p.t
    ld, init = s["ld"], s["init"]
    ld0 = ld.n

    def sync(e):
        ld.inc(e.dma_start(out=t["ident"][:], in_=t["ident_hbm"][:, :]), 16)
        ld.inc(e.dma_start(out=t["s012_sb"][0:1, :], in_=t["s012_hbm"].unsqueeze(0)), 16)
    p.block.sync(sync)

    def vec(e):
        init.inc(e.memset(t["ones_col"][:], 1.0), 1)
        init.inc(e.memset(t["ones_col_b"][:], 1.0), 1)
        init.inc(e.memset(t["ones_row"][:], 1.0), 1)
        init.inc(e.memset(t["ones_row_b"][:], 1.0), 1)
        e.wait_ge(ld.h, ld0 + 32)
        init.inc(e.tensor_copy(t["identb"][:], t["ident"][:]), 1)
    p.block.vector(vec)
    p.init_done = init.n


def _emit_l0(p, stream_src):
    nc, s, t = p.nc, p.s, p.t
    ld, mm, ev, act, hh = s["ld"], s["mm"], s["ev"], s["act"], s["hh"]

    phase = contextlib.ExitStack()
    xs = phase.enter_context(nc.sbuf_tensor("l0_xs", [128, KT * T], F32))
    xib_scope = contextlib.ExitStack()
    xibT = xib_scope.enter_context(nc.sbuf_tensor("xibT", [128, KT * T], F32))
    inner = contextlib.ExitStack()
    xb = inner.enter_context(nc.sbuf_tensor("l0_xb", [128, KT * T], BF16))
    stag = inner.enter_context(nc.sbuf_tensor("l0_stag", [128, 2 * D], F32))
    wiT_sb = inner.enter_context(nc.sbuf_tensor("wiT_sb", [128, KT * D], BF16))
    bihT = inner.enter_context(nc.sbuf_tensor("bihT", [128, 2 * KT], F32))
    ps_x = [inner.enter_context(nc.psum_tensor(f"l0ps{i}", [128, 512], F32)) for i in range(4)]

    ld0 = ld.n
    ev0 = ev.n

    def sync(e):
        for k in range(KT):
            ld.inc(e.dma_start(out=xs[:, k * T:(k + 1) * T], in_=stream_src[k * 128:(k + 1) * 128, :]), 16)
        for m in range(KT):
            ld.inc(e.dma_start(out=bihT[:, m:m + 1], in_=t["bi_hbm"][m * 128:(m + 1) * 128].unsqueeze(1)), 16)
            ld.inc(e.dma_start(out=bihT[:, KT + m:KT + m + 1], in_=t["bh_hbm"][m * 128:(m + 1) * 128].unsqueeze(1)), 16)
        for q in range(4):
            if q > 0:
                e.wait_ge(ev.h, ev0 + 1 + q)
            ld.inc(e.dma_start(out=stag[:, :].rearrange("p (k d) -> p k d", k=2),
                               in_=t["wiT_hbm"].rearrange("(k p) d -> p k d", p=128)[:, 2 * q:2 * q + 2, :]), 16)
    p.block.sync(sync)
    after_stream = ld0 + 16 * KT
    after_bias = after_stream + 16 * 2 * KT

    def vec(e):
        e.wait_ge(s["init"].h, p.init_done)
        e.wait_ge(ld.h, after_stream)
        for k in range(KT):
            e.tensor_copy(xb[:, k * T:(k + 1) * T], xs[:, k * T:(k + 1) * T])
        e.wait_ge(ld.h, after_bias)
        ev.inc(e.tensor_tensor(bihT[:, 0:KT], bihT[:, 0:KT], bihT[:, KT:2 * KT], ALU.add), 1)
        for q in range(4):
            e.wait_ge(ld.h, after_bias + 16 * (q + 1))
            ev.inc(e.tensor_copy(wiT_sb[:, q * 2 * D:(q + 1) * 2 * D], stag[:, :]), 1)
    p.block.vector(vec)
    wi_ready = ev.n

    mm0 = mm.n
    ev_xi0 = ev.n

    def tens(e):
        e.wait_ge(ev.h, wi_ready)
        for idx in range(KT * NTC):
            mt, tc = divmod(idx, NTC)
            if idx >= 4:
                e.wait_ge(ev.h, ev_xi0 + (idx - 4) + 1)
            pb = ps_x[idx % 4]
            ins = None
            for k in range(KT):
                ins = e.matmul(pb[:, :],
                               wiT_sb[:, k * D + mt * 128: k * D + mt * 128 + 128],
                               xb[:, k * T + tc * 512: k * T + tc * 512 + 512],
                               start=(k == 0), stop=(k == KT - 1))
            mm.inc(ins, 1)
    p.block.tensor(tens)

    def vec2(e):
        for idx in range(KT * NTC):
            mt, tc = divmod(idx, NTC)
            e.wait_ge(mm.h, mm0 + idx + 1)
            pb = ps_x[idx % 4]
            ev.inc(e.tensor_scalar(xibT[:, mt * T + tc * 512: mt * T + tc * 512 + 512],
                                   pb[:, :], bihT[:, mt:mt + 1], None, ALU.add), 1)
    p.block.vector(vec2)
    xib_done = ev.n
    mm_xi_done = mm.n
    inner.close()

    if os.environ.get("MK_DBG") == "xib":
        ln = s["ln"]
        def vend(e):
            e.wait_ge(ev.h, xib_done)
            for k in range(KT):
                e.tensor_copy(xs[:, k * T:(k + 1) * T], xibT[:, k * T:(k + 1) * T])
            ln.inc(e.tensor_copy(t["junk"][:, 0:1], t["junk"][:, 1:2]), 1)
        p.block.vector(vend)
        p.stream_ready = ln.n
        p._l0_ret = (xs, phase)
        return

    inner2 = contextlib.ExitStack()
    whT_sb = inner2.enter_context(nc.sbuf_tensor("whT_sb", [128, KT * D], BF16))
    stag_scope = contextlib.ExitStack()
    stag2 = stag_scope.enter_context(nc.sbuf_tensor("l0_stag2", [128, 2 * D], F32))

    ld_wh0 = ld.n
    ev_wh0 = ev.n

    def sync2(e):
        e.wait_ge(mm.h, mm_xi_done)
        for q in range(4):
            if q > 0:
                e.wait_ge(ev.h, ev_wh0 + q)
            ld.inc(e.dma_start(out=stag2[:, :].rearrange("p (k d) -> p k d", k=2),
                               in_=t["whT_hbm"].rearrange("(k p) d -> p k d", p=128)[:, 2 * q:2 * q + 2, :]), 16)
    p.block.sync(sync2)

    def vec3(e):
        for q in range(4):
            e.wait_ge(ld.h, ld_wh0 + 16 * (q + 1))
            ev.inc(e.tensor_copy(whT_sb[:, q * 2 * D:(q + 1) * 2 * D], stag2[:, :]), 1)
    p.block.vector(vec3)
    wh_ready = ev.n
    stag_scope.close()
    hsT = inner2.enter_context(nc.sbuf_tensor("hsT", [128, KT * (T + 1)], BF16))
    ps_h = [inner2.enter_context(nc.psum_tensor(f"rnnps{m}", [128, 512], F32)) for m in range(KT)]

    hh0 = hh.n

    def vec4(e):
        for m in range(KT):
            hh.inc(e.memset(hsT[:, m * (T + 1):m * (T + 1) + 1], 0.0), 1)
    p.block.vector(vec4)

    mm_rnn0 = mm.n

    def tens2(e):
        e.wait_ge(ev.h, wh_ready)
        for tt in range(1, T + 1):
            e.wait_ge(hh.h, hh0 + KT * tt)
            for m in range(KT):
                ins = None
                for k in range(KT):
                    ins = e.matmul(ps_h[m][:, 0:1],
                                   whT_sb[:, k * D + m * 128:k * D + m * 128 + 128],
                                   hsT[:, k * (T + 1) + (tt - 1):k * (T + 1) + tt],
                                   start=(k == 0), stop=(k == KT - 1))
                mm.inc(ins, 1)
    p.block.tensor(tens2)

    def scal(e):
        for tt in range(1, T + 1):
            for m in range(KT):
                e.wait_ge(mm.h, mm_rnn0 + KT * (tt - 1) + m + 1)
                hh.inc(e.activation(out=hsT[:, m * (T + 1) + tt:m * (T + 1) + tt + 1],
                                    in_=ps_h[m][:, 0:1], func=AF.Tanh,
                                    bias=xibT[:, m * T + (tt - 1):m * T + tt], scale=1.0), 1)
    p.block.scalar(scal)
    rnn_done = hh.n

    if os.environ.get("MK_DBG") == "whs":
        ln = s["ln"]
        def vwhs(e):
            e.wait_ge(ev.h, wh_ready)
            for q in range(4):
                e.tensor_copy(xs[:, q * 2 * D:(q + 1) * 2 * D], whT_sb[:, q * 2 * D:(q + 1) * 2 * D])
            e.wait_ge(hh.h, hh0 + KT)
            for m in range(KT):
                e.tensor_copy(xs[:, 4 * 2 * D + m: 4 * 2 * D + m + 1], hsT[:, m * (T + 1):m * (T + 1) + 1])
            ln.inc(e.tensor_copy(t["junk"][:, 0:1], t["junk"][:, 1:2]), 1)
        p.block.vector(vwhs)
        p.stream_ready = ln.n
        inner2.close()
        p._l0_ret = (xs, phase)
        return

    if os.environ.get("MK_DBG") == "rnn":
        ln = s["ln"]
        def vrnn(e):
            e.wait_ge(hh.h, rnn_done)
            for k in range(KT):
                e.tensor_copy(xs[:, k * T:(k + 1) * T], hsT[:, k * (T + 1) + 1:k * (T + 1) + T + 1])
            ln.inc(e.tensor_copy(t["junk"][:, 0:1], t["junk"][:, 1:2]), 1)
        p.block.vector(vrnn)
        p.stream_ready = ln.n
        inner2.close()
        p._l0_ret = (xs, phase)
        return

    def vec5(e):
        e.wait_ge(hh.h, rnn_done)
        for k in range(KT):
            ev.inc(e.tensor_tensor(xs[:, k * T:(k + 1) * T], xs[:, k * T:(k + 1) * T],
                                   hsT[:, k * (T + 1) + 1:k * (T + 1) + T + 1], ALU.add), 1)
    p.block.vector(vec5)
    y_ready = ev.n
    inner2.close()
    xib_scope.close()
    p._l0_ret = (xs, phase)

    if os.environ.get("MK_DBG") == "noln":
        ln = s["ln"]
        def vend(e):
            e.wait_ge(ev.h, y_ready)
            ln.inc(e.tensor_copy(t["junk"][:, 0:1], t["junk"][:, 1:2]), 1)
        p.block.vector(vend)
        p.stream_ready = ln.n
    else:
        _emit_ln(p, xs, y_ready, tag="ln1")


def _emit_ln(p, xs, ev_y_ready, tag):
    nc, s, t = p.nc, p.s, p.t
    mm, ev, act, ln = s["mm"], s["ev"], s["act"], s["ln"]

    inner = contextlib.ExitStack()
    ysq = inner.enter_context(nc.sbuf_tensor(f"{tag}_ysq", [128, T], F32))
    rows = inner.enter_context(nc.sbuf_tensor(f"{tag}_rows", [1, 4 * T], F32))
    MU, VB, RA, TMP = (i * T for i in range(4))

    act0 = act.n
    mm0 = mm.n

    stats = contextlib.ExitStack()
    ps_s = stats.enter_context(nc.psum_tensor(f"{tag}_ps_s", [1, T], F32))
    ps_q = stats.enter_context(nc.psum_tensor(f"{tag}_ps_q", [1, T], F32))

    def scal(e):
        for k in range(KT):
            e.wait_ge(ev.h, ev_y_ready - KT + k + 1)
            if k >= 1:
                e.wait_ge(mm.h, mm0 + k * NTC)
            act.inc(e.activation(out=ysq[:, :], in_=xs[:, k * T:(k + 1) * T], func=AF.Square), 1)
    p.block.scalar(scal)

    def tens(e):
        e.wait_ge(s["init"].h, p.init_done)
        e.wait_ge(ev.h, ev_y_ready)
        for k in range(KT):
            e.wait_ge(act.h, act0 + k + 1)
            for tc in range(NTC):
                e.matmul(ps_s[0:1, tc * 512:(tc + 1) * 512], t["ones_col"][:, 0:1],
                         xs[:, k * T + tc * 512: k * T + tc * 512 + 512],
                         start=(k == 0), stop=(k == KT - 1))
                mm.inc(e.matmul(ps_q[0:1, tc * 512:(tc + 1) * 512], t["ones_col"][:, 0:1],
                                ysq[:, tc * 512:(tc + 1) * 512],
                                start=(k == 0), stop=(k == KT - 1)), 1)
    p.block.tensor(tens)
    mm_stats = mm.n

    ev_v0 = ev.n

    def vec(e):
        e.wait_ge(mm.h, mm_stats)
        e.tensor_scalar(rows[0:1, MU:MU + T], ps_s[0:1, :], 1.0 / D, None, ALU.mult)
        e.tensor_scalar(rows[0:1, TMP:TMP + T], ps_q[0:1, :], 1.0 / D, None, ALU.mult)
        e.tensor_tensor(rows[0:1, RA:RA + T], rows[0:1, MU:MU + T], rows[0:1, MU:MU + T], ALU.mult)
        ev.inc(e.tensor_tensor(rows[0:1, VB:VB + T], rows[0:1, TMP:TMP + T], rows[0:1, RA:RA + T], ALU.subtract), 1)
    p.block.vector(vec)
    ev_var = ev.n

    def vec1b(e):
        e.wait_ge(ev.h, ev_var)
        ev.inc(e.tensor_scalar(rows[0:1, VB:VB + T], rows[0:1, VB:VB + T], LN_EPS, None, ALU.add), 1)
    p.block.vector(vec1b)
    ev_vareps = ev.n

    def scal2(e):
        e.wait_ge(ev.h, ev_vareps)
        e.activation(out=rows[0:1, TMP:TMP + T], in_=rows[0:1, VB:VB + T], func=AF.Ln, scale=1.0)
        act.inc(e.activation(out=rows[0:1, RA:RA + T], in_=rows[0:1, TMP:TMP + T], func=AF.Exp, scale=-0.5), 1)
    p.block.scalar(scal2)
    act_rstd = act.n

    def vec2(e):
        e.wait_ge(act.h, act_rstd)
        e.tensor_tensor(rows[0:1, TMP:TMP + T], rows[0:1, MU:MU + T], rows[0:1, RA:RA + T], ALU.mult)
        ev.inc(e.tensor_scalar(rows[0:1, VB:VB + T], rows[0:1, TMP:TMP + T], -1.0, None, ALU.mult), 1)
    p.block.vector(vec2)
    ev_rows = ev.n
    stats.close()

    bc = contextlib.ExitStack()
    ps_bc = [bc.enter_context(nc.psum_tensor(f"{tag}_bc{i}", [128, 512], F32)) for i in range(2)]
    mm_bc0 = mm.n

    def tens2(e):
        e.wait_ge(ev.h, ev_rows)
        for tc in range(NTC):
            if tc >= 1:
                e.wait_ge(ev.h, ev_rows + tc * KT)
            e.matmul(ps_bc[0][:, :], t["ones_row"][0:1, :], rows[0:1, RA + tc * 512:RA + tc * 512 + 512],
                     start=True, stop=True)
            mm.inc(e.matmul(ps_bc[1][:, :], t["ones_row"][0:1, :], rows[0:1, VB + tc * 512:VB + tc * 512 + 512],
                            start=True, stop=True), 1)
    p.block.tensor(tens2)

    def vec3(e):
        for tc in range(NTC):
            e.wait_ge(mm.h, mm_bc0 + tc + 1)
            for k in range(KT):
                sl = slice(k * T + tc * 512, k * T + tc * 512 + 512)
                e.tensor_tensor(xs[:, sl], xs[:, sl], ps_bc[0][:, :], ALU.mult)
                ev.inc(e.tensor_tensor(xs[:, sl], xs[:, sl], ps_bc[1][:, :], ALU.add), 1)
        ln.inc(e.tensor_copy(t["junk"][:, 0:1], t["junk"][:, 1:2]), 1)
    p.block.vector(vec3)
    p.stream_ready = ln.n
    bc.close()
    inner.close()


def _emit_moe(p):
    nc, s, t = p.nc, p.s, p.t
    ld, mm, ev, act, ln = s["ld"], s["mm"], s["ev"], s["act"], s["ln"]
    IC = 16
    TH = 1024
    NE = 9

    phase = contextlib.ExitStack()
    xs = phase.enter_context(nc.sbuf_tensor("moe_xs", [128, KT * T], F32))
    inner = contextlib.ExitStack()
    xb = inner.enter_context(nc.sbuf_tensor("moe_xb", [128, KT * TH], BF16))
    aT = inner.enter_context(nc.sbuf_tensor("moe_aT", [128, IC * TH], BF16))
    wd_sb = inner.enter_context(nc.sbuf_tensor("moe_wd", [128, IC * D], BF16))
    wg_sb = [inner.enter_context(nc.sbuf_tensor(f"moe_wg{i}", [128, D], BF16)) for i in range(2)]
    wu_sb = [inner.enter_context(nc.sbuf_tensor(f"moe_wu{i}", [128, D], BF16)) for i in range(2)]
    cw_tm = inner.enter_context(nc.sbuf_tensor("moe_cwtm", [128, 2 * 64], F32))
    cw_row = inner.enter_context(nc.sbuf_tensor("moe_cwrow", [1, TH], BF16))
    cw_b = inner.enter_context(nc.sbuf_tensor("moe_cwb", [128, TH], BF16))
    g_sb = [inner.enter_context(nc.sbuf_tensor(f"moe_g{i}", [128, 512], BF16)) for i in range(2)]
    u_sb = [inner.enter_context(nc.sbuf_tensor(f"moe_u{i}", [128, 512], BF16)) for i in range(2)]
    wr_sb = inner.enter_context(nc.sbuf_tensor("moe_wr", [128, KT * 8], F32))
    rb_sb = inner.enter_context(nc.sbuf_tensor("moe_rb", [8, 1], F32))
    m1 = inner.enter_context(nc.sbuf_tensor("moe_m1", [128, 2], F32))
    msk = inner.enter_context(nc.sbuf_tensor("moe_msk", [128, 24], F32))
    probsT = inner.enter_context(nc.sbuf_tensor("moe_probs", [8, TH], F32))

    ps_g = [inner.enter_context(nc.psum_tensor(f"moe_psg{i}", [128, 512], F32)) for i in range(2)]
    ps_u = [inner.enter_context(nc.psum_tensor(f"moe_psu{i}", [128, 512], F32)) for i in range(2)]
    ps_d = [inner.enter_context(nc.psum_tensor(f"moe_psd{i}", [128, 512], F32)) for i in range(4)]

    stream_ready = p.stream_ready
    mm_fence = mm.n

    wgs = [t["wgsT_hbm"]] + [t[f"wg{e}T_hbm"] for e in range(8)]
    wus = [t["wusT_hbm"]] + [t[f"wu{e}T_hbm"] for e in range(8)]
    wds = [t["wdsT_hbm"]] + [t[f"wd{e}T_hbm"] for e in range(8)]

    sgu = s["sgu"]; sdn = s["sdn"]; ssl = s["ssl"]; sat = s["sat"]
    sde = s["sde"]; swu = s["swu"]; swd = s["swd"]; scw = s["scw"]
    b_gu, b_dn, b_sl, b_at, b_de, b_wu, b_wd, b_cw = (x.n for x in (sgu, sdn, ssl, sat, sde, swu, swd, scw))
    b_ld = ld.n
    b_ev = ev.n
    b_act = act.n

    st = s["st"]
    pgd = s["pgd"]
    b_pgd = pgd.n
    stream_in_dram = p.stream_in_dram

    def sync(e):
        e.wait_ge(st.h, stream_in_dram)
        e.wait_ge(ln.h, stream_ready)
        e.wait_ge(mm.h, mm_fence)
        for k in range(KT):
            ld.inc(e.dma_start(out=xs[:, k * T:(k + 1) * T], in_=t["xs_dram"][k * 128:(k + 1) * 128, :]), 16)
        for k in range(KT):
            ld.inc(e.dma_start(out=wr_sb[:, k * 8:(k + 1) * 8], in_=t["wrT_hbm"][k * 128:(k + 1) * 128, :]), 16)
        ld.inc(e.dma_start(out=rb_sb[0:8, 0:1], in_=t["rb_hbm"].unsqueeze(1)), 16)
    p.block.sync(sync)

    def gp_w(e):
        e.wait_ge(st.h, stream_in_dram)
        e.wait_ge(mm.h, mm_fence)
        for th in range(2):
            for j in range(NE):
                J = th * NE + j
                for ic in range(IC):
                    gi = J * IC + ic
                    if gi >= 2:
                        e.wait_ge(sgu.h, b_gu + 2 * (gi - 1))
                    pgd.inc(e.dma_start(out=wg_sb[gi % 2][:, :].rearrange("p (k c) -> p k c", k=KT),
                                        in_=wgs[j].rearrange("(k p) i -> p k i", p=128)[:, :, ic * 128:(ic + 1) * 128]), 16)
                    pgd.inc(e.dma_start(out=wu_sb[gi % 2][:, :].rearrange("p (k c) -> p k c", k=KT),
                                        in_=wus[j].rearrange("(k p) i -> p k i", p=128)[:, :, ic * 128:(ic + 1) * 128]), 16)
                for ic in range(IC):
                    if ic == 0:
                        e.wait_ge(sdn.h, b_dn + 16 * J)
                    pgd.inc(e.dma_start(out=wd_sb[:, ic * D:(ic + 1) * D], in_=wds[j][ic * 128:(ic + 1) * 128, :]), 16)
    p.block.gpsimd(gp_w)

    def pgd_wgu_done(gi):
        J, ic = divmod(gi, IC)
        return b_pgd + (J * 48 + 2 * (ic + 1)) * 16

    def pgd_wd_done(J):
        return b_pgd + (J * 48 + 48) * 16

    ev_n = [b_ev]
    act_n = [b_act]
    mm_n = [mm_fence]

    for th in range(2):
        toff = th * TH

        ev_xb0 = ev_n[0]

        def vec_xb(e, th=th, toff=toff, ev_xb0=ev_xb0):
            if th == 0:
                e.wait_ge(ld.h, b_ld + 8 * 16)
            for k in range(KT):
                ev.inc(e.tensor_copy(xb[:, k * TH:(k + 1) * TH], xs[:, k * T + toff:k * T + toff + TH]), 1)
        p.block.vector(vec_xb)
        ev_n[0] = ev.n
        xb_ready = ev.n

        mm_r0 = mm_n[0]

        def tens_r(e, th=th, toff=toff, mm_r0=mm_r0, xb_ready=xb_ready):
            e.wait_ge(ev.h, xb_ready)
            e.wait_ge(ld.h, b_ld + 17 * 16)
            e.wait_ge(sde.h, b_de + 16 * (th * NE))
            for tc in range(2):
                ins = None
                for k in range(KT):
                    ins = e.matmul(ps_d[tc % 2][0:8, :], wr_sb[:, k * 8:(k + 1) * 8],
                                   xs[:, k * T + toff + tc * 512: k * T + toff + tc * 512 + 512],
                                   start=(k == 0), stop=(k == KT - 1))
                mm.inc(ins, 1)
        p.block.tensor(tens_r)
        mm_router = mm.n
        act_r0 = act_n[0]

        def act_r(e, th=th, mm_r0=mm_r0):
            for tc in range(2):
                e.wait_ge(mm.h, mm_r0 + tc + 1)
                act.inc(e.activation(out=probsT[0:8, tc * 512:(tc + 1) * 512], in_=ps_d[tc % 2][0:8, :],
                                     func=AF.Sigmoid, bias=rb_sb[0:8, 0:1], scale=1.0), 1)
        p.block.scalar(act_r)
        act_n[0] = act.n
        act_probs = act.n
        ev_r0 = ev.n

        def tens_r2(e, th=th, act_probs=act_probs, ev_r0=ev_r0):
            e.wait_ge(act.h, act_probs)
            for tt in range(8):
                if tt >= 2:
                    e.wait_ge(ev.h, ev_r0 + 7 * (tt - 1))
                mm.inc(e.transpose(ps_d[2 + tt % 2][0:128, 0:8], probsT[0:8, tt * 128:(tt + 1) * 128],
                                   t["ident"][0:8, 0:8]), 1)
        p.block.tensor(tens_r2)
        mm_tr0 = mm_router

        def vec_r(e, th=th, mm_tr0=mm_tr0):
            tp = s["tp"]

            def st_(ins):
                tp.inc(ins, 1)
                e.wait_ge(tp.h, tp.n)
                return ins
            for tt in range(8):
                e.wait_ge(mm.h, mm_tr0 + tt + 1)
                pb = ps_d[2 + tt % 2]
                c0 = th * 64 + tt * 8
                st_(e.tensor_copy(cw_tm[:, c0:c0 + 8], pb[0:128, 0:8]))
                st_(e.reduce_max(m1[:, 0:1], cw_tm[:, c0:c0 + 8], axis=mybir.AxisListType.X))
                st_(e.tensor_scalar(msk[:, 0:8], cw_tm[:, c0:c0 + 8], m1[:, 0:1], None, ALU.is_ge))
                st_(e.tensor_scalar(msk[:, 8:16], msk[:, 0:8], -1e30, None, ALU.mult))
                st_(e.tensor_tensor(msk[:, 16:24], cw_tm[:, c0:c0 + 8], msk[:, 8:16], ALU.add))
                st_(e.reduce_max(m1[:, 1:2], msk[:, 16:24], axis=mybir.AxisListType.X))
                st_(e.tensor_scalar(msk[:, 0:8], cw_tm[:, c0:c0 + 8], m1[:, 1:2], None, ALU.is_ge))
                if os.environ.get("MK_DBG") == "cwprobs":
                    ev.inc(e.tensor_copy(cw_tm[:, c0:c0 + 8], cw_tm[:, c0:c0 + 8]) if False else e.tensor_scalar(cw_tm[:, c0:c0 + 8], cw_tm[:, c0:c0 + 8], 1.0, None, ALU.mult), 7)
                else:
                    ev.inc(e.tensor_tensor(cw_tm[:, c0:c0 + 8], cw_tm[:, c0:c0 + 8], msk[:, 0:8], ALU.mult), 7)
        p.block.vector(vec_r)
        ev_n[0] = ev.n
        ev_cwtm = ev.n
        mm_n[0] = mm.n

        for j in range(NE):
            J = th * NE + j
            routed = j >= 1
            R = th * 8 + (j - 1)
            c0cw = b_cw + R * 10

            if routed:
                mm_cw0 = mm.n

                def tens_cw(e, th=th, j=j, J=J, R=R, c0cw=c0cw, ev_cwtm=ev_cwtm):
                    e.wait_ge(ev.h, ev_cwtm)
                    e.wait_ge(sde.h, b_de + 16 * J)
                    for tt in range(8):
                        if tt >= 4:
                            e.wait_ge(scw.h, c0cw + (tt - 4) + 1)
                        mm.inc(e.transpose(ps_d[tt % 4][0:1, 0:128],
                                           cw_tm[:, th * 64 + tt * 8 + (j - 1):th * 64 + tt * 8 + j],
                                           t["ident"][:, :]), 1)
                    for tc in range(2):
                        e.wait_ge(scw.h, c0cw + max(4 * (tc + 1), 5 + tc))
                        mm.inc(e.matmul(ps_d[tc % 4][:, :], t["ones_row_b"][0:1, :],
                                        cw_row[0:1, tc * 512:tc * 512 + 512], start=True, stop=True), 1)
                p.block.tensor(tens_cw)

                def vec_cw(e, mm_cw0=mm_cw0):
                    for tt in range(8):
                        e.wait_ge(mm.h, mm_cw0 + tt + 1)
                        scw.inc(e.tensor_copy(cw_row[0:1, tt * 128:(tt + 1) * 128], ps_d[tt % 4][0:1, 0:128]), 1)
                    for tc in range(2):
                        e.wait_ge(mm.h, mm_cw0 + 8 + tc + 1)
                        scw.inc(e.tensor_copy(cw_b[:, tc * 512:(tc + 1) * 512], ps_d[tc % 4][:, :]), 1)
                p.block.vector(vec_cw)
                mm_n[0] = mm.n

            def tens_gu(e, J=J, xb_ready=xb_ready):
                e.wait_ge(ev.h, xb_ready)
                for ic in range(IC):
                    gi = J * IC + ic
                    db = gi % 2
                    e.wait_ge(pgd.h, pgd_wgu_done(gi))
                    for tc in range(2):
                        qi = gi * 2 + tc
                        if qi >= 2:
                            e.wait_ge(ssl.h, b_sl + qi - 1)
                            e.wait_ge(sat.h, b_at + qi - 1)
                        for k in range(KT):
                            e.matmul(ps_g[qi % 2][:, :], wg_sb[db][:, k * 128:(k + 1) * 128],
                                     xb[:, k * TH + tc * 512:k * TH + tc * 512 + 512],
                                     start=(k == 0), stop=(k == KT - 1))
                        ins = None
                        for k in range(KT):
                            ins = e.matmul(ps_u[qi % 2][:, :], wu_sb[db][:, k * 128:(k + 1) * 128],
                                           xb[:, k * TH + tc * 512:k * TH + tc * 512 + 512],
                                           start=(k == 0), stop=(k == KT - 1))
                        sgu.inc(ins, 1)
            p.block.tensor(tens_gu)


            def act_sl(e, J=J):
                for ic in range(IC):
                    for tc in range(2):
                        qi = (J * IC + ic) * 2 + tc
                        e.wait_ge(sgu.h, b_gu + qi + 1)
                        if qi >= 2:
                            e.wait_ge(sat.h, b_at + qi - 1)
                        ssl.inc(e.activation(out=g_sb[qi % 2][:, :], in_=ps_g[qi % 2][:, :], func=AF.Silu), 1)
            p.block.scalar(act_sl)

            def vec_au(e, J=J, routed=routed, c0cw=c0cw):
                for ic in range(IC):
                    for tc in range(2):
                        qi = (J * IC + ic) * 2 + tc
                        e.wait_ge(sgu.h, b_gu + qi + 1)
                        if routed:
                            e.wait_ge(scw.h, c0cw + 8 + tc + 1)
                            e.tensor_tensor(u_sb[qi % 2][:, :], ps_u[qi % 2][:, :],
                                            cw_b[:, tc * 512:(tc + 1) * 512], ALU.mult)
                        else:
                            e.tensor_copy(u_sb[qi % 2][:, :], ps_u[qi % 2][:, :])
                        e.wait_ge(ssl.h, b_sl + qi + 1)
                        e.wait_ge(sdn.h, b_dn + 16 * J)
                        sat.inc(e.tensor_tensor(aT[:, ic * TH + tc * 512:ic * TH + tc * 512 + 512],
                                                g_sb[qi % 2][:, :], u_sb[qi % 2][:, :], ALU.mult), 1)
            p.block.vector(vec_au)

            def tens_dn(e, J=J, routed=routed, c0cw=c0cw):
                e.wait_ge(sat.h, b_at + 32 * (J + 1))
                e.wait_ge(pgd.h, pgd_wd_done(J))
                if routed:
                    e.wait_ge(scw.h, c0cw + 10)
                for tc in range(2):
                    for ht in range(KT):
                        di = J * 16 + tc * KT + ht
                        if di >= 4:
                            e.wait_ge(sde.h, b_de + di - 3)
                        ins = None
                        for ic in range(IC):
                            ins = e.matmul(ps_d[di % 4][:, :], wd_sb[:, ic * D + ht * 128:ic * D + ht * 128 + 128],
                                           aT[:, ic * TH + tc * 512:ic * TH + tc * 512 + 512],
                                           start=(ic == 0), stop=(ic == IC - 1))
                        sdn.inc(ins, 1)
            p.block.tensor(tens_dn)
            mm_n[0] = mm.n

            def vec_dn(e, J=J, toff=toff):
                for tc in range(2):
                    for ht in range(KT):
                        di = J * 16 + tc * KT + ht
                        e.wait_ge(sdn.h, b_dn + di + 1)
                        sl = slice(ht * T + toff + tc * 512, ht * T + toff + tc * 512 + 512)
                        sde.inc(e.tensor_tensor(xs[:, sl], xs[:, sl], ps_d[di % 4][:, :], ALU.add), 1)
            p.block.vector(vec_dn)
            ev_n[0] = ev.n

    if os.environ.get("MK_DBG") in ("cw", "cwprobs"):
        def vcw(e):
            e.wait_ge(sde.h, b_de + 18 * 16)
            e.tensor_copy(xs[:, 0:128], cw_tm[:, 0:128])
            e.tensor_copy(xs[:, 128:130], m1[:, 0:2])
            e.tensor_copy(xs[:, 130:154], msk[:, 0:24])
            ln.inc(e.tensor_copy(t["junk"][:, 0:1], t["junk"][:, 1:2]), 1)
        p.block.vector(vcw)
        p.stream_ready = ln.n
        inner.close()
        p._moe_ret = (xs, phase)
        return

    def vec_bridge(e):
        for k in range(KT):
            e.wait_ge(sde.h, b_de + 8 * 16 + 8 + k + 1)
            e.wait_ge(sde.h, b_de + 17 * 16 + 8 + k + 1)
            ev.inc(e.tensor_copy(t["junk"][:, 0:1], t["junk"][:, 1:2]), 1)
    p.block.vector(vec_bridge)
    y_ready = ev.n

    if os.environ.get("MK_DBG") == "moeraw":
        def vraw(e):
            for k in range(KT):
                e.wait_ge(ev.h, y_ready - KT + k + 1)
            ln.inc(e.tensor_copy(t["junk"][:, 0:1], t["junk"][:, 1:2]), 1)
        p.block.vector(vraw)
        p.stream_ready = ln.n
        inner.close()
        p._moe_ret = (xs, phase)
        return
    inner.close()

    _emit_ln(p, xs, y_ready, tag="ln2")
    p._moe_ret = (xs, phase)


def _emit_attn(p):
    nc, s, t = p.nc, p.s, p.t
    ld, mm, ev, act, ln, st = s["ld"], s["mm"], s["ev"], s["act"], s["ln"], s["st"]
    pj = s["pj"]; pg = s["pg"]; pgd = s["pgd"]
    pp = s["pp"]; pa = s["pa"]; po = s["po"]; pi = s["pi"]; pc = s["pc"]
    pS = s["pS"]; pmx = s["pmx"]; pex = s["pex"]; psc = s["psc"]
    ptr = s["ptr"]; pte = s["pte"]; pav = s["pav"]; poe = s["poe"]; tp = s["tp"]

    phase = contextlib.ExitStack()
    OT = phase.enter_context(nc.sbuf_tensor("at_OT", [128, KT * T], BF16))
    qkv = contextlib.ExitStack()
    QT = qkv.enter_context(nc.sbuf_tensor("at_QT", [128, KT * T], BF16))
    KTs = qkv.enter_context(nc.sbuf_tensor("at_KT", [128, KT * T], BF16))
    V = qkv.enter_context(nc.sbuf_tensor("at_V", [128, 16 * D], BF16))
    rext = qkv.enter_context(nc.sbuf_tensor("at_rext", [128, 4096], BF16))
    bvT = qkv.enter_context(nc.sbuf_tensor("at_bvT", [128, KT], F32))

    stream_in_dram = p.stream_in_dram
    mm_fence = mm.n
    b_ld = ld.n
    b_pg = pg.n
    b_pgd = pgd.n
    b_pj = pj.n
    b_mm = mm.n

    proj = contextlib.ExitStack()
    xb = proj.enter_context(nc.sbuf_tensor("at_xb", [128, KT * T], BF16))
    w_sb = proj.enter_context(nc.sbuf_tensor("at_wsb", [128, KT * D], BF16))
    bqkT = proj.enter_context(nc.sbuf_tensor("at_bqk", [128, 2 * KT], F32))
    ps_pr = [proj.enter_context(nc.psum_tensor(f"at_pspr{i}", [128, 512], F32)) for i in range(4)]

    def gp_xb(e):
        e.wait_ge(st.h, stream_in_dram)
        e.wait_ge(mm.h, mm_fence)
        e.wait_ge(ln.h, p.stream_ready)
        for k in range(KT):
            pgd.inc(e.dma_start(out=xb[:, k * T:(k + 1) * T], in_=t["xs_dram"][k * 128:(k + 1) * 128, :]), 16)
        for wi, wh in enumerate(["wqT_hbm", "wkT_hbm", "wvT_hbm"]):
            if wi > 0:
                e.wait_ge(mm.h, b_mm + 32 * wi)
            for kt in range(KT):
                pgd.inc(e.dma_start(out=w_sb[:, kt * D:(kt + 1) * D], in_=t[wh][kt * 128:(kt + 1) * 128, :]), 16)
        for hf in range(4):
            pgd.inc(e.dma_start(out=rext[0:64, hf * 1024:(hf + 1) * 1024],
                                in_=t["rextT_hbm"][:, hf * 1024:(hf + 1) * 1024]), 16)
            pgd.inc(e.dma_start(out=rext[64:128, hf * 1024:(hf + 1) * 1024],
                                in_=t["rextT_hbm"][:, hf * 1024:(hf + 1) * 1024]), 16)
    p.block.gpsimd(gp_xb)

    def sync_w(e):
        e.wait_ge(st.h, stream_in_dram)
        e.wait_ge(mm.h, mm_fence)
        for m in range(KT):
            ld.inc(e.dma_start(out=bqkT[:, m:m + 1], in_=t["bq_hbm"][m * 128:(m + 1) * 128].unsqueeze(1)), 16)
            ld.inc(e.dma_start(out=bqkT[:, KT + m:KT + m + 1], in_=t["bk_hbm"][m * 128:(m + 1) * 128].unsqueeze(1)), 16)
            ld.inc(e.dma_start(out=bvT[:, m:m + 1], in_=t["bv_hbm"][m * 128:(m + 1) * 128].unsqueeze(1)), 16)
    p.block.sync(sync_w)

    def tens_proj(e):
        e.wait_ge(pgd.h, b_pgd + 16 * KT)
        for wi in range(3):
            for gidx in range(32):
                g = wi * 32 + gidx
                e.wait_ge(pgd.h, b_pgd + 16 * (KT + KT * (wi + 1)))
                if g >= 4:
                    e.wait_ge(pj.h, b_pj + g - 3)
                pb = ps_pr[g % 4]
                ins = None
                if wi < 2:
                    mt, tc = divmod(gidx, NTC)
                    for k in range(KT):
                        ins = e.matmul(pb[:, :], w_sb[:, k * D + mt * 128:k * D + mt * 128 + 128],
                                       xb[:, k * T + tc * 512:k * T + tc * 512 + 512],
                                       start=(k == 0), stop=(k == KT - 1))
                else:
                    jt, fh = divmod(gidx, 2)
                    for k in range(KT):
                        ins = e.matmul(pb[:, :], xb[:, k * T + jt * 128:k * T + jt * 128 + 128],
                                       w_sb[:, k * D + fh * 512:k * D + fh * 512 + 512],
                                       start=(k == 0), stop=(k == KT - 1))
                mm.inc(ins, 1)
    p.block.tensor(tens_proj)

    def vec_proj(e):
        for wi in range(3):
            for gidx in range(32):
                g = wi * 32 + gidx
                e.wait_ge(mm.h, b_mm + g + 1)
                pb = ps_pr[g % 4]
                if wi < 2:
                    mt, tc = divmod(gidx, NTC)
                    dst = QT if wi == 0 else KTs
                    pj.inc(e.tensor_scalar(dst[:, mt * T + tc * 512:mt * T + tc * 512 + 512],
                                           pb[:, :], bqkT[:, wi * KT + mt:wi * KT + mt + 1], None, ALU.add), 1)
                else:
                    jt, fh = divmod(gidx, 2)
                    pj.inc(e.tensor_copy(V[:, jt * D + fh * 512:jt * D + fh * 512 + 512], pb[:, :]), 1)
    p.block.vector(vec_proj)
    proj_done = pj.n
    rext_done = b_pg + 28
    proj.close()

    hl = contextlib.ExitStack()
    pstage = [hl.enter_context(nc.sbuf_tensor(f"at_pst{i}", [128, 2176], BF16)) for i in range(2)]
    sk_sb = [hl.enter_context(nc.sbuf_tensor(f"at_sk{i}", [128, T], BF16)) for i in range(2)]
    S_row = [hl.enter_context(nc.sbuf_tensor(f"at_S{i}", [128, T], F32)) for i in range(2)]
    expS = [hl.enter_context(nc.sbuf_tensor(f"at_eS{i}", [128, T], BF16)) for i in range(2)]
    expST = [hl.enter_context(nc.sbuf_tensor(f"at_eST{i}", [128, T], BF16)) for i in range(2)]
    mxb = hl.enter_context(nc.sbuf_tensor("at_mx", [128, 8], F32))
    ps_a = [hl.enter_context(nc.psum_tensor(f"at_psa{i}", [128, 512], F32)) for i in range(4)]
    ps_t = [hl.enter_context(nc.psum_tensor(f"at_ptt{i}", [128, 512], BF16)) for i in range(2)]
    ps_o = [hl.enter_context(nc.psum_tensor(f"at_pso{i}", [64, 128], F32)) for i in range(2)]

    NSL = 256
    b_pp, b_pa, b_po, b_pi, b_pc, b_pS, b_pmx, b_pex, b_psc, b_ptr, b_pte, b_pav, b_poe = (
        x.n for x in (pp, pa, po, pi, pc, pS, pmx, pex, psc, ptr, pte, pav, poe))
    CW = [512, 512, 512, 512, 128]
    COF = [0, 512, 1024, 1536, 2048]

    def qslice(h, ic):
        return QT[64 * (h % 2):64 * (h % 2) + 64, (h // 2) * T + ic * 128:(h // 2) * T + ic * 128 + 128]

    def tens_hl(e):
        e.wait_ge(pj.h, proj_done)
        e.wait_ge(pgd.h, rext_done)
        e.wait_ge(s["init"].h, p.init_done)
        for oi in range(NSL + 4):
            if oi < NSL:
                h, ic = divmod(oi, 16)
                k0s = 1920 - ic * 128
                for c in range(5):
                    cn = 5 * oi + c
                    if cn >= 2:
                        e.wait_ge(pa.h, b_pa + cn - 1)
                    pp.inc(e.matmul(ps_a[cn % 2][:, 0:CW[c]], qslice(h, ic),
                                    rext[64 * (h % 2):64 * (h % 2) + 64, k0s + COF[c]:k0s + COF[c] + CW[c]],
                                    start=True, stop=True), 1)
            if 2 <= oi < NSL + 2:
                g1 = oi - 2
                h, ic = divmod(g1, 16)
                for jc in range(NTC):
                    cn = 4 * g1 + jc
                    if cn >= 2:
                        e.wait_ge(pS.h, b_pS + cn - 1)
                    pc.inc(e.matmul(ps_a[2 + cn % 2][:, :], qslice(h, ic),
                                    KTs[64 * (h % 2):64 * (h % 2) + 64,
                                        (h // 2) * T + jc * 512:(h // 2) * T + jc * 512 + 512],
                                    start=True, stop=True), 1)
            if 4 <= oi < NSL + 4:
                g3 = oi - 4
                h, ic = divmod(g3, 16)
                e.wait_ge(psc.h, b_psc + g3 + 1)
                for gq in range(4):
                    gqi = 4 * g3 + gq
                    if gqi >= 2:
                        e.wait_ge(pte.h, b_pte + gqi - 1)
                    ins = None
                    for j4 in range(4):
                        jt = gq * 4 + j4
                        ins = e.matmul(ps_t[gqi % 2][:, j4 * 128:(j4 + 1) * 128],
                                       expS[g3 % 2][:, jt * 128:(jt + 1) * 128],
                                       t["identb"][:, :], is_transpose=True,
                                       start=(j4 == 0), stop=(j4 == 3))
                    ptr.inc(ins, 1)
                if g3 >= 2:
                    e.wait_ge(poe.h, b_poe + g3 - 1)
                ins = None
                for jt in range(16):
                    if jt % 4 == 0:
                        e.wait_ge(pte.h, b_pte + 4 * g3 + jt // 4 + 1)
                    ins = e.matmul(ps_o[g3 % 2][0:64, :], V[:, jt * D + h * 64:jt * D + h * 64 + 64],
                                   expST[g3 % 2][:, jt * 128:(jt + 1) * 128],
                                   start=(jt == 0), stop=(jt == 15))
                pav.inc(ins, 1)
    p.block.tensor(tens_hl)

    def gp_hl(e):
        for oi in range(NSL + 1):
            if oi < NSL:
                h, ic = divmod(oi, 16)
                e.wait_ge(pa.h, b_pa + 5 * (oi + 1))
                po.inc(e.dma_start(out=t["pscr"][h % 2, ic * 128 * 2176:(ic + 1) * 128 * 2176]
                                   .rearrange("(a c) -> a c", a=128),
                                   in_=pstage[oi % 2][:, :]), 16)
            if oi >= 1:
                g1 = oi - 1
                h, ic = divmod(g1, 16)
                e.wait_ge(po.h, b_po + 16 * (g1 + 1))
                if g1 >= 2:
                    e.wait_ge(pS.h, b_pS + 4 * (g1 - 1))
                for jc in range(NTC):
                    sap = bass.AP(tensor=t["pscr_h"],
                                  offset=(h % 2) * 16 * 128 * 2176 + ic * 128 * 2176 + jc * 512 + 127,
                                  ap=[[2175, 128], [1, 512]])
                    pi.inc(e.dma_start(out=sk_sb[g1 % 2][:, jc * 512:(jc + 1) * 512], in_=sap), 16)
    p.block.gpsimd(gp_hl)

    def act_hl(e):
        for oi in range(NSL + 3):
            if oi < NSL:
                for c in range(5):
                    cn = 5 * oi + c
                    e.wait_ge(pp.h, b_pp + cn + 1)
                    if oi >= 2 and c == 0:
                        e.wait_ge(po.h, b_po + 16 * (oi - 1))
                    pa.inc(e.copy(pstage[oi % 2][:, COF[c]:COF[c] + CW[c]], ps_a[cn % 2][:, 0:CW[c]]), 1)
            if oi >= 3:
                g2 = oi - 3
                e.wait_ge(pS.h, b_pS + 4 * (g2 + 1))
                if g2 >= 2:
                    e.wait_ge(ptr.h, b_ptr + 4 * (g2 - 1))
                pex.inc(e.activation(out=expS[g2 % 2][:, :], in_=S_row[g2 % 2][:, :], func=AF.Exp,
                                     scale=0.125,
                                     accum_out=mxb[:, (g2 % 2) * 4 + 2:(g2 % 2) * 4 + 3]), 1)
    p.block.scalar(act_hl)

    def vec_hl(e):
        def st_(ins):
            tp.inc(ins, 1)
            e.wait_ge(tp.h, tp.n)
            return ins
        for oi in range(NSL + 4):
            if 2 <= oi < NSL + 2:
                g1 = oi - 2
                par = g1 % 2
                if g1 >= 2:
                    e.wait_ge(pex.h, b_pex + g1 - 1)
                for jc in range(NTC):
                    cn = 4 * g1 + jc
                    e.wait_ge(pc.h, b_pc + cn + 1)
                    e.wait_ge(pi.h, b_pi + 64 * g1 + 16 * (jc + 1))
                    pS.inc(e.tensor_tensor(S_row[par][:, jc * 512:(jc + 1) * 512],
                                           ps_a[2 + cn % 2][:, :],
                                           sk_sb[par][:, jc * 512:(jc + 1) * 512], ALU.add), 1)
            if 3 <= oi < NSL + 3:
                g2 = oi - 3
                par = g2 % 2
                e.wait_ge(pex.h, b_pex + g2 + 1)
                st_(e.reciprocal(mxb[:, par * 4 + 3:par * 4 + 4], mxb[:, par * 4 + 2:par * 4 + 3]))
                psc.inc(e.tensor_scalar(expS[par][:, :], expS[par][:, :],
                                        mxb[:, par * 4 + 3:par * 4 + 4], None, ALU.mult), 1)
            if oi >= 4:
                g3 = oi - 4
                h, ic = divmod(g3, 16)
                par = g3 % 2
                if g3 >= 2:
                    e.wait_ge(pav.h, b_pav + g3 - 1)
                for gq in range(4):
                    gqi = 4 * g3 + gq
                    e.wait_ge(ptr.h, b_ptr + gqi + 1)
                    pte.inc(e.tensor_copy(expST[par][:, gq * 512:(gq + 1) * 512], ps_t[gqi % 2][:, :]), 1)
                e.wait_ge(pav.h, b_pav + g3 + 1)
                poe.inc(e.tensor_scalar(OT[64 * (h % 2):64 * (h % 2) + 64,
                                           (h // 2) * T + ic * 128:(h // 2) * T + ic * 128 + 128],
                                        ps_o[par][0:64, :],
                                        bvT[64 * (h % 2):64 * (h % 2) + 64, h // 2:h // 2 + 1],
                                        None, ALU.add), 1)
    p.block.vector(vec_hl)
    heads_done_poe = poe.n
    heads_done_pav = pav.n
    hl.close()
    qkv.close()

    wo = contextlib.ExitStack()
    xs = phase.enter_context(nc.sbuf_tensor("at_xs", [128, KT * T], F32))
    wstg = [wo.enter_context(nc.sbuf_tensor(f"at_wstg{i}", [128, D], F32)) for i in range(2)]
    wo_sb = wo.enter_context(nc.sbuf_tensor("at_wo", [128, KT * D], BF16))
    boT = wo.enter_context(nc.sbuf_tensor("at_bo", [128, KT], F32))
    t1b = wo.enter_context(nc.sbuf_tensor("at_t1", [128, 512], F32))
    ps_w = [wo.enter_context(nc.psum_tensor(f"at_psw{i}", [128, 512], F32)) for i in range(4)]

    b_ld2 = ld.n
    b_pg2 = pg.n
    b_mm2 = mm.n
    b_ev2 = ev.n

    def sync_wo(e):
        e.wait_ge(poe.h, heads_done_poe)
        e.wait_ge(pav.h, heads_done_pav)
        for k in range(KT):
            ld.inc(e.dma_start(out=xs[:, k * T:(k + 1) * T], in_=t["xs_dram"][k * 128:(k + 1) * 128, :]), 16)
        for m in range(KT):
            ld.inc(e.dma_start(out=boT[:, m:m + 1], in_=t["bo_hbm"][m * 128:(m + 1) * 128].unsqueeze(1)), 16)
        for kt in range(KT):
            if kt >= 2:
                e.wait_ge(pg.h, b_pg2 + kt - 1)
            ld.inc(e.dma_start(out=wstg[kt % 2][:, :], in_=t["woT_hbm"][kt * 128:(kt + 1) * 128, :]), 16)
    p.block.sync(sync_wo)

    def gp_wo(e):
        for kt in range(KT):
            e.wait_ge(ld.h, b_ld2 + 16 * (16 + kt + 1))
            pg.inc(e.tensor_copy(wo_sb[:, kt * D:(kt + 1) * D], wstg[kt % 2][:, :]), 1)
    p.block.gpsimd(gp_wo)

    def tens_wo(e):
        e.wait_ge(pg.h, b_pg2 + KT)
        for g in range(32):
            tc, mt = divmod(g, KT)
            if g >= 4:
                e.wait_ge(ev.h, b_ev2 + g - 3)
            ins = None
            for k in range(KT):
                ins = e.matmul(ps_w[g % 4][:, :], wo_sb[:, k * D + mt * 128:k * D + mt * 128 + 128],
                               OT[:, k * T + tc * 512:k * T + tc * 512 + 512],
                               start=(k == 0), stop=(k == KT - 1))
            mm.inc(ins, 1)
    p.block.tensor(tens_wo)

    def vec_wo(e):
        e.wait_ge(ld.h, b_ld2 + 16 * 16)
        for g in range(32):
            tc, mt = divmod(g, KT)
            e.wait_ge(mm.h, b_mm2 + g + 1)
            sl = slice(mt * T + tc * 512, mt * T + tc * 512 + 512)
            e.tensor_scalar(t1b[:, :], ps_w[g % 4][:, :], boT[:, mt:mt + 1], None, ALU.add)
            ev.inc(e.tensor_tensor(xs[:, sl], xs[:, sl], t1b[:, :], ALU.add), 1)
    p.block.vector(vec_wo)
    y_ready = ev.n
    wo.close()

    _emit_ln(p, xs, y_ready, tag="ln3")
    p._attn_ret = (xs, phase)


def _emit_ffn(p):
    nc, s, t = p.nc, p.s, p.t
    ld, mm, ev, act, ln, st = s["ld"], s["mm"], s["ev"], s["act"], s["ln"], s["st"]
    sgu = s["sgu"]; sdn = s["sdn"]; sat = s["sat"]; hh = s["hh"]
    fev = s["ssl"]; swd = s["swd"]; pmx = s["pmx"]; pgd = s["pgd"]; pj = s["pj"]
    po = s["po"]
    TH = 512
    NF1 = 32

    phase = contextlib.ExitStack()
    xs = phase.enter_context(nc.sbuf_tensor("ffn_xs", [128, KT * T], F32))
    inner = contextlib.ExitStack()
    xb = inner.enter_context(nc.sbuf_tensor("ffn_xb", [128, KT * TH], BF16))
    h1T = inner.enter_context(nc.sbuf_tensor("ffn_h1", [128, NF1 * TH], BF16))
    h2T = inner.enter_context(nc.sbuf_tensor("ffn_h2", [128, NF1 * TH], BF16))
    k0c = [inner.enter_context(nc.sbuf_tensor(f"ffn_k0c{i}", [128, KT * 128], BF16)) for i in range(2)]
    kc = [inner.enter_context(nc.sbuf_tensor(f"ffn_kc{i}", [128, NF1 * 128], BF16)) for i in range(2)]
    s0a = [inner.enter_context(nc.sbuf_tensor(f"ffn_s0a{i}", [128, KT * 128], U8)) for i in range(2)]
    s0b = [inner.enter_context(nc.sbuf_tensor(f"ffn_s0b{i}", [128, KT * 128], U8)) for i in range(2)]
    rs = inner.enter_context(nc.sbuf_tensor("ffn_rs", [1, 2 * TH], F32))
    svec = inner.enter_context(nc.sbuf_tensor("ffn_svec", [128, 4], F32))
    t1b = inner.enter_context(nc.sbuf_tensor("ffn_t1", [128, 512], F32))
    ps1 = [inner.enter_context(nc.psum_tensor(f"ffn_ps{i}", [128, 512], F32)) for i in range(4)]
    ps_r = [inner.enter_context(nc.psum_tensor(f"ffn_psr{i}", [1, 512], F32)) for i in range(2)]
    ps_s = inner.enter_context(nc.psum_tensor("ffn_pss", [128, 4], F32))

    stream_in_dram = p.stream_in_dram
    mm_fence = mm.n
    b_ld = ld.n
    b_pgd = pgd.n
    b_po = po.n
    b_gu, b_dn, b_at, b_hh, b_fev, b_wd, b_pmx, b_mm, b_pj = (
        x.n for x in (sgu, sdn, sat, hh, fev, swd, pmx, mm, pj))

    def tens_s(e):
        e.wait_ge(st.h, stream_in_dram)
        e.wait_ge(mm.h, mm_fence)
        e.wait_ge(s["init"].h, p.init_done)
        mm.inc(e.matmul(ps_s[:, 0:3], t["ones_row"][0:1, :], t["s012_sb"][0:1, 0:3],
                        start=True, stop=True), 1)
    p.block.tensor(tens_s)

    def vec_s(e):
        tp = s["tp"]
        e.wait_ge(mm.h, b_mm + 1)
        tp.inc(e.tensor_scalar(svec[:, 0:1], ps_s[:, 0:1], 4.0, None, ALU.mult), 1)
        tp.inc(e.tensor_scalar(svec[:, 1:2], ps_s[:, 1:2], 4.0, None, ALU.mult), 1)
        tp.inc(e.tensor_scalar(svec[:, 2:3], ps_s[:, 2:3], 2.0, None, ALU.mult), 1)
        e.wait_ge(tp.h, tp.n)
        pj.inc(e.tensor_copy(svec[:, 3:4], svec[:, 0:1]), 1)
    p.block.vector(vec_s)
    svec_done = pj.n

    def swd_k0(i):
        return b_wd + i + 1

    C0 = 0
    C1 = 32 * 128 * KT * 128
    C2 = C1 + 32 * 128 * NF1 * 128

    def sync(e):
        e.wait_ge(st.h, stream_in_dram)
        e.wait_ge(mm.h, mm_fence)
        for k in range(KT):
            ld.inc(e.dma_start(out=xs[:, k * T:(k + 1) * T], in_=t["xs_dram"][k * 128:(k + 1) * 128, :]), 16)
        for i in range(NF1):
            if i >= 2:
                e.wait_ge(swd.h, swd_k0(i - 2))
            ld.inc(e.dma_start(out=s0a[i % 2][:, :].rearrange("p (k c) -> p k c", k=KT),
                               in_=t["k0a_hbm"].rearrange("(k p) f -> p k f", p=128)[:, :, i * 128:(i + 1) * 128]), 16)
            ld.inc(e.dma_start(out=s0b[i % 2][:, :].rearrange("p (k c) -> p k c", k=KT),
                               in_=t["k0b_hbm"].rearrange("(k p) f -> p k f", p=128)[:, :, i * 128:(i + 1) * 128]), 16)
        for q in range(1, 4):
            for i in range(NF1):
                gi = q * NF1 + i
                e.wait_ge(po.h, b_po + 16 * (i + 1) if q == 1 else b_po)
                e.wait_ge(sgu.h, (b_gu + 32 * q + i - 1) if i >= 2 else (b_gu + 32 * q))
                ld.inc(e.dma_start(out=k0c[gi % 2][:, :],
                                   in_=t["wck"][C0 + i * 128 * KT * 128:C0 + (i + 1) * 128 * KT * 128]
                                   .rearrange("(a c) -> a c", a=128)), 16)

    def lda_k0c(q, i):
        return b_ld + (8 + 64 + (q - 1) * 32 + i + 1) * 16
    p.block.sync(sync)

    def gp(e):
        for q in range(4):
            toff = q * TH
            if q == 0:
                e.wait_ge(st.h, stream_in_dram)
                e.wait_ge(mm.h, mm_fence)
            else:
                e.wait_ge(sgu.h, b_gu + 32 * q)
            for k in range(KT):
                pgd.inc(e.dma_start(out=xb[:, k * TH:(k + 1) * TH],
                                    in_=t["xs_dram"][k * 128:(k + 1) * 128, toff:toff + TH]), 16)
            if q == 0:
                for i in range(NF1):
                    e.wait_ge(ld.h, b_ld + (8 + 2 * (i + 1)) * 16)
                    if i >= 2:
                        e.wait_ge(sgu.h, b_gu + i - 1)
                    swd.inc(e.tensor_tensor(k0c[i % 2][:, :], s0a[i % 2][:, :], s0b[i % 2][:, :], ALU.subtract), 1)
                    po.inc(e.dma_start(out=t["wck"][C0 + i * 128 * KT * 128:C0 + (i + 1) * 128 * KT * 128]
                                       .rearrange("(a c) -> a c", a=128), in_=k0c[i % 2][:, :]), 16)
            for i in range(NF1):
                if i >= 2:
                    e.wait_ge(sdn.h, b_dn + 32 * q + i - 1)
                else:
                    e.wait_ge(sat.h, b_at + 8 * q)
                pgd.inc(e.dma_start(out=kc[i % 2][:, :].rearrange("p (k c) -> p k c", k=NF1),
                                    in_=t["k1_hbm"].rearrange("(k p) f -> p k f", p=128)[:, :, i * 128:(i + 1) * 128]), 16)
            for i in range(KT):
                if i >= 2:
                    e.wait_ge(sat.h, b_at + 8 * q + i - 1)
                else:
                    e.wait_ge(sdn.h, b_dn + 32 * (q + 1))
                pgd.inc(e.dma_start(out=kc[i % 2][:, :].rearrange("p (k c) -> p k c", k=NF1),
                                    in_=t["k2_hbm"].rearrange("(k p) f -> p k f", p=128)[:, :, i * 128:(i + 1) * 128]), 16)
    p.block.gpsimd(gp)

    def pgd_k1(q, i):
        return b_pgd + (q * 48 + 8 + i + 1) * 16

    def pgd_k2(q, i):
        return b_pgd + (q * 48 + 40 + i + 1) * 16




    def tens(e):
        e.wait_ge(pj.h, svec_done)
        for q in range(4):
            e.wait_ge(pgd.h, b_pgd + 16 * KT * (q + 1))
            for i in range(NF1):
                if q == 0:
                    e.wait_ge(swd.h, swd_k0(i))
                else:
                    e.wait_ge(ld.h, lda_k0c(q, i))
                gg = 72 * q + i
                if gg >= 4:
                    e.wait_ge(fev.h, b_fev + gg - 3)
                ins = None
                for k in range(KT):
                    ins = e.matmul(ps1[gg % 4][:, :], k0c[(q * NF1 + i) % 2][:, k * 128:(k + 1) * 128],
                                   xb[:, k * TH:k * TH + TH],
                                   start=(k == 0), stop=(k == KT - 1))
                sgu.inc(ins, 1)
            e.wait_ge(fev.h, b_fev + 72 * q + 32)
            e.wait_ge(pmx.h, b_pmx + 2 * q)
            ins = None
            for k in range(NF1):
                ins = e.matmul(ps_r[0][0:1, :], t["ones_col_b"][:, 0:1],
                               h1T[:, k * TH:(k + 1) * TH], start=(k == 0), stop=(k == NF1 - 1))
            hh.inc(ins, 1)
            for i in range(NF1):
                e.wait_ge(pgd.h, pgd_k1(q, i))
                gg = 72 * q + 32 + i
                if gg >= 4:
                    e.wait_ge(fev.h, b_fev + gg - 3)
                if i == 0:
                    e.wait_ge(pmx.h, b_pmx + 2 * q + 1)
                ins = None
                for k in range(NF1):
                    ins = e.matmul(ps1[gg % 4][:, :], kc[(q * NF1 + i) % 2][:, k * 128:(k + 1) * 128],
                                   h1T[:, k * TH:(k + 1) * TH], start=(k == 0), stop=False)
                ins = e.matmul(ps1[gg % 4][:, :], t["ones_row"][0:1, :],
                               rs[0:1, 0:TH], start=False, stop=True)
                sdn.inc(ins, 1)
            e.wait_ge(fev.h, b_fev + 72 * q + 64)
            e.wait_ge(pmx.h, b_pmx + 2 * q)
            ins = None
            for k in range(NF1):
                ins = e.matmul(ps_r[1][0:1, :], t["ones_col_b"][:, 0:1],
                               h2T[:, k * TH:(k + 1) * TH], start=(k == 0), stop=(k == NF1 - 1))
            hh.inc(ins, 1)
            for i in range(KT):
                e.wait_ge(pgd.h, pgd_k2(q, i))
                gg = 72 * q + 64 + i
                if gg >= 4:
                    e.wait_ge(fev.h, b_fev + gg - 3)
                if i == 0:
                    e.wait_ge(pmx.h, b_pmx + 2 * q + 2)
                ins = None
                for k in range(NF1):
                    ins = e.matmul(ps1[gg % 4][:, :], kc[(q * KT + i) % 2][:, k * 128:(k + 1) * 128],
                                   h2T[:, k * TH:(k + 1) * TH], start=(k == 0), stop=False)
                ins = e.matmul(ps1[gg % 4][:, :], t["ones_row"][0:1, :],
                               rs[0:1, TH:2 * TH], start=False, stop=True)
                sat.inc(ins, 1)
    p.block.tensor(tens)

    def vec(e):
        e.wait_ge(ld.h, b_ld + 16 * KT)
        for q in range(4):
            for i in range(NF1):
                gg = 72 * q + i
                e.wait_ge(sgu.h, b_gu + 32 * q + i + 1)
                if q >= 1 and i == 0:
                    e.wait_ge(sdn.h, b_dn + 32 * q)
                    e.wait_ge(hh.h, b_hh + 2 * q - 1)
                fev.inc(e.tensor_scalar(h1T[:, i * TH:(i + 1) * TH],
                                        ps1[gg % 4][:, :], svec[:, 0:1], 0.0, ALU.mult, ALU.max), 1)
            e.wait_ge(hh.h, b_hh + 2 * q + 1)
            pmx.inc(e.tensor_scalar(rs[0:1, 0:TH], ps_r[0][0:1, :], -0.5, None, ALU.mult), 1)
            for i in range(NF1):
                gg = 72 * q + 32 + i
                e.wait_ge(sdn.h, b_dn + 32 * q + i + 1)
                if q >= 1 and i == 0:
                    e.wait_ge(sat.h, b_at + 8 * q)
                    e.wait_ge(hh.h, b_hh + 2 * q)
                fev.inc(e.tensor_scalar(h2T[:, i * TH:(i + 1) * TH],
                                        ps1[gg % 4][:, :], svec[:, 1:2], 0.0, ALU.mult, ALU.max), 1)
            e.wait_ge(hh.h, b_hh + 2 * q + 2)
            pmx.inc(e.tensor_scalar(rs[0:1, TH:2 * TH], ps_r[1][0:1, :], -0.5, None, ALU.mult), 1)
            for i in range(KT):
                gg = 72 * q + 64 + i
                e.wait_ge(sat.h, b_at + 8 * q + i + 1)
                e.tensor_scalar(t1b[:, :], ps1[gg % 4][:, :], svec[:, 2:3], None, ALU.mult)
                sl = slice(i * T + q * TH, i * T + q * TH + TH)
                fev.inc(e.tensor_tensor(xs[:, sl], xs[:, sl], t1b[:, :], ALU.add), 1)
    p.block.vector(vec)

    def vec_bridge(e):
        for k in range(KT):
            e.wait_ge(fev.h, b_fev + 72 * 3 + 64 + k + 1)
            ev.inc(e.tensor_copy(t["junk"][:, 0:1], t["junk"][:, 1:2]), 1)
    p.block.vector(vec_bridge)
    y_ready = ev.n
    inner.close()

    _emit_ln(p, xs, y_ready, tag="ln4")
    p._ffn_ret = (xs, phase)


def build_program(phases=("l0",)):
    nc = bass.Bass("TRN2", target_bir_lowering=False, debug=False)
    t = {}
    t["xT_hbm"] = nc.dram_tensor("xT", [D, T], F32, kind="ExternalInput").ap()
    t["ident_hbm"] = nc.dram_tensor("ident", [128, 128], F32, kind="ExternalInput").ap()
    t["s012_hbm"] = nc.dram_tensor("s012", [3], F32, kind="ExternalInput").ap()
    if "l0" in phases:
        t["wiT_hbm"] = nc.dram_tensor("wiT", [D, D], F32, kind="ExternalInput").ap()
        t["whT_hbm"] = nc.dram_tensor("whT", [D, D], F32, kind="ExternalInput").ap()
        t["bi_hbm"] = nc.dram_tensor("bi", [D], F32, kind="ExternalInput").ap()
        t["bh_hbm"] = nc.dram_tensor("bh", [D], F32, kind="ExternalInput").ap()
    if "moe" in phases:
        for e in range(8):
            t[f"wg{e}T_hbm"] = nc.dram_tensor(f"wg{e}T", [D, 2048], F32, kind="ExternalInput").ap()
            t[f"wu{e}T_hbm"] = nc.dram_tensor(f"wu{e}T", [D, 2048], F32, kind="ExternalInput").ap()
            t[f"wd{e}T_hbm"] = nc.dram_tensor(f"wd{e}T", [2048, D], F32, kind="ExternalInput").ap()
        t["wgsT_hbm"] = nc.dram_tensor("wgsT", [D, 2048], F32, kind="ExternalInput").ap()
        t["wusT_hbm"] = nc.dram_tensor("wusT", [D, 2048], F32, kind="ExternalInput").ap()
        t["wdsT_hbm"] = nc.dram_tensor("wdsT", [2048, D], F32, kind="ExternalInput").ap()
        t["wrT_hbm"] = nc.dram_tensor("wrT", [D, 8], F32, kind="ExternalInput").ap()
        t["rb_hbm"] = nc.dram_tensor("rb", [8], F32, kind="ExternalInput").ap()
    if "attn" in phases:
        for w in ["wq", "wk", "wv", "wo"]:
            t[w + "T_hbm"] = nc.dram_tensor(w + "T", [D, D], F32, kind="ExternalInput").ap()
        for b in ["bq", "bk", "bv", "bo"]:
            t[b + "_hbm"] = nc.dram_tensor(b, [D], F32, kind="ExternalInput").ap()
        t["rextT_hbm"] = nc.dram_tensor("rextT", [64, 4096], F32, kind="ExternalInput").ap()
        t["pscr_h"] = nc.dram_tensor("pscr", [2, 16 * 128 * 2176], BF16, kind="Internal")
        t["pscr"] = t["pscr_h"].ap()
    if "ffn" in phases:
        t["k0a_hbm"] = nc.dram_tensor("k0a", [D, 4096], U8, kind="ExternalInput").ap()
        t["k0b_hbm"] = nc.dram_tensor("k0b", [D, 4096], U8, kind="ExternalInput").ap()
        t["k1_hbm"] = nc.dram_tensor("k1", [4096, 4096], U8, kind="ExternalInput").ap()
        t["k2_hbm"] = nc.dram_tensor("k2", [4096, D], U8, kind="ExternalInput").ap()
        t["wck"] = nc.dram_tensor("wck", [32 * 128 * KT * 128], BF16, kind="Internal").ap()
    t["outT_hbm"] = nc.dram_tensor("outT", [D, T], F32, kind="ExternalOutput").ap()
    t["xs_dram"] = nc.dram_tensor("xs_dram", [D, T], F32, kind="Internal").ap()

    with contextlib.ExitStack() as es:
        t["ident"] = es.enter_context(nc.sbuf_tensor("identsb", [128, 128], F32))
        t["identb"] = es.enter_context(nc.sbuf_tensor("identbsb", [128, 128], BF16))
        t["ones_col"] = es.enter_context(nc.sbuf_tensor("ones_col", [128, 1], F32))
        t["ones_col_b"] = es.enter_context(nc.sbuf_tensor("ones_col_b", [128, 1], BF16))
        t["ones_row"] = es.enter_context(nc.sbuf_tensor("ones_row", [1, 128], F32))
        t["ones_row_b"] = es.enter_context(nc.sbuf_tensor("ones_row_b", [1, 128], BF16))
        t["s012_sb"] = es.enter_context(nc.sbuf_tensor("s012_sb", [1, 3], F32))
        t["junk"] = es.enter_context(nc.sbuf_tensor("junk", [128, 2], BF16))

        sems = {}
        for name in ["ld", "mm", "ev", "act", "hh", "init", "ln", "st",
                     "sgu", "sdn", "ssl", "sat", "sde", "swu", "swd", "scw", "tp",
                     "pj", "pg", "pgd", "pp", "pa", "po", "pi", "pc",
                     "pS", "pmx", "pex", "psc", "ptr", "pte", "pav", "poe"]:
            sems[name] = Sem(es.enter_context(nc.semaphore(name)))
        block = es.enter_context(nc.Block())
        p = P(nc, block, es, sems, t)

        _emit_init(p)
        p.stream_in_dram = 0
        last = phases[-1]

        def store(xs, scope, is_last):
            st = sems["st"]
            stream_ready = p.stream_ready
            st0 = st.n
            dst = t["outT_hbm"] if is_last else t["xs_dram"]

            def gp(e):
                e.wait_ge(sems["ln"].h, stream_ready)
                for k in range(KT):
                    st.inc(e.dma_start(out=dst[k * 128:(k + 1) * 128, :], in_=xs[:, k * T:(k + 1) * T]), 16)
                if is_last:
                    e.wait_ge(st.h, st0 + 16 * KT)
            p.block.gpsimd(gp)
            p.stream_in_dram = st.n
            scope.close()

        if "l0" in phases:
            _emit_l0(p, t["xT_hbm"])
            store(*p._l0_ret, last == "l0")
        if "moe" in phases:
            _emit_moe(p)
            store(*p._moe_ret, last == "moe")
        if "attn" in phases:
            _emit_attn(p)
            store(*p._attn_ret, last == "attn")
        if "ffn" in phases:
            _emit_ffn(p)
            store(*p._ffn_ret, last == "ffn")
    return nc


def _host_inputs_per_core(inputs, b, phases):
    x = np.asarray(inputs["L0_attn_x"])[b]
    m = {
        "xT": np.ascontiguousarray(x.T),
        "ident": np.eye(128, dtype=np.float32),
        "s012": np.stack([np.float32(inputs["L1_ffn_s0"]), np.float32(inputs["L1_ffn_s1"]),
                          np.float32(inputs["L1_ffn_s2"])]).astype(np.float32),
    }
    if "l0" in phases:
        m["wiT"] = np.ascontiguousarray(np.asarray(inputs["L0_attn_Wi"]).T)
        m["whT"] = np.ascontiguousarray(np.asarray(inputs["L0_attn_Wh"]).T)
        m["bi"] = np.asarray(inputs["L0_attn_bi"])
        m["bh"] = np.asarray(inputs["L0_attn_bh"])
    if "moe" in phases:
        for e in range(8):
            m[f"wg{e}T"] = np.ascontiguousarray(np.asarray(inputs["L0_ffn_Wg"])[e].T)
            m[f"wu{e}T"] = np.ascontiguousarray(np.asarray(inputs["L0_ffn_Wu"])[e].T)
            m[f"wd{e}T"] = np.ascontiguousarray(np.asarray(inputs["L0_ffn_Wd"])[e].T)
        m["wgsT"] = np.ascontiguousarray(np.asarray(inputs["L0_ffn_Wg_s"]).T)
        m["wusT"] = np.ascontiguousarray(np.asarray(inputs["L0_ffn_Wu_s"]).T)
        m["wdsT"] = np.ascontiguousarray(np.asarray(inputs["L0_ffn_Wd_s"]).T)
        m["wrT"] = np.ascontiguousarray(np.asarray(inputs["L0_ffn_Wr"]).T)
        m["rb"] = np.asarray(inputs["L0_ffn_rb"])
    if "attn" in phases:
        m["wqT"] = np.ascontiguousarray(np.asarray(inputs["L1_attn_Wq"]).T)
        m["wkT"] = np.ascontiguousarray(np.asarray(inputs["L1_attn_Wk"]).T)
        m["wvT"] = np.ascontiguousarray(np.asarray(inputs["L1_attn_Wv"]).T)
        m["woT"] = np.ascontiguousarray(np.asarray(inputs["L1_attn_Wo"]).T)
        m["bq"] = np.asarray(inputs["L1_attn_bq"])
        m["bk"] = np.asarray(inputs["L1_attn_bk"])
        m["bv"] = np.asarray(inputs["L1_attn_bv"])
        m["bo"] = np.asarray(inputs["L1_attn_bo"])
        rt = np.asarray(inputs["L1_attn_rel_table"])
        rext = rt[np.clip(3070 - np.arange(4096), 0, 2046)]
        m["rextT"] = np.ascontiguousarray(rext.T)
    if "ffn" in phases:
        k0 = np.asarray(inputs["L1_ffn_k0"])
        m["k0a"] = np.ascontiguousarray(k0[:D]).view(np.uint8)
        m["k0b"] = np.ascontiguousarray(k0[D:]).view(np.uint8)
        m["k1"] = np.ascontiguousarray(np.asarray(inputs["L1_ffn_k1"])).view(np.uint8)
        m["k2"] = np.ascontiguousarray(np.asarray(inputs["L1_ffn_k2"])).view(np.uint8)
    return m


def kernel(**inputs):
    phases = ("l0", "moe", "attn", "ffn")
    nc = build_program(phases)
    in_maps = [_host_inputs_per_core(inputs, b, phases) for b in range(8)]
    res = run_bass_kernel_spmd(nc, in_maps, core_ids=list(range(8)))
    return np.stack([r["outT"].T for r in res.results], axis=0)
